# version 11 (final)
# baseline (speedup 1.0000x reference)
import numpy as np
import ml_dtypes
import contextlib
import concourse.bass as bass
import concourse.mybir as mybir
from concourse.bass_utils import run_bass_kernel_spmd

F32 = mybir.dt.float32
BF16 = mybir.dt.bfloat16
I32 = mybir.dt.int32
AF = mybir.ActivationFunctionType
ALU = mybir.AluOpType
AX = mybir.AxisListType

NCORE = 8
TOK = 16384
TPC = TOK // NCORE
D = 1024
BF = ml_dtypes.bfloat16


class Prog:
    ENGINES = ("sync", "tensor", "vector", "scalar", "gpsimd")

    def __init__(self):
        self.stages = []
        self.cur = []

    def op(self, eng, fn, dma=False, inc=None, pre=None):
        assert eng in self.ENGINES
        if inc is None:
            inc = 16 if dma else 1
        self.cur.append((eng, fn, inc, pre))

    def bar(self):
        if self.cur:
            self.stages.append(self.cur)
            self.cur = []

    def emit(self, block, sem):
        self.bar()
        per = {e: [] for e in self.ENGINES}
        cum = 0
        for stage in self.stages:
            start = cum
            for eng, fn, inc, pre in stage:
                per[eng].append((fn, start, inc, pre))
                cum += inc
        total = cum

        def mk(items, is_sync):
            def body(eng):
                last_wait = 0
                for fn, wait, inc, pre in items:
                    if wait > last_wait:
                        eng.wait_ge(sem, wait)
                        last_wait = wait
                    if pre:
                        for (s, v) in pre:
                            eng.wait_ge(s, v)
                    ins = fn(eng)
                    if inc:
                        ins.then_inc(sem, inc)
                if is_sync:
                    eng.wait_ge(sem, total)
            return body

        for e in self.ENGINES:
            if per[e] or e == "sync":
                getattr(block, e)(mk(per[e], e == "sync"))


class Ctx:
    def __init__(self):
        self.nc = bass.Bass("TRN2", target_bir_lowering=False)
        self.st = contextlib.ExitStack()
        self.scopes = [self.st]
        self.p = Prog()
        self._n = 0
        self.prefix = ""

    def push_scope(self):
        self.scopes.append(contextlib.ExitStack())

    def pop_scope(self):
        self.scopes.pop().close()

    def sb(self, shape, dt, name=None):
        self._n += 1
        nm = (self.prefix + name) if name else f"sb{self._n}"
        return self.scopes[-1].enter_context(
            self.nc.sbuf_tensor(nm, shape, dt))

    def ps(self, shape, dt=F32, name=None):
        self._n += 1
        nm = (self.prefix + name) if name else f"ps{self._n}"
        return self.scopes[-1].enter_context(
            self.nc.psum_tensor(nm, shape, dt))

    def inp(self, name, shape, dt=F32):
        return self.nc.dram_tensor(name, shape, dt, kind="ExternalInput").ap()

    def out(self, name, shape, dt=F32):
        return self.nc.dram_tensor(name, shape, dt, kind="ExternalOutput").ap()

    def dram(self, name, shape, dt, shared=False):
        return self.nc.dram_tensor(name, shape, dt,
                                   addr_space="Shared" if shared else "Local")

    def setup_common(self, consts=()):
        self.ones_col = self.sb([128, 1], BF16, "ones_col")
        self.ones_row = self.sb([1, 128], F32, "ones_row")
        self.ones_row_b = self.sb([1, 128], BF16, "ones_row_b")
        p = self.p
        p.op("vector", lambda e: e.memset(self.ones_col[:], 1.0))
        p.op("vector", lambda e: e.memset(self.ones_row[:], 1.0))
        p.op("vector", lambda e: e.memset(self.ones_row_b[:], 1.0))
        cl = sorted(set(float(x) for x in consts))
        if cl:
            self.cpool = self.sb([128, len(cl)], F32, "cpool")
            for i, cv in enumerate(cl):
                p.op("vector", lambda e, i=i, cv=cv: e.memset(self.cpool[:, i:i + 1], cv))
                self.nc.const_aps.register(F32, cv, self.cpool[:, i:i + 1])
        p.bar()

    def run(self, in_maps, core_ids, **kw):
        nc = self.finalize()
        return run_bass_kernel_spmd(nc, in_maps, core_ids=core_ids, **kw)

    def finalize(self):
        block = self.st.enter_context(self.nc.Block())
        sem = self.st.enter_context(self.nc.semaphore("SEQ"))
        with self.st:
            self.p.emit(block, sem)
        return self.nc


def ln_stats(c, xb, C, T, eps, ps2, rows=None):
    p = c.p
    nch = C // 128
    if rows is None:
        sqb = c.sb([128, 512], BF16)
        mu = c.sb([1, T], F32)
        m2 = c.sb([1, T], F32)
        rstd = c.sb([1, T], F32)
        mur = c.sb([1, T], F32)
    else:
        sqb, mu, m2, rstd, mur = rows["sqb"], rows["mu"], rows["m2"], rows["rstd"], rows["mur"]
    for sl in range(0, T, 512):
        for ch in range(nch):
            xap = xb[:, ch * T + sl: ch * T + sl + 512]
            p.op("vector", lambda e, xap=xap: e.tensor_mul(sqb[:], xap, xap))
            p.bar()
            p.op("tensor", lambda e, xap=xap, ch=ch: e.matmul(
                ps2[0:1, 0:512], c.ones_col[:], xap, start=(ch == 0), stop=(ch == nch - 1)))
            p.op("tensor", lambda e, ch=ch: e.matmul(
                ps2[0:1, 512:1024], c.ones_col[:], sqb[:], start=(ch == 0), stop=(ch == nch - 1)))
            p.bar()
        p.op("vector", lambda e, sl=sl: e.tensor_scalar(
            mu[0:1, sl:sl + 512], ps2[0:1, 0:512], 1.0 / C, None, ALU.mult, ALU.bypass))
        p.op("scalar", lambda e, sl=sl: e.mul(m2[0:1, sl:sl + 512], ps2[0:1, 512:1024], 1.0 / C))
        p.bar()
    p.op("vector", lambda e: e.tensor_mul(rstd[:], mu[:], mu[:]))
    p.bar()
    p.op("vector", lambda e: e.tensor_sub(rstd[:], m2[:], rstd[:]))
    p.bar()
    p.op("scalar", lambda e: e.activation(m2[:], rstd[:], AF.Sqrt, bias=float(eps), scale=1.0))
    p.bar()
    p.op("vector", lambda e: e.reciprocal(rstd[:], m2[:]))
    p.bar()
    p.op("vector", lambda e: e.tensor_mul(mur[:], mu[:], rstd[:]))
    p.bar()
    return {"mu": mu, "rstd": rstd, "mur": mur, "m2": m2, "sqb": sqb}


def bcast_row(c, row, T, ps1, out=None, dt=F32):
    p = c.p
    out = out if out is not None else c.sb([128, T], dt)
    for sl in range(0, T, 512):
        p.op("tensor", lambda e, sl=sl: e.matmul(
            ps1[:, 0:512], c.ones_row[:], row[0:1, sl:sl + 512], start=True, stop=True))
        p.bar()
        p.op("vector", lambda e, sl=sl: e.tensor_copy(out[:, sl:sl + 512], ps1[:, 0:512]))
        p.bar()
    return out


def load_w_bf16(c, hbm, C, F, name=None, pre=None):
    p = c.p
    t = c.sb([128, (C // 128) * F], BF16, name)
    for ch in range(C // 128):
        p.op("gpsimd", lambda e, ch=ch: e.dma_start(
            out=t[:, ch * F:(ch + 1) * F], in_=hbm[ch * 128:(ch + 1) * 128, :]), dma=True, pre=pre)
    return t


def colsum_neg_row(c, wt, C, F, ps2):
    p = c.p
    nch = C // 128
    out = c.sb([1, F], F32)
    for sl in range(0, F, 512):
        w = min(512, F - sl)
        for ch in range(nch):
            p.op("tensor", lambda e, ch=ch, sl=sl, w=w: e.matmul(
                ps2[0:1, 0:w], c.ones_col[:], wt[:, ch * F + sl:ch * F + sl + w],
                start=(ch == 0), stop=(ch == nch - 1)))
        p.bar()
        p.op("vector", lambda e, sl=sl, w=w: e.tensor_scalar(
            out[0:1, sl:sl + w], ps2[0:1, 0:w], -1.0, None, ALU.mult, ALU.bypass))
        p.bar()
    return out


def ln_proj(c, xb, C, T, wt, Fdim, mu_row, rstd_bc, psA, out_cb, bias_col=None,
            act=None, out_dt=BF16, negcs_f32=None, extra_rank1=None, wt_stride=None):
    p = c.p
    nch = C // 128
    nfo = Fdim // 128
    WS = wt_stride if wt_stride is not None else Fdim
    for fo in range(nfo):
        for sl in range(0, T, 512):
            for ch in range(nch):
                p.op("tensor", lambda e, fo=fo, sl=sl, ch=ch: e.matmul(
                    psA[:, 0:512], wt[:, ch * WS + fo * 128: ch * WS + fo * 128 + 128],
                    xb[:, ch * T + sl: ch * T + sl + 512],
                    start=(ch == 0), stop=False))
            p.op("tensor", lambda e, fo=fo, sl=sl: e.matmul(
                psA[:, 0:512], negcs_f32[0:1, fo * 128:fo * 128 + 128],
                mu_row[0:1, sl:sl + 512], start=False, stop=True))
            p.bar()
            tgt = out_cb(fo, sl)
            if act is None and bias_col is None:
                p.op("vector", lambda e, fo=fo, sl=sl, tgt=tgt: e.tensor_mul(
                    tgt, psA[:, 0:512], rstd_bc[:, sl:sl + 512]))
                p.bar()
            else:
                tmp = c.sb([128, 512], F32, None) if not hasattr(c, "_lnp_tmp") else c._lnp_tmp
                c._lnp_tmp = tmp
                p.op("vector", lambda e, sl=sl, tmp=tmp: e.tensor_mul(
                    tmp[:], psA[:, 0:512], rstd_bc[:, sl:sl + 512]))
                p.bar()
                bias = bias_col[:, fo:fo + 1] if bias_col is not None else 0.0
                if act is None or act == AF.Copy:
                    p.op("vector", lambda e, tmp=tmp, bias=bias, tgt=tgt: e.tensor_scalar(
                        tgt, tmp[:], bias, None, ALU.add, ALU.bypass))
                else:
                    p.op("scalar", lambda e, tmp=tmp, bias=bias, fn=act, tgt=tgt: e.activation(
                        tgt, tmp[:], fn, bias=bias, scale=1.0))
                p.bar()


def igather(c, out_ap, src_ap, idx_col, pre=None):
    c.p.op("gpsimd", lambda e: e.indirect_dma_start(
        out=out_ap, out_offset=None, in_=src_ap,
        in_offset=bass.IndirectOffsetOnAxis(ap=idx_col, axis=0)), dma=True, pre=pre)


def p1_emit(c, d):
    c.prefix = "s1_"
    T = TPC
    x_h = d["xT"]
    dw_h = d["p1_down_w"]
    db_h = d["p1_down_b"]
    p = c.p
    c.push_scope()
    xb = c.sb([128, 8 * T], BF16, "xb")
    for ch in range(8):
        p.op("gpsimd", lambda e, ch=ch: e.dma_start(
            out=xb[:, ch * T:(ch + 1) * T], in_=x_h[ch * 128:(ch + 1) * 128, :]), dma=True)
    wt = load_w_bf16(c, dw_h, D, 256, "dw")
    dbc = c.sb([128, 2], F32, "dbc")
    p.op("sync", lambda e: e.dma_start(out=dbc[:], in_=db_h[:]), dma=True)
    p.bar()
    ps2 = c.ps([1, 1024], name="ps_stat")
    psA = c.ps([128, 512], name="psA")
    stats = ln_stats(c, xb, D, T, 1e-5, ps2)
    negcs = colsum_neg_row(c, wt, D, 256, ps2)
    rstd_bc = bcast_row(c, stats["rstd"], T, psA)
    hT = c.sb([128, 2 * T], BF16, "hT_sb")

    def out_cb(fo, sl):
        return hT[:, fo * T + sl: fo * T + sl + 512]

    ln_proj(c, xb, D, T, wt, 256, stats["mu"], rstd_bc, psA, out_cb,
            bias_col=dbc, act=AF.Silu, negcs_f32=negcs)
    for fo in range(2):
        p.op("sync", lambda e, fo=fo: e.dma_start(
            out=d["h_loc"].ap()[fo * 128:(fo + 1) * 128, :], in_=hT[:, fo * T:(fo + 1) * T]), dma=True)
    p.bar()
    p.op("gpsimd", lambda e: e.collective_compute(
        "AllGather", ALU.bypass, replica_groups=[list(range(NCORE))],
        ins=[d["h_loc"].ap().opt()], outs=[d["G1"].ap().opt()]).then_inc(d["cc1"], 1), inc=0)
    p.bar()
    c.pop_scope()
    if hasattr(c, "_lnp_tmp"):
        del c._lnp_tmp


def p2_emit(c, d, NM=512):
    c.prefix = "s2_"
    K = 16
    L = NM * K
    p = c.p
    wq_h = d["p2_wq"]
    wk_h = d["p2_wk"]
    wv_h = d["p2_wv"]
    lrw_h = d["p2_lrw"]
    lrb_h = d["p2_lrb"]
    tokc_h = d["p2_tokc"]
    invc_h = d["p2_invc"]
    maskT_h = d["p2_maskT"]
    identf_h = d["identf"]
    W1_h = d["p2_W1"]
    b1_h = d["p2_b1"]
    gw_h = d["p2_gw"]
    gb_h = d["p2_gb"]
    gwc_h = d["p2_gwc"]
    gidx = d["gidx"]
    c.push_scope()
    hT = c.sb([128, 2 * L], BF16, "hT_sb")
    xqT = c.sb([128, L], BF16, "xqT")
    xkT = c.sb([128, L], BF16, "xkT")
    tgtT = c.sb([128, L], BF16, "tgtT")
    wqs = c.sb([128, 2 * 128], BF16, "wqs")
    wks = c.sb([128, 2 * 128], BF16, "wks")
    wvs = c.sb([128, 2 * 128], BF16, "wvs")
    lrww = c.sb([128, 2], BF16, "lrww")
    lrow = c.sb([1, L], F32, "lrow")
    identf = c.sb([128, 128], F32, "identf_sb")
    identb = c.sb([128, 128], BF16, "identb_sb")
    maskT = c.sb([16, 16], F32, "maskT_sb")
    masktokT = c.sb([16, 16], F32, "masktokT")
    tokc = c.sb([16, 1], F32, "tokc_sb")
    invc = c.sb([16, 1], F32, "invc_sb")
    t15c = c.sb([16, 1], F32, "t15c")
    gwro = c.sb([1, 128], F32, "gwro")
    gbro = c.sb([1, 128], F32, "gbro")
    gw2r = c.sb([1, 128], F32, "gw2r")
    gbgw = c.sb([1, 128], F32, "gbgw")
    gbgwb = c.sb([16, 128], F32, "gbgwb")
    sgbc = c.sb([16, 1], F32, "sgbc")
    sgb1 = c.sb([1, 1], F32, "sgb1")
    gw2b = c.sb([16, 128], F32, "gw2b")
    gwb16 = c.sb([16, 1024], F32, "gwb16")
    gbb16 = c.sb([16, 1024], F32, "gbb16")
    W1sb = c.sb([128, 2 * 128], F32, "W1sb")
    W1bf = c.sb([128, 2 * 128], BF16, "W1bf")
    b1r = c.sb([1, 2 * 128], F32, "b1r")
    b1bf = c.sb([1, 2 * 128], BF16, "b1bf")
    o1 = c.sb([16, 16], F32, "o1")
    nct = c.sb([16, 16], BF16, "nct")
    s_ = c.sb([16, 1], F32, "s_")
    ssq = c.sb([16, 1], F32, "ssq")
    zsq = c.sb([16, 128], F32, "zsq")
    mu = c.sb([16, 1], F32, "mu")
    m2 = c.sb([16, 1], F32, "m2")
    u_ = c.sb([16, 1], F32, "u_")
    stdv = c.sb([16, 1], F32, "stdv")
    rstd = c.sb([16, 1], F32, "rstd")
    xh = c.sb([16, 128], F32, "xh")
    t1 = c.sb([16, 128], F32, "t1")
    st1 = c.sb([16, 1], F32, "st1")
    sgg = c.sb([16, 1], F32, "sgg")
    gy = c.sb([16, 128], F32, "gy")
    u1 = c.sb([16, 128], F32, "u1")
    mgy = c.sb([16, 1], F32, "mgy")
    q5 = c.sb([16, 128], F32, "q5")
    sgx = c.sb([16, 1], F32, "sgx")
    nmgyx = c.sb([16, 1], F32, "nmgyx")
    aa = c.sb([16, 128], F32, "aa")
    grad = c.sb([16, 128], BF16, "grad")
    sct = c.sb([16, 1], F32, "sct")
    leta = c.sb([16, 1], F32, "leta")
    letab = c.sb([16, 1], BF16, "letab")
    lk = c.sb([16, 128], BF16, "lk")
    xqtok = c.sb([16, 1024], F32, "xqtok")
    zbb = c.sb([16, 1024], F32, "zbb")
    srow = c.sb([16, 8], F32, "srow")
    sqrow = c.sb([16, 8], F32, "sqrow")
    murow = c.sb([16, 8], F32, "murow")
    m2row = c.sb([16, 8], F32, "m2row")
    urow = c.sb([16, 8], F32, "urow")
    stdrow = c.sb([16, 8], F32, "stdrow")
    rstdrow = c.sb([16, 8], F32, "rstdrow")
    zb2 = c.sb([16, 1024], F32, "zb2")
    outc = c.sb([16, 1024], BF16, "outc")
    pbig = c.ps([128, 512], name="pbig")
    pw1 = c.ps([128, 128], name="pw1")
    pzA = c.ps([16, 128], name="pzA")
    pD = c.ps([16, 512], name="pD")
    pgg = c.ps([16, 128], BF16, name="pgg")
    pxq = c.ps([16, 128], BF16, name="pxq")
    pxk = c.ps([16, 128], BF16, name="pxk")
    praw = pD[0:16, 0:16]
    plr = pD[0:16, 16:17]
    pb1 = pD[0:1, 32:160]
    pzbar = pbig[0:16, 0:128]

    G1ap = d["G1"].ap()
    for ch in range(2):
        for cc in range(4):
            igather(c, hT[:, ch * L + cc * 2048: ch * L + (cc + 1) * 2048],
                    G1ap, gidx[:, ch * 4 + cc: ch * 4 + cc + 1], pre=[(d["cc1"], 1)])
        for t, hh in ((wqs, wq_h), (wks, wk_h), (wvs, wv_h)):
            p.op("gpsimd", lambda e, t=t, ch=ch, hh=hh: e.dma_start(
                out=t[:, ch * 128:(ch + 1) * 128], in_=hh[ch * 128:(ch + 1) * 128, :]), dma=True)
    p.op("gpsimd", lambda e: e.dma_start(out=lrww[:], in_=lrw_h[:]), dma=True)
    p.op("sync", lambda e: e.dma_start(out=identf[:], in_=identf_h[:]), dma=True)
    p.op("gpsimd", lambda e: e.dma_start(out=identb[:], in_=identf_h[:]), dma=True)
    p.op("sync", lambda e: e.dma_start(out=maskT[:], in_=maskT_h[:]), dma=True)
    p.op("sync", lambda e: e.dma_start(out=tokc[:], in_=tokc_h[:]), dma=True)
    p.op("sync", lambda e: e.dma_start(out=invc[:], in_=invc_h[:]), dma=True)
    p.op("sync", lambda e: e.dma_start(out=W1sb[:, 0:128], in_=W1_h[:]), dma=True)
    p.op("sync", lambda e: e.dma_start(out=b1r[0:1, 0:128], in_=b1_h[:]), dma=True)
    p.op("sync", lambda e: e.dma_start(out=gwro[:], in_=gw_h[:]), dma=True)
    p.op("sync", lambda e: e.dma_start(out=gbro[:], in_=gb_h[:]), dma=True)
    lrbt = c.sb([1, 1], F32, "lrbt")
    p.op("sync", lambda e: e.dma_start(out=lrbt[:], in_=lrb_h[:]), dma=True)
    gwcol = c.sb([128, 1], F32, "gwcol")
    p.op("sync", lambda e: e.dma_start(out=gwcol[:], in_=gwc_h[:]), dma=True)
    p.bar()
    p.op("vector", lambda e: e.tensor_add(tokc[:], tokc[:], invc[:]))
    p.bar()
    p.op("vector", lambda e: e.tensor_scalar_max(tokc[:], tokc[:], 0.0))
    p.bar()
    tokrow = c.sb([1, 16], F32, "tokrow")
    p.op("tensor", lambda e: e.matmul(pD[0:1, 256:272], tokc[:], identf[0:16, 0:16], start=True, stop=True))
    p.bar()
    p.op("vector", lambda e: e.tensor_copy(tokrow[:], pD[0:1, 256:272]))
    p.bar()
    p.op("tensor", lambda e: e.matmul(praw, c.ones_row[0:1, 0:16], tokrow[:], start=True, stop=True))
    p.bar()
    p.op("vector", lambda e: e.tensor_mul(masktokT[:], maskT[:], praw))
    p.bar()
    p.op("tensor", lambda e: e.matmul(pD[0:1, 272:273], tokc[:], identf[0:16, 15:16], start=True, stop=True))
    p.bar()
    tok15s = c.sb([1, 1], F32, "tok15s")
    p.op("vector", lambda e: e.tensor_copy(tok15s[:], pD[0:1, 272:273]))
    p.bar()
    p.op("tensor", lambda e: e.matmul(pD[0:16, 288:289], c.ones_row[0:1, 0:16], tok15s[:], start=True, stop=True))
    p.bar()
    p.op("vector", lambda e: e.tensor_scalar(t15c[:], pD[0:16, 288:289], -1.0, None, ALU.mult, ALU.bypass))
    p.bar()
    p.op("vector", lambda e: e.tensor_mul(gw2r[:], gwro[:], gwro[:]))
    p.op("gpsimd", lambda e: e.tensor_mul(gbgw[:], gbro[:], gwro[:]))
    p.bar()
    p.op("tensor", lambda e: e.matmul(pbig[0:16, 0:128], c.ones_row[0:1, 0:16], gw2r[:], start=True, stop=True))
    p.bar()
    p.op("vector", lambda e: e.tensor_copy(gw2b[:], pbig[0:16, 0:128]))
    p.bar()
    p.op("tensor", lambda e: e.matmul(pbig[0:16, 0:128], c.ones_row[0:1, 0:16], gbgw[:], start=True, stop=True))
    p.bar()
    p.op("vector", lambda e: e.tensor_copy(gbgwb[:], pbig[0:16, 0:128]))
    p.bar()
    p.op("vector", lambda e: e.reduce_sum(sgb1[:], gbgw[:], axis=AX.X))
    p.bar()
    p.op("tensor", lambda e: e.matmul(pbig[0:16, 128:129], c.ones_row[0:1, 0:16], sgb1[:], start=True, stop=True))
    p.bar()
    p.op("vector", lambda e: e.tensor_copy(sgbc[:], pbig[0:16, 128:129]))
    p.bar()
    for r in range(8):
        p.op("tensor", lambda e: e.matmul(pbig[0:16, 0:128], c.ones_row[0:1, 0:16], gwro[:], start=True, stop=True))
        p.bar()
        p.op("vector", lambda e, r=r: e.tensor_copy(gwb16[:, r * 128:(r + 1) * 128], pbig[0:16, 0:128]))
        p.bar()
        p.op("tensor", lambda e: e.matmul(pbig[0:16, 0:128], c.ones_row[0:1, 0:16], gbro[:], start=True, stop=True))
        p.bar()
        p.op("vector", lambda e, r=r: e.tensor_copy(gbb16[:, r * 128:(r + 1) * 128], pbig[0:16, 0:128]))
        p.bar()
    p.op("vector", lambda e: e.tensor_copy(W1bf[:, 0:128], W1sb[:, 0:128]))
    p.op("scalar", lambda e: e.activation(b1bf[0:1, 0:128], b1r[0:1, 0:128], AF.Copy, bias=0.0, scale=1.0))
    p.bar()
    for (dst, wt) in ((xqT, wqs), (xkT, wks), (tgtT, wvs)):
        for sl in range(0, L, 512):
            for ch in range(2):
                p.op("tensor", lambda e, dst=dst, wt=wt, sl=sl, ch=ch: e.matmul(
                    pbig[:, 0:512], wt[:, ch * 128:(ch + 1) * 128],
                    hT[:, ch * L + sl: ch * L + sl + 512], start=(ch == 0), stop=(ch == 1)))
            p.bar()
            p.op("vector", lambda e, dst=dst, sl=sl: e.tensor_copy(dst[:, sl:sl + 512], pbig[:, 0:512]))
            p.bar()
    _w = min(2048, L)
    for sl in range(0, L, _w):
        p.op("vector", lambda e, sl=sl, _w=_w: e.tensor_sub(tgtT[:, sl:sl + _w], tgtT[:, sl:sl + _w], xkT[:, sl:sl + _w]))
        p.bar()
    for sl in range(0, L, 512):
        for ch in range(2):
            p.op("tensor", lambda e, sl=sl, ch=ch: e.matmul(
                pD[0:1, 0:512], lrww[:, ch:ch + 1], hT[:, ch * L + sl:ch * L + sl + 512],
                start=(ch == 0), stop=(ch == 1)))
        p.bar()
        p.op("scalar", lambda e, sl=sl: e.activation(lrow[0:1, sl:sl + 512], pD[0:1, 0:512], AF.Sigmoid, bias=lrbt[0:1, 0:1], scale=1.0))
        p.bar()
    outs_ap = d["outs_loc"].ap()
    for n in range(NM):
        par = n % 2
        q = n % 8
        t0 = n * K
        W1b_cur = W1bf[:, par * 128:(par + 1) * 128]
        W1s_cur = W1sb[:, par * 128:(par + 1) * 128]
        W1b_nxt = W1bf[:, (1 - par) * 128:(2 - par) * 128]
        W1s_nxt = W1sb[:, (1 - par) * 128:(2 - par) * 128]
        b1_cur = b1r[0:1, par * 128:(par + 1) * 128]
        b1b_cur = b1bf[0:1, par * 128:(par + 1) * 128]
        b1_nxt = b1r[0:1, (1 - par) * 128:(2 - par) * 128]
        b1b_nxt = b1bf[0:1, (1 - par) * 128:(2 - par) * 128]
        xk_sl = xkT[:, t0:t0 + K]
        xq_sl = xqT[:, t0:t0 + K]
        tg_sl = tgtT[:, t0:t0 + K]
        p.op("tensor", lambda e, xk_sl=xk_sl, xq_sl=xq_sl: e.matmul(praw, xk_sl, xq_sl, start=True, stop=True))
        p.op("tensor", lambda e, n=n: e.transpose(plr, lrow[0:1, n * K:(n + 1) * K], identf[0:1, 0:1]))
        p.op("tensor", lambda e, tg_sl=tg_sl: e.transpose(pgg[:], tg_sl, identb[:]))
        p.op("tensor", lambda e, xq_sl=xq_sl: e.transpose(pxq[:], xq_sl, identb[:]))
        p.op("tensor", lambda e, xk_sl=xk_sl: e.transpose(pxk[:], xk_sl, identb[:]))
        p.op("tensor", lambda e, xk_sl=xk_sl, W1b_cur=W1b_cur: e.matmul(pzA[:], xk_sl, W1b_cur, start=True, stop=False))
        p.op("tensor", lambda e, b1_cur=b1_cur: e.matmul(pzA[:], c.ones_row[0:1, 0:16], b1_cur, start=False, stop=True))
        p.op("tensor", lambda e, xq_sl=xq_sl, W1b_cur=W1b_cur: e.matmul(pzbar, xq_sl, W1b_cur, start=True, stop=False))
        p.op("tensor", lambda e, b1_cur=b1_cur: e.matmul(pzbar, c.ones_row[0:1, 0:16], b1_cur, start=False, stop=False, skip_group_check=True))
        p.bar()
        p.op("vector", lambda e: e.reduce_sum(s_[:], pzA[:], axis=AX.X))
        p.op("scalar", lambda e, q=q: e.activation(xqtok[:, q * 128:(q + 1) * 128], pxq[:], AF.Copy, bias=0.0, scale=1.0))
        p.op("vector", lambda e: e.tensor_mul(u1[:], pgg[:], gwb16[:, 0:128]))
        p.bar()
        p.op("vector", lambda e: e.scalar_tensor_tensor(o1[:], praw, 1.0, masktokT[:], ALU.add, ALU.mult))
        p.op("scalar", lambda e: e.activation(zsq[:], pzA[:], AF.Square, bias=0.0, scale=1.0, accum_out=ssq[:]))
        p.op("vector", lambda e: e.tensor_sub(gy[:], gbgwb[:], u1[:]))
        p.op("vector", lambda e: e.reduce_sum(sgg[:], u1[:], axis=AX.X))
        p.bar()
        p.op("vector", lambda e: e.tensor_scalar(mu[:], s_[:], 1.0 / 128, None, ALU.mult, ALU.bypass))
        p.op("gpsimd", lambda e: e.tensor_scalar(m2[:], ssq[:], 1.0 / 128, None, ALU.mult, ALU.bypass))
        p.op("vector", lambda e: e.tensor_scalar(sct[:], plr, -1.0 / 128, None, ALU.mult, ALU.bypass))
        p.bar()
        p.op("vector", lambda e: e.scalar_tensor_tensor(u_[:], mu[:], mu[:, 0:1], m2[:], ALU.mult, ALU.subtract))
        p.op("gpsimd", lambda e: e.tensor_scalar(leta[:], sct[:], t15c[:, 0:1], None, ALU.mult, ALU.bypass))
        p.op("vector", lambda e: e.tensor_scalar(nct[:], o1[:], sct[:, 0:1], None, ALU.mult, ALU.bypass))
        p.bar()
        p.op("scalar", lambda e: e.activation(stdv[:], u_[:], AF.Sqrt, bias=1e-6, scale=-1.0))
        p.op("vector", lambda e: e.tensor_copy(letab[:], leta[:]))
        p.op("vector", lambda e: e.tensor_scalar(lk[:], pxk[:], leta[:, 0:1], None, ALU.mult, ALU.bypass))
        p.bar()
        p.op("vector", lambda e: e.reciprocal(rstd[:], stdv[:]))
        p.bar()
        p.op("vector", lambda e: e.tensor_scalar(xh[:], pzA[:], mu[:, 0:1], rstd[:, 0:1], ALU.subtract, ALU.mult))
        p.bar()
        p.op("vector", lambda e: e.scalar_tensor_tensor(t1[:], xh[:], 1.0, gw2b[:], ALU.mult, ALU.mult, accum_out=st1[:]))
        p.bar()
        p.op("vector", lambda e: e.tensor_add(gy[:], gy[:], t1[:]))
        p.op("gpsimd", lambda e: e.tensor_sub(mgy[:], st1[:], sgg[:]))
        p.bar()
        p.op("vector", lambda e: e.scalar_tensor_tensor(q5[:], gy[:], -1.0 / 128, xh[:], ALU.mult, ALU.mult, accum_out=sgx[:]))
        p.op("gpsimd", lambda e: e.tensor_scalar(mgy[:], mgy[:], sgbc[:, 0:1], 1.0 / 128, ALU.add, ALU.mult))
        p.bar()
        p.op("vector", lambda e: e.scalar_tensor_tensor(aa[:], xh[:], sgx[:, 0:1], gy[:], ALU.mult, ALU.add))
        p.bar()
        p.op("vector", lambda e: e.tensor_scalar(grad[:], aa[:], mgy[:, 0:1], rstd[:, 0:1], ALU.subtract, ALU.mult))
        p.bar()
        p.op("tensor", lambda e: e.matmul(pzbar, nct[:], grad[:], start=False, stop=True, skip_group_check=True))
        p.op("tensor", lambda e: e.matmul(pw1[:], lk[:], grad[:], start=True, stop=True))
        p.op("tensor", lambda e: e.matmul(pb1, letab[:], grad[:], start=True, stop=True))
        p.bar()
        p.op("vector", lambda e, W1s_nxt=W1s_nxt, W1s_cur=W1s_cur: e.tensor_sub(W1s_nxt, W1s_cur, pw1[:]))
        p.op("vector", lambda e, W1b_nxt=W1b_nxt, W1s_cur=W1s_cur: e.tensor_sub(W1b_nxt, W1s_cur, pw1[:]))
        p.op("vector", lambda e, b1_nxt=b1_nxt, b1_cur=b1_cur: e.tensor_sub(b1_nxt, b1_cur, pb1))
        p.op("scalar", lambda e, q=q: e.activation(zbb[:, q * 128:(q + 1) * 128], pzbar, AF.Copy, bias=0.0, scale=1.0))
        p.bar()
        if q == 7:
            g0 = n - 7
            sview = zbb[:].rearrange("p (s n2) -> p s n2", s=8)
            p.op("vector", lambda e: e.reduce_sum(srow[:], sview, axis=AX.X))
            p.op("gpsimd", lambda e: e.tensor_mul(zb2[:], zbb[:], zbb[:]))
            p.bar()
            p.op("vector", lambda e: e.reduce_sum(sqrow[:], zb2[:].rearrange("p (s n2) -> p s n2", s=8), axis=AX.X))
            p.op("gpsimd", lambda e: e.tensor_scalar(murow[:], srow[:], 1.0 / 128, None, ALU.mult, ALU.bypass))
            p.bar()
            p.op("vector", lambda e: e.tensor_scalar(m2row[:], sqrow[:], 1.0 / 128, None, ALU.mult, ALU.bypass))
            p.bar()
            p.op("vector", lambda e: e.tensor_mul(urow[:], murow[:], murow[:]))
            p.bar()
            p.op("vector", lambda e: e.tensor_sub(urow[:], urow[:], m2row[:]))
            p.bar()
            p.op("scalar", lambda e: e.activation(stdrow[:], urow[:], AF.Sqrt, bias=1e-6, scale=-1.0))
            p.bar()
            p.op("vector", lambda e: e.reciprocal(rstdrow[:], stdrow[:]))
            p.bar()
            p.op("vector", lambda e: e.tensor_sub(
                zb2[:].rearrange("p (s n2) -> p s n2", s=8), sview,
                murow[:].rearrange("p s -> p s ()").broadcast_to((16, 8, 128))))
            p.bar()
            p.op("vector", lambda e: e.tensor_mul(
                zb2[:].rearrange("p (s n2) -> p s n2", s=8),
                zb2[:].rearrange("p (s n2) -> p s n2", s=8),
                rstdrow[:].rearrange("p s -> p s ()").broadcast_to((16, 8, 128))))
            p.bar()
            p.op("gpsimd", lambda e: e.tensor_add(xqtok[:], xqtok[:], gbb16[:]))
            p.bar()
            p.op("vector", lambda e: e.tensor_mul(zb2[:], zb2[:], gwb16[:]))
            p.bar()
            p.op("vector", lambda e: e.tensor_add(outc[:], zb2[:], xqtok[:]))
            p.bar()
            p.op("sync", lambda e, g0=g0: e.dma_start(
                out=outs_ap[g0 * K:g0 * K + 128, :].rearrange("(j k) f -> k j f", k=16),
                in_=outc[:].rearrange("k (j f) -> k j f", j=8)), dma=True)
            p.bar()
    p.op("gpsimd", lambda e: e.collective_compute(
        "AllGather", ALU.bypass, replica_groups=[[0, 1, 2, 3], [4, 5, 6, 7]],
        ins=[d["outs_loc"].ap().opt()], outs=[d["G2"].ap().opt()]).then_inc(d["cc2"], 1), inc=0)
    p.bar()
    c.pop_scope()


def p3_emit(c, d):
    c.prefix = "s3_"
    T = TPC
    p = c.p
    x_h = d["xT"]
    wo_h = d["p3_wo"]
    up_h = d["p3_up"]
    pw_h = d["p3_pw"]
    pb_h = d["p3_pb"]
    lw_h = d["p3_lw"]
    lb_h = d["p3_lb"]
    ub_h = d["p3_ub"]
    gidx = d["gidx"]
    c.push_scope()
    ob = c.sb([128, 2 * T], BF16, "ob")
    identb = c.sb([128, 128], BF16, "identb3")
    p.op("gpsimd", lambda e: e.dma_start(out=identb[:], in_=d["identf"][:]), dma=True)
    ttA = c.sb([128, 128], BF16, "ttA")
    ttB = c.sb([128, 128], BF16, "ttB")
    psT = c.ps([128, 128], BF16, name="psT")
    psU = c.ps([128, 128], BF16, name="psU")
    G2v = d["G2"].ap()
    p.bar()
    for hh in range(2):
        for tt in range(16):
            col = 8 + hh * 16 + tt
            tile = ttA if tt % 2 == 0 else ttB
            pst = psT if tt % 2 == 0 else psU
            igather(c, tile[:], G2v, gidx[:, col:col + 1], pre=[(d["cc2"], 1)])
            p.bar()
            p.op("tensor", lambda e, tile=tile, pst=pst: e.transpose(pst[:], tile[:], identb[:]))
            p.bar()
            p.op("vector", lambda e, hh=hh, tt=tt, pst=pst: e.tensor_copy(
                ob[:, hh * T + tt * 128: hh * T + tt * 128 + 128], pst[:]))
            p.bar()
    xT = c.sb([128, 8 * T], F32, "xT_sb")
    for ch in range(8):
        p.op("sync", lambda e, ch=ch: e.dma_start(out=xT[:, ch * T:(ch + 1) * T], in_=x_h[ch * 128:(ch + 1) * 128, :]), dma=True)
    wo = load_w_bf16(c, wo_h, 256, 256, "wo_sb")
    up = load_w_bf16(c, up_h, 256, D, "up_sb")
    small = {}
    for nm, hh, ncol in (("pw", pw_h, 2), ("pb", pb_h, 2), ("lw", lw_h, 2), ("lb", lb_h, 2), ("ub", ub_h, 8)):
        t = c.sb([128, ncol], F32, nm)
        p.op("sync", lambda e, t=t, hh=hh: e.dma_start(out=t[:], in_=hh[:]), dma=True)
        small[nm] = t
    p.bar()
    ps2 = c.ps([1, 1024], name="ps_stat")
    psA = c.ps([128, 512], name="psA")
    psB = c.ps([128, 512], name="psB")
    pbv = c.sb([128, 2], F32, "pbv")
    pb_bf = c.sb([128, 2], BF16, "pb_bf")
    p.op("vector", lambda e: e.tensor_copy(pb_bf[:], small["pb"][:]))
    p.bar()
    for fo in range(2):
        for ch in range(2):
            p.op("tensor", lambda e, fo=fo, ch=ch: e.matmul(
                psA[:, 0:1], wo[:, ch * 256 + fo * 128: ch * 256 + fo * 128 + 128],
                pb_bf[:, ch:ch + 1], start=(ch == 0), stop=(ch == 1)))
        p.bar()
        p.op("vector", lambda e, fo=fo: e.tensor_copy(pbv[:, fo:fo + 1], psA[:, 0:1]))
        p.bar()
    for ch in range(2):
        p.op("vector", lambda e, ch=ch: e.tensor_scalar(
            wo[:, ch * 256:(ch + 1) * 256], wo[:, ch * 256:(ch + 1) * 256],
            small["pw"][:, ch:ch + 1], None, ALU.mult, ALU.bypass))
    p.bar()
    st1 = ln_stats(c, ob, 256, T, 1e-5, ps2)
    negcs1 = colsum_neg_row(c, wo, 256, 256, ps2)
    rstd1bc = bcast_row(c, st1["rstd"], T, psA)
    o3 = c.sb([128, 2 * T], BF16, "o3")

    def cb1(fo, sl):
        return o3[:, fo * T + sl: fo * T + sl + 512]

    ln_proj(c, ob, 256, T, wo, 256, st1["mu"], rstd1bc, psA, cb1,
            bias_col=pbv, act=AF.Copy, negcs_f32=negcs1, wt_stride=256)
    biasv = c.sb([128, 8], F32, "biasv")
    lb_bf = c.sb([128, 2], BF16, "lb_bf")
    p.op("vector", lambda e: e.tensor_copy(lb_bf[:], small["lb"][:]))
    p.bar()
    for fo in range(8):
        for ch in range(2):
            p.op("tensor", lambda e, fo=fo, ch=ch: e.matmul(
                psA[:, 0:1], up[:, ch * D + fo * 128: ch * D + fo * 128 + 128],
                lb_bf[:, ch:ch + 1], start=(ch == 0), stop=(ch == 1)))
        p.bar()
        p.op("vector", lambda e, fo=fo: e.tensor_add(biasv[:, fo:fo + 1], psA[:, 0:1], small["ub"][:, fo:fo + 1]))
        p.bar()
    for ch in range(2):
        p.op("vector", lambda e, ch=ch: e.tensor_scalar(
            up[:, ch * D:(ch + 1) * D], up[:, ch * D:(ch + 1) * D],
            small["lw"][:, ch:ch + 1], None, ALU.mult, ALU.bypass))
    p.bar()
    st2 = ln_stats(c, o3, 256, T, 1e-5, ps2)
    rstd2bc = bcast_row(c, st2["rstd"], T, psA)
    negcs2 = colsum_neg_row(c, up, 256, D, ps2)
    tmp = c.sb([128, 512], F32, "l3tmp")
    x1_ap = d["x1_loc"].ap()
    for fo in range(8):
        for sl in range(0, T, 512):
            for ch in range(2):
                p.op("tensor", lambda e, fo=fo, sl=sl, ch=ch: e.matmul(
                    psB[:, 0:512], up[:, ch * D + fo * 128: ch * D + fo * 128 + 128],
                    o3[:, ch * T + sl: ch * T + sl + 512], start=(ch == 0), stop=False))
            p.op("tensor", lambda e, fo=fo, sl=sl: e.matmul(
                psB[:, 0:512], negcs2[0:1, fo * 128:fo * 128 + 128],
                st2["mu"][0:1, sl:sl + 512], start=False, stop=True))
            p.bar()
            p.op("vector", lambda e, sl=sl: e.tensor_mul(tmp[:], psB[:, 0:512], rstd2bc[:, sl:sl + 512]))
            p.bar()
            p.op("vector", lambda e, fo=fo, sl=sl: e.scalar_tensor_tensor(
                xT[:, fo * T + sl: fo * T + sl + 512], tmp[:], biasv[:, fo:fo + 1],
                xT[:, fo * T + sl: fo * T + sl + 512], ALU.add, ALU.add))
            p.bar()
    for ch in range(8):
        p.op("sync", lambda e, ch=ch: e.dma_start(out=x1_ap[ch * 128:(ch + 1) * 128, :], in_=xT[:, ch * T:(ch + 1) * T]), dma=True)
    p.bar()
    c.pop_scope()
    if hasattr(c, "_lnp_tmp"):
        del c._lnp_tmp


def p4_emit(c, d):
    c.prefix = "s4_"
    T = TPC
    p = c.p
    x_h = d["x1_loc"].ap()
    wT_h = d["p4_WT"]
    ldT_h = d["p4_ldT"]
    da_h = d["p4_da"]
    luT_h = d["p4_luT"]
    uaT_h = d["p4_uaT"]
    b_h = d["p4_b"]
    c.push_scope()
    xT = c.sb([128, 8 * T], F32, "xT_sb")
    xb = c.sb([128, 8 * T], BF16, "xb")
    for ch in range(8):
        p.op("sync", lambda e, ch=ch: e.dma_start(out=xT[:, ch * T:(ch + 1) * T], in_=x_h[ch * 128:(ch + 1) * 128, :]), dma=True)
        p.op("gpsimd", lambda e, ch=ch: e.dma_start(out=xb[:, ch * T:(ch + 1) * T], in_=x_h[ch * 128:(ch + 1) * 128, :]), dma=True)
    weff = c.sb([128, 8 * D], BF16, "weff")
    ldT = c.sb([100, 4], BF16, "ldT_sb")
    da = c.sb([100, D], BF16, "da_sb")
    luT = c.sb([4, 50], BF16, "luT_sb")
    uaT = c.sb([50, D], BF16, "uaT_sb")
    bcol = c.sb([128, 8], F32, "bcol_sb")
    for t, hh in ((ldT, ldT_h), (da, da_h), (luT, luT_h), (uaT, uaT_h)):
        p.op("gpsimd", lambda e, t=t, hh=hh: e.dma_start(out=t[:], in_=hh[:]), dma=True)
    p.op("sync", lambda e: e.dma_start(out=bcol[:], in_=b_h[:]), dma=True)
    p.bar()
    ps2 = c.ps([1, 1024], name="ps_stat")
    psA = c.ps([128, 512], name="psA")
    psB = c.ps([128, 512], name="psB")
    psC = c.ps([128, 1024], name="psC")
    t1 = c.sb([4, D], BF16, "t1_li")
    for sl in range(0, D, 512):
        p.op("tensor", lambda e, sl=sl: e.matmul(psA[0:4, 0:512], ldT[:], da[:, sl:sl + 512], start=True, stop=True))
        p.bar()
        p.op("vector", lambda e, sl=sl: e.tensor_copy(t1[:, sl:sl + 512], psA[0:4, 0:512]))
        p.bar()
    t2 = c.sb([50, D], BF16, "t2_li")
    for sl in range(0, D, 512):
        p.op("tensor", lambda e, sl=sl: e.matmul(psA[0:50, 0:512], luT[:], t1[:, sl:sl + 512], start=True, stop=True))
        p.bar()
        p.op("vector", lambda e, sl=sl: e.tensor_copy(t2[:, sl:sl + 512], psA[0:50, 0:512]))
        p.bar()
    for ch in range(8):
        p.op("gpsimd", lambda e, ch=ch: e.dma_start(out=weff[:, ch * D:(ch + 1) * D], in_=wT_h[ch * 128:(ch + 1) * 128, :]), dma=True)
    p.bar()
    for ch in range(8):
        for sl in range(0, D, 512):
            p.op("tensor", lambda e, ch=ch, sl=sl: e.matmul(
                psC[:, 0:512], t2[:, ch * 128:(ch + 1) * 128], uaT[:, sl:sl + 512], start=True, stop=True))
            p.bar()
            p.op("vector", lambda e, ch=ch, sl=sl: e.tensor_add(
                weff[:, ch * D + sl: ch * D + sl + 512], weff[:, ch * D + sl: ch * D + sl + 512], psC[:, 0:512]))
            p.bar()
    st = ln_stats(c, xb, D, T, 1e-5, ps2)
    negcs = colsum_neg_row(c, weff, D, D, ps2)
    rstd_bc = bcast_row(c, st["rstd"], T, psA)
    tmp = c.sb([128, 512], F32, "l4tmp")
    x2b = c.sb([128, 8 * T], BF16, "x2b")
    for fo in range(8):
        for sl in range(0, T, 512):
            for ch in range(8):
                p.op("tensor", lambda e, fo=fo, sl=sl, ch=ch: e.matmul(
                    psB[:, 0:512], weff[:, ch * D + fo * 128: ch * D + fo * 128 + 128],
                    xb[:, ch * T + sl: ch * T + sl + 512], start=(ch == 0), stop=False))
            p.op("tensor", lambda e, fo=fo, sl=sl: e.matmul(
                psB[:, 0:512], negcs[0:1, fo * 128:fo * 128 + 128],
                st["mu"][0:1, sl:sl + 512], start=False, stop=True))
            p.bar()
            p.op("vector", lambda e, sl=sl: e.tensor_mul(tmp[:], psB[:, 0:512], rstd_bc[:, sl:sl + 512]))
            p.bar()
            p.op("vector", lambda e, fo=fo, sl=sl: e.scalar_tensor_tensor(
                xT[:, fo * T + sl: fo * T + sl + 512], tmp[:], bcol[:, fo:fo + 1],
                xT[:, fo * T + sl: fo * T + sl + 512], ALU.add, ALU.add))
            p.bar()
            p.op("vector", lambda e, fo=fo, sl=sl: e.tensor_copy(
                x2b[:, fo * T + sl: fo * T + sl + 512], xT[:, fo * T + sl: fo * T + sl + 512]))
            p.bar()
    st2 = ln_stats(c, x2b, D, T, 1e-5, ps2, rows=st)
    rstd2bc = bcast_row(c, st2["rstd"], T, psA, out=rstd_bc)
    part = c.sb([128, 8], F32, "part_sb")
    scr = c.sb([128, 512], F32, "l4scr")
    pcol = c.sb([128, 1], F32, "pcol")
    smur = c.sb([1, 1], F32, "smur")
    p.op("vector", lambda e: e.reduce_sum(smur[:], st2["mur"][:], axis=AX.X))
    p.bar()
    smurc = c.sb([128, 1], F32, "smurc")
    p.op("tensor", lambda e: e.matmul(psA[:, 0:1], c.ones_row[:], smur[:], start=True, stop=True))
    p.bar()
    p.op("vector", lambda e: e.tensor_copy(smurc[:], psA[:, 0:1]))
    p.bar()
    for ch in range(8):
        for sl in range(0, T, 512):
            p.op("vector", lambda e, ch=ch, sl=sl: e.scalar_tensor_tensor(
                scr[:], x2b[:, ch * T + sl: ch * T + sl + 512], 1.0,
                rstd2bc[:, sl:sl + 512], ALU.mult, ALU.mult, accum_out=pcol[:]))
            p.bar()
            if sl == 0:
                p.op("vector", lambda e, ch=ch: e.tensor_copy(part[:, ch:ch + 1], pcol[:]))
            else:
                p.op("vector", lambda e, ch=ch: e.tensor_add(part[:, ch:ch + 1], part[:, ch:ch + 1], pcol[:]))
            p.bar()
        p.op("vector", lambda e, ch=ch: e.tensor_sub(part[:, ch:ch + 1], part[:, ch:ch + 1], smurc[:]))
        p.bar()
    x2_ap = d["x2_loc"].ap()
    r2_ap = d["r2_loc"].ap()
    for ch in range(8):
        p.op("sync", lambda e, ch=ch: e.dma_start(out=x2_ap[ch * 128:(ch + 1) * 128, :], in_=xT[:, ch * T:(ch + 1) * T]), dma=True)
    p.op("sync", lambda e: e.dma_start(out=d["part_loc"].ap(), in_=part[:]), dma=True)
    p.op("sync", lambda e: e.dma_start(out=r2_ap[0:1, :], in_=st2["mu"][:]), dma=True)
    p.op("sync", lambda e: e.dma_start(out=r2_ap[1:2, :], in_=st2["rstd"][:]), dma=True)
    p.bar()
    B3 = d["B3"].ap()
    p.op("sync", lambda e: e.dma_start(out=B3[0:1024, :], in_=x2_ap[:, T - 40:T]), dma=True)
    p.op("sync", lambda e: e.dma_start(out=B3[1024:1026, :], in_=r2_ap[:, T - 40:T]), dma=True)
    p.bar()
    p.op("gpsimd", lambda e: e.collective_compute(
        "AllReduce", ALU.add, replica_groups=[[0, 1, 2, 3], [4, 5, 6, 7]],
        ins=[d["part_loc"].ap().opt()], outs=[d["partsum_loc"].ap().opt()]).then_inc(d["cc3a"], 1), inc=0)
    p.op("gpsimd", lambda e: e.collective_compute(
        "AllGather", ALU.bypass, replica_groups=[list(range(NCORE))],
        ins=[d["B3"].ap().opt()], outs=[d["G3"].ap().opt()]).then_inc(d["cc3b"], 1), inc=0)
    p.bar()
    c.pop_scope()
    if hasattr(c, "_lnp_tmp"):
        del c._lnp_tmp


def p5_emit(c, d):
    c.prefix = "s5_"
    T = TPC
    HL = 40
    p = c.p
    x_h = d["x2_loc"].ap()
    r2_h = d["r2_loc"].ap()
    hv_h = d["p5_hv"]
    w1_h = d["p5_W1"]
    w2_h = d["p5_W2"]
    mw_h = d["p5_mwT"]
    b1_h = d["p5_b1c"]
    b2_h = d["p5_b2c"]
    gidx = d["gidx"]
    gidx1 = d["gidx1"]
    G3v = d["G3"].ap()
    c.push_scope()
    xb = c.sb([128, 8 * T], BF16, "xb")
    for ch in range(8):
        p.op("gpsimd", lambda e, ch=ch: e.dma_start(out=xb[:, ch * T:(ch + 1) * T], in_=x_h[ch * 128:(ch + 1) * 128, :]), dma=True)
    xhb = c.sb([128, 8 * HL], BF16, "xhb")
    xhf = c.sb([128, 8 * HL], F32, "xhf")
    for ch in range(8):
        igather(c, xhf[:, ch * HL:(ch + 1) * HL], G3v, gidx[:, 40 + ch:40 + ch + 1], pre=[(d["cc3b"], 1)])
    w1 = load_w_bf16(c, w1_h, D, D, "w1_sb")
    w2 = load_w_bf16(c, w2_h, D, D, "w2_sb")
    mwT = c.sb([128, 8 * 41], F32, "mwT_sb")
    for ch in range(8):
        p.op("sync", lambda e, ch=ch: e.dma_start(out=mwT[:, ch * 41:(ch + 1) * 41], in_=mw_h[ch * 128:(ch + 1) * 128, :]), dma=True)
    smalls = {}
    for nm, hh in (("hv", hv_h), ("b1", b1_h), ("b2", b2_h)):
        shp = [128, 1] if nm == "hv" else [128, 8]
        t = c.sb(shp, F32, nm + "_sb")
        p.op("sync", lambda e, t=t, hh=hh: e.dma_start(out=t[:], in_=hh[:]), dma=True)
        smalls[nm] = t
    for nm, row, w in (("r2mu", 0, T), ("r2rs", 1, T)):
        t = c.sb([1, w], F32, nm + "_sb")
        p.op("sync", lambda e, t=t, row=row: e.dma_start(out=t[:], in_=r2_h[row:row + 1, :]), dma=True)
        smalls[nm] = t
    for nm, colI in (("r2hmu", 0), ("r2hrs", 1)):
        t = c.sb([2, HL], F32, nm + "_sb")
        igather(c, t[:], G3v, gidx1[0:2, colI:colI + 1], pre=[(d["cc3b"], 1)])
        smalls[nm] = t
    lnsum = c.sb([128, 8], F32, "lnsum")
    p.op("sync", lambda e: e.dma_start(out=lnsum[:], in_=d["partsum_loc"].ap()), dma=True, pre=[(d["cc3a"], 1)])
    p.bar()
    for ch in range(8):
        p.op("vector", lambda e, ch=ch: e.tensor_copy(xhb[:, ch * HL:(ch + 1) * HL], xhf[:, ch * HL:(ch + 1) * HL]))
    p.bar()
    ps2 = c.ps([1, 1024], name="ps_stat")
    psA = c.ps([128, 512], name="psA")
    psB = c.ps([128, 512], name="psB")
    psH = c.ps([128, 40], name="psH")
    tapT = c.sb([128, 8 * 40], F32, "tapT")
    for ch in range(8):
        p.op("vector", lambda e, ch=ch: e.scalar_tensor_tensor(
            tapT[:, ch * 40:(ch + 1) * 40], mwT[:, ch * 41 + 1:(ch + 1) * 41], 1.0,
            mwT[:, ch * 41:ch * 41 + 1].broadcast_to((128, 40)), ALU.mult, ALU.subtract))
    p.bar()
    PW = HL + T
    pT = c.sb([128, 8 * PW], BF16, "pT")
    negcs = colsum_neg_row(c, w1, D, D, ps2)
    mu_r = smalls["r2mu"][:]
    rstd_r = smalls["r2rs"][:]
    rstd_bc = bcast_row(c, rstd_r, T, psA)
    tmp = c.sb([128, 512], F32, "l5tmp")
    for fo in range(8):
        for sl in range(0, T, 512):
            for ch in range(8):
                p.op("tensor", lambda e, fo=fo, sl=sl, ch=ch: e.matmul(
                    psB[:, 0:512], w1[:, ch * D + fo * 128: ch * D + fo * 128 + 128],
                    xb[:, ch * T + sl: ch * T + sl + 512], start=(ch == 0), stop=False))
            p.op("tensor", lambda e, fo=fo, sl=sl: e.matmul(
                psB[:, 0:512], negcs[0:1, fo * 128:fo * 128 + 128],
                mu_r[0:1, sl:sl + 512], start=False, stop=True))
            p.bar()
            p.op("vector", lambda e, sl=sl: e.tensor_mul(tmp[:], psB[:, 0:512], rstd_bc[:, sl:sl + 512]))
            p.bar()
            p.op("vector", lambda e, fo=fo, sl=sl: e.tensor_scalar(
                pT[:, fo * PW + HL + sl: fo * PW + HL + sl + 512], tmp[:],
                smalls["b1"][:, fo:fo + 1], None, ALU.add, ALU.bypass))
            p.bar()
    rstdh_bc = c.sb([128, HL], F32, "rstdh_bc")
    p.op("tensor", lambda e: e.matmul(psH[:], c.ones_row[:], smalls["r2hrs"][0:1, :], start=True, stop=True))
    p.bar()
    p.op("vector", lambda e: e.tensor_copy(rstdh_bc[:], psH[:]))
    p.bar()
    for fo in range(8):
        for ch in range(8):
            p.op("tensor", lambda e, fo=fo, ch=ch: e.matmul(
                psH[:], w1[:, ch * D + fo * 128: ch * D + fo * 128 + 128],
                xhb[:, ch * HL:(ch + 1) * HL], start=(ch == 0), stop=False))
        p.op("tensor", lambda e, fo=fo: e.matmul(
            psH[:], negcs[0:1, fo * 128:fo * 128 + 128],
            smalls["r2hmu"][0:1, :], start=False, stop=True))
        p.bar()
        p.op("vector", lambda e: e.tensor_mul(tmp[:, 0:HL], psH[:], rstdh_bc[:]))
        p.bar()
        p.op("vector", lambda e, fo=fo: e.tensor_scalar(
            tmp[:, 64:64 + HL], tmp[:, 0:HL], smalls["b1"][:, fo:fo + 1], None, ALU.add, ALU.bypass))
        p.bar()
        p.op("vector", lambda e, fo=fo: e.tensor_scalar(
            pT[:, fo * PW: fo * PW + HL], tmp[:, 64:64 + HL], smalls["hv"][:, 0:1], None, ALU.mult, ALU.bypass))
        p.bar()
    accV = c.sb([128, T], F32, "accV")
    accG = c.sb([128, T], F32, "accG")
    lnsumb = c.sb([128, 8], BF16, "lnsumb")
    totcol = c.sb([128, 8], F32, "totcol")
    b18 = c.sb([128, 8], F32, "b18")
    mtcol = c.sb([128, 8], F32, "mtcol")
    p.op("scalar", lambda e: e.mul(b18[:], smalls["b1"][:], 8192.0))
    p.bar()
    p.op("vector", lambda e: e.tensor_copy(lnsumb[:], lnsum[:]))
    p.bar()
    for ot in range(8):
        for ch in range(8):
            p.op("tensor", lambda e, ot=ot, ch=ch: e.matmul(
                psA[:, 0:1], w1[:, ch * D + ot * 128: ch * D + ot * 128 + 128],
                lnsumb[:, ch:ch + 1], start=(ch == 0), stop=(ch == 7)))
        p.bar()
        p.op("vector", lambda e, ot=ot: e.tensor_add(totcol[:, ot:ot + 1], psA[:, 0:1], b18[:, ot:ot + 1]))
        p.bar()
    for ch in range(8):
        p.op("vector", lambda e, ch=ch: e.tensor_mul(mtcol[:, ch:ch + 1], mwT[:, ch * 41:ch * 41 + 1], totcol[:, ch:ch + 1]))
    p.bar()
    tmpG = c.sb([128, T], F32, "tmpG")
    x3row = c.sb([128, T], F32, "x3row")
    tmpG2 = x3row
    tgbufs = (tmpG, tmpG2)
    for ch in range(8):
        pool_ops = []
        for jj in range(14):
            jg = 26 + jj
            srcG = pT[:, ch * PW + (HL - jg): ch * PW + (HL - jg) + T]
            tapG = tapT[:, ch * 40 + jg: ch * 40 + jg + 1]
            buf = jj % 2
            pool_ops.append(lambda e, srcG=srcG, tapG=tapG, buf=buf: e.tensor_scalar(
                tgbufs[buf][:], srcG, tapG, None, ALU.mult, ALU.bypass))
            if jj == 0:
                pool_ops.append(lambda e, buf=buf: e.tensor_copy(accG[:], tgbufs[buf][:]))
            else:
                pool_ops.append(lambda e, buf=buf: e.tensor_add(accG[:], accG[:], tgbufs[buf][:]))
        pq = list(pool_ops)
        for j in range(26):
            srcV = pT[:, ch * PW + (HL - j): ch * PW + (HL - j) + T]
            tapV = tapT[:, ch * 40 + j: ch * 40 + j + 1]
            if j == 0:
                p.op("vector", lambda e, srcV=srcV, tapV=tapV: e.tensor_scalar(
                    accV[:], srcV, tapV, None, ALU.mult, ALU.bypass))
            else:
                p.op("vector", lambda e, srcV=srcV, tapV=tapV: e.scalar_tensor_tensor(
                    accV[:], srcV, tapV, accV[:], ALU.mult, ALU.add))
            if pq:
                p.op("gpsimd", pq.pop(0))
                if len(pq) > 26 - j:
                    p.op("gpsimd", pq.pop(0))
            p.bar()
        while pq:
            p.op("gpsimd", pq.pop(0))
            p.bar()
        p.op("vector", lambda e, ch=ch: e.scalar_tensor_tensor(
            accV[:], accV[:], mtcol[:, ch:ch + 1], accG[:], ALU.add, ALU.add))
        p.bar()
        p.op("vector", lambda e, ch=ch: e.tensor_copy(pT[:, ch * PW + HL:(ch + 1) * PW], accV[:]))
        p.bar()
    x3b = xb
    xbuf = c.sb([128, 512], F32, "xbuf")
    x3_ap = d["x3_loc"].ap()
    x3b_ap = d["x3b_loc"].ap()
    r3_ap = d["r3_loc"].ap()
    for fo in range(8):
        for sl in range(0, T, 512):
            p.op("sync", lambda e, fo=fo, sl=sl: e.dma_start(
                out=xbuf[:], in_=x_h[fo * 128:(fo + 1) * 128, sl:sl + 512]), dma=True)
            for ch in range(8):
                p.op("tensor", lambda e, fo=fo, sl=sl, ch=ch: e.matmul(
                    psB[:, 0:512], w2[:, ch * D + fo * 128: ch * D + fo * 128 + 128],
                    pT[:, ch * PW + HL + sl: ch * PW + HL + sl + 512], start=(ch == 0), stop=(ch == 7)))
            p.bar()
            p.op("vector", lambda e, fo=fo, sl=sl: e.scalar_tensor_tensor(
                x3row[:, sl:sl + 512], psB[:, 0:512], smalls["b2"][:, fo:fo + 1],
                xbuf[:], ALU.add, ALU.add))
            p.bar()
            p.op("vector", lambda e, fo=fo, sl=sl: e.tensor_copy(
                x3b[:, fo * T + sl: fo * T + sl + 512], x3row[:, sl:sl + 512]))
            p.bar()
        p.op("sync", lambda e, fo=fo: e.dma_start(out=x3_ap[fo * 128:(fo + 1) * 128, :], in_=x3row[:]), dma=True)
        p.op("sync", lambda e, fo=fo: e.dma_start(out=x3b_ap[fo * 128:(fo + 1) * 128, :], in_=x3b[:, fo * T:(fo + 1) * T]), dma=True)
        p.bar()
    st3 = ln_stats(c, x3b, D, T, 1e-5, ps2)
    p.op("sync", lambda e: e.dma_start(out=r3_ap[0:1, :], in_=st3["mu"][:]), dma=True)
    p.op("sync", lambda e: e.dma_start(out=r3_ap[1:2, :], in_=st3["rstd"][:]), dma=True)
    p.bar()
    p.op("gpsimd", lambda e: e.collective_compute(
        "AllGather", ALU.bypass, replica_groups=[list(range(NCORE))],
        ins=[d["x3b_loc"].ap().opt()], outs=[d["G4"].ap().opt()]).then_inc(d["cc4a"], 1), inc=0)
    p.op("gpsimd", lambda e: e.collective_compute(
        "AllGather", ALU.bypass, replica_groups=[list(range(NCORE))],
        ins=[d["r3_loc"].ap().opt()], outs=[d["G4b"].ap().opt()]).then_inc(d["cc4b"], 1), inc=0)
    p.bar()
    c.pop_scope()


def p6_emit(c, d, do_cc5=True, do_attn=True, kt_eng="gpsimd", lvl=9):
    c.prefix = "s6_"
    S = 8192
    p = c.p
    qS_h = d["p6_qS"]
    kS_h = d["p6_kS"]
    vS_h = d["p6_vS"]
    qb_h = d["p6_qb"]
    kb_h = d["p6_kb"]
    vb_h = d["p6_vb"]
    cos_h = d["p6_cosT"]
    sin_h = d["p6_sinsT"]
    gidx = d["gidx"]
    gidx1 = d["gidx1"]
    G4v = d["G4"].ap().rearrange("r (s q) -> (r s) q", s=2)
    G4bv = d["G4b"].ap().rearrange("r (s q) -> (r s) q", s=2)
    c.push_scope()
    QT = c.sb([128, 2 * S], BF16, "QT")
    KT = c.sb([128, 2 * S], BF16, "KT")
    VTO = c.sb([128, 64 * 264], BF16, "VTO")
    qS = load_w_bf16(c, qS_h, D, 256, "qS_sb")
    kS = load_w_bf16(c, kS_h, D, 256, "kS_sb")
    vS = load_w_bf16(c, vS_h, D, 256, "vS_sb")
    qb = c.sb([128, 2], F32, "qb_sb")
    kb = c.sb([128, 2], F32, "kb_sb")
    vb = c.sb([128, 2], F32, "vb_sb")
    identb6 = c.sb([128, 128], BF16, "identb6")
    p.op("gpsimd", lambda e: e.dma_start(out=identb6[:], in_=d["identf"][:]), dma=True)
    p.op("sync", lambda e: e.dma_start(out=qb[:], in_=qb_h[:]), dma=True)
    p.op("sync", lambda e: e.dma_start(out=kb[:], in_=kb_h[:]), dma=True)
    p.op("sync", lambda e: e.dma_start(out=vb[:], in_=vb_h[:]), dma=True)
    p.bar()
    TS = 1024
    c.push_scope()
    QKp = c.ps([128, 1024], name="QKp")
    Vtp = c.ps([128, 512], BF16, name="Vtp")
    ps_stat = c.ps([1, 512], name="ps_stat6")
    bcastp = c.ps([128, 512], name="bcastp")
    negq = colsum_neg_row(c, qS, D, 256, ps_stat)
    negk = colsum_neg_row(c, kS, D, 256, ps_stat)
    negv = colsum_neg_row(c, vS, D, 256, ps_stat)
    for h in range(4):
        p.op("vector", lambda e, h=h: e.memset(
            VTO[:].rearrange("p (t q) -> p t q", t=64)[:, :, h * 66 + 64: h * 66 + 66], 1.0))
    p.bar()
    xsb2 = c.sb([128, 2 * 8 * TS], BF16, "xsb2")
    murow2 = c.sb([2, 2 * TS], F32, "murow2")
    rsrow2 = c.sb([2, 2 * TS], F32, "rsrow2")
    rstd_bc = c.sb([128, TS], F32, "rstd_bc6")
    tmpE = c.sb([128, 1024], F32, "tmpE")
    vT = c.sb([128, 2 * TS], BF16, "vT6")
    rtq = c.sb([128, TS], BF16, "rtq")
    rtk = c.sb([128, TS], BF16, "rtk")
    cosT2 = c.sb([128, 2 * TS], BF16, "cosT2")
    sinsT2 = c.sb([128, 2 * TS], BF16, "sinsT2")

    def gather_ops(tsl, buf):
        ops = []
        for ch in range(8):
            ops.append(lambda e, tsl=tsl, buf=buf, ch=ch: e.indirect_dma_start(
                out=xsb2[:, buf * 8 * TS + ch * TS: buf * 8 * TS + (ch + 1) * TS],
                out_offset=None, in_=G4v,
                in_offset=bass.IndirectOffsetOnAxis(ap=gidx[:, 48 + tsl * 8 + ch: 48 + tsl * 8 + ch + 1], axis=0)))
        ops.append(lambda e, tsl=tsl, buf=buf: e.indirect_dma_start(
            out=murow2[:, buf * TS:(buf + 1) * TS], out_offset=None, in_=G4bv,
            in_offset=bass.IndirectOffsetOnAxis(ap=gidx1[0:2, 2 + tsl: 3 + tsl], axis=0)))
        ops.append(lambda e, tsl=tsl, buf=buf: e.indirect_dma_start(
            out=rsrow2[:, buf * TS:(buf + 1) * TS], out_offset=None, in_=G4bv,
            in_offset=bass.IndirectOffsetOnAxis(ap=gidx1[0:2, 10 + tsl: 11 + tsl], axis=0)))
        ops.append(lambda e, tsl=tsl, buf=buf: e.dma_start(
            out=cosT2[:, buf * TS:(buf + 1) * TS], in_=cos_h[:, tsl * TS:(tsl + 1) * TS]))
        ops.append(lambda e, tsl=tsl, buf=buf: e.dma_start(
            out=sinsT2[:, buf * TS:(buf + 1) * TS], in_=sin_h[:, tsl * TS:(tsl + 1) * TS]))
        return ops

    for op_ in gather_ops(0, 0):
        p.op("gpsimd", op_, dma=True, pre=[(d["cc4a"], 1), (d["cc4b"], 1)])
    p.bar()
    for tsl in range(8):
        t0 = tsl * TS
        buf = tsl % 2
        xsb = xsb2[:, buf * 8 * TS:(buf + 1) * 8 * TS]
        murow = murow2[0:1, buf * TS:(buf + 1) * TS]
        rsrow = rsrow2[0:1, buf * TS:(buf + 1) * TS]
        pend = gather_ops(tsl + 1, 1 - buf) if tsl < 7 else []
        if lvl < 1:
            p.bar()
            continue
        p.op("tensor", lambda e, rsrow=rsrow: e.matmul(bcastp[:, 0:512], c.ones_row[:], rsrow[0:1, 0:512], start=True, stop=True))
        p.bar()
        p.op("vector", lambda e: e.tensor_copy(rstd_bc[:, 0:512], bcastp[:, 0:512]))
        p.op("tensor", lambda e, rsrow=rsrow: e.matmul(QKp[:, 0:512], c.ones_row[:], rsrow[0:1, 512:1024], start=True, stop=True))
        p.bar()
        p.op("vector", lambda e: e.tensor_copy(rstd_bc[:, 512:1024], QKp[:, 0:512]))
        p.bar()
        units = []
        for (dst, wt, ng, bcol) in ((QT, qS, negq, qb), (KT, kS, negk, kb), (vT, vS, negv, vb)):
            for pair in range(2):
                for sl in range(0, TS, 512):
                    units.append((dst, wt, ng, bcol, pair, sl))
        NU = len(units)

        def unit_mm(u):
            dst, wt, ng, bcol, pair, sl = units[u]
            pb = QKp[:, (u % 2) * 512:(u % 2) * 512 + 512]
            for ch in range(8):
                p.op("tensor", lambda e, wt=wt, pair=pair, sl=sl, ch=ch, pb=pb, xsb=xsb: e.matmul(
                    pb, wt[:, ch * 256 + pair * 128: ch * 256 + pair * 128 + 128],
                    xsb[:, ch * TS + sl: ch * TS + sl + 512], start=(ch == 0), stop=False))
            p.op("tensor", lambda e, ng=ng, pair=pair, sl=sl, pb=pb, murow=murow: e.matmul(
                pb, ng[0:1, pair * 128:pair * 128 + 128],
                murow[0:1, sl:sl + 512], start=False, stop=True))

        def unit_mul(u):
            _, _, _, _, pair, sl = units[u]
            pb = QKp[:, (u % 2) * 512:(u % 2) * 512 + 512]
            tb = tmpE[:, (u % 2) * 512:(u % 2) * 512 + 512]
            p.op("vector", lambda e, pb=pb, tb=tb, sl=sl: e.tensor_mul(tb, pb, rstd_bc[:, sl:sl + 512]))

        def unit_bias(u):
            dst, _, _, bcol, pair, sl = units[u]
            tb = tmpE[:, (u % 2) * 512:(u % 2) * 512 + 512]
            if dst is vT:
                seg = vT[:, pair * TS + sl: pair * TS + sl + 512]
            else:
                seg = dst[:, pair * S + t0 + sl: pair * S + t0 + sl + 512]
            p.op("vector", lambda e, seg=seg, tb=tb, bcol=bcol, pair=pair: e.tensor_scalar(
                seg, tb, bcol[:, pair:pair + 1], None, ALU.add, ALU.bypass))

        import os as _os
        _ser = _os.environ.get("P6SER", "0") == "1"
        for st in range(NU + 2):
            if lvl < 2:
                break
            if st < NU:
                unit_mm(st)
                if _ser:
                    p.bar()
            if 1 <= st <= NU:
                unit_mul(st - 1)
                if _ser:
                    p.bar()
            if 2 <= st <= NU + 1:
                unit_bias(st - 2)
            if st < len(pend):
                p.op("gpsimd", pend[st], dma=True)
            p.bar()
        if lvl < 3:
            continue
        rope_q = []
        rope_k = []
        for (dst, rt, ops_l) in ((QT, rtq, rope_q), (KT, rtk, rope_k)):
            for pair in range(2):
                base = pair * S + t0
                seg = dst[:, base: base + TS]
                grp = []
                for r0 in range(0, 128, 32):
                    src_r = (r0 + 32) if (r0 % 64 == 0) else (r0 - 32)
                    grp.append(lambda e, seg=seg, r0=r0, src_r=src_r, rt=rt: e.tensor_copy(
                        rt[r0:r0 + 32, :], seg[src_r:src_r + 32, :]))
                ops_l.append(grp)
                ops_l.append([lambda e, rt=rt, buf=buf: e.tensor_mul(rt[:], rt[:], sinsT2[:, buf * TS:(buf + 1) * TS])])
                ops_l.append([lambda e, seg=seg, buf=buf: e.tensor_mul(seg, seg, cosT2[:, buf * TS:(buf + 1) * TS])])
                ops_l.append([lambda e, seg=seg, rt=rt: e.tensor_add(seg, seg, rt[:])])
        for s in range(9):
            if s < 8 and lvl >= 4:
                tt = s
                for pair in range(2):
                    p.op("tensor", lambda e, tt=tt, pair=pair: e.transpose(
                        Vtp[:, (tt % 2) * 256 + pair * 128: (tt % 2) * 256 + pair * 128 + 128],
                        vT[:, pair * TS + tt * 128: pair * TS + tt * 128 + 128], identb6[:]))
                if s < len(rope_q):
                    for f in rope_q[s]:
                        p.op("vector", f)
                    for f in rope_k[s]:
                        p.op(kt_eng, f)
            if s >= 1 and lvl >= 4:
                tt = s - 1
                gt = tsl * 8 + tt
                for pair in range(2):
                    for hh in range(2):
                        h4 = pair * 2 + hh
                        p.op("vector", lambda e, tt=tt, gt=gt, pair=pair, hh=hh, h4=h4: e.tensor_copy(
                            VTO[:, gt * 264 + h4 * 66: gt * 264 + h4 * 66 + 64],
                            Vtp[:, (tt % 2) * 256 + pair * 128 + hh * 64: (tt % 2) * 256 + pair * 128 + hh * 64 + 64]))
            p.bar()
    if d.get("dbg_xsb2") is not None:
        p.op("sync", lambda e: e.dma_start(out=d["dbg_xsb2"], in_=xsb2[:]), dma=True)
        p.op("sync", lambda e: e.dma_start(out=d["dbg_mu2"], in_=murow2[:]), dma=True)
        p.op("sync", lambda e: e.dma_start(out=d["dbg_rs2"], in_=rsrow2[:]), dma=True)
        p.bar()
    c.pop_scope()
    if d.get("dbg_QT") is not None:
        p.op("sync", lambda e: e.dma_start(out=d["dbg_QT"], in_=QT[:]), dma=True)
        p.op("sync", lambda e: e.dma_start(out=d["dbg_KT"], in_=KT[:]), dma=True)
        p.op("sync", lambda e: e.dma_start(out=d["dbg_VTO"], in_=VTO[:]), dma=True)
        p.bar()
    if not do_attn:
        c.pop_scope()
        return
    c.push_scope()
    SC0 = c.ps([128, 1024], name="SC0")
    SC1 = c.ps([128, 1024], name="SC1")
    ACC = c.ps([65, 1024], name="ACC")
    BCp = c.ps([64, 1024], name="BCp")
    SCs = (SC0, SC1)
    PT2 = c.sb([128, 2048], BF16, "PT2")
    accsb = c.sb([64, 1024], F32, "accsb")
    recr = c.sb([1, 1024], F32, "recr")
    abuf = c.sb([64, 1024], BF16, "abuf")
    ao_ap = d["ao_loc"].ap()
    iters = [(h, qb_) for h in range(4) for qb_ in range(8)]

    def emit_evac(ph, pq0):
        return [
            [("vector", lambda e: e.tensor_copy(accsb[:], ACC[0:64, :])),
             ("vector", lambda e: e.reciprocal(recr[:], ACC[64:65, :]))],
            [("tensor", lambda e: e.matmul(BCp[:, 0:512], c.ones_row[0:1, 0:64], recr[0:1, 0:512], start=True, stop=True)),
             ("tensor", lambda e: e.matmul(BCp[:, 512:1024], c.ones_row[0:1, 0:64], recr[0:1, 512:1024], start=True, stop=True))],
            [("vector", lambda e: e.tensor_mul(abuf[:], accsb[:], BCp[:]))],
            [("sync", lambda e, ph=ph, pq0=pq0: e.dma_start(
                out=ao_ap[ph * 64:(ph + 1) * 64, pq0:pq0 + 1024], in_=abuf[:]))],
        ]

    carried = []
    for (h, qb_) in iters:
        pair = h // 2
        r0 = (h % 2) * 64
        q0 = qb_ * 1024
        for t in range(66):
            if t < 64:
                sc = SCs[t % 2]
                for half in range(2):
                    p.op("tensor", lambda e, t=t, half=half, sc=sc, pair=pair, r0=r0, q0=q0: e.matmul(
                        sc[:, half * 512:(half + 1) * 512],
                        KT[r0:r0 + 64, pair * S + t * 128: pair * S + t * 128 + 128],
                        QT[r0:r0 + 64, pair * S + q0 + half * 512: pair * S + q0 + half * 512 + 512],
                        start=True, stop=True))
            if 1 <= t <= 64:
                sc = SCs[(t - 1) % 2]
                p.op("scalar", lambda e, t=t, sc=sc: e.activation(
                    PT2[:, ((t - 1) % 2) * 1024: ((t - 1) % 2) * 1024 + 1024],
                    sc[:], AF.Exp, bias=-24.0, scale=0.125))
            if 2 <= t <= 65:
                ta = t - 2
                for half in range(2):
                    p.op("tensor", lambda e, ta=ta, half=half, h=h: e.matmul(
                        ACC[:, half * 512:(half + 1) * 512],
                        VTO[:, ta * 264 + h * 66: ta * 264 + h * 66 + 65],
                        PT2[:, (ta % 2) * 1024 + half * 512: (ta % 2) * 1024 + half * 512 + 512],
                        start=(ta == 0), stop=(ta == 63)))
            if t < len(carried):
                for eng, fn in carried[t]:
                    p.op(eng, fn, dma=(eng == "sync"))
            p.bar()
        carried = emit_evac(h, q0)
    for slot in carried:
        for eng, fn in slot:
            p.op(eng, fn, dma=(eng == "sync"))
        p.bar()
    c.pop_scope()
    if do_cc5:
        p.op("gpsimd", lambda e: e.collective_compute(
            "AllGather", ALU.bypass, replica_groups=[list(range(NCORE))],
            ins=[d["ao_loc"].ap().opt()], outs=[d["G5"].ap().opt()]).then_inc(d["cc5"], 1), inc=0)
        p.bar()
    c.pop_scope()


def p7_emit(c, d):
    c.prefix = "s7_"
    T = TPC
    p = c.p
    x_h = d["x3_loc"].ap()
    r3_h = d["r3_loc"].ap()
    gw_h = d["p7_gwT"]
    gb_h = d["p7_gb"]
    ew_h = d["p7_ewT"]
    eb_h = d["p7_ebrow"]
    fw_h = d["p7_ffnT"]
    fb_h = d["p7_fb"]
    id_h = d["identf"]
    out_h = d["outT"]
    gidx = d["gidx"]
    G5v = d["G5"].ap().rearrange("r (s q) -> (r s) q", s=4)
    c.push_scope()
    aoT = c.sb([128, 8 * T], BF16, "aoT_sb")
    for ch in range(8):
        igather(c, aoT[:, ch * T:(ch + 1) * T], G5v, gidx[:, 112 + ch:112 + ch + 1], pre=[(d["cc5"], 1)])
    gwT = load_w_bf16(c, gw_h, D, 8, "gwT_sb")
    identf = c.sb([128, 128], F32, "identf7")
    identb = c.sb([128, 128], BF16, "identb7")
    p.op("sync", lambda e: e.dma_start(out=identf[:], in_=id_h[:]), dma=True)
    p.op("gpsimd", lambda e: e.dma_start(out=identb[:], in_=id_h[:]), dma=True)
    smalls = {}
    for nm, hh, shp in (("gb", gb_h, [8, 1]), ("fb", fb_h, [128, 8])):
        t = c.sb(shp, F32, nm + "_sb7")
        p.op("sync", lambda e, t=t, hh=hh: e.dma_start(out=t[:], in_=hh[:]), dma=True)
        smalls[nm] = t
    for nm, row in (("rmu", 0), ("rrs", 1)):
        t = c.sb([1, T], F32, nm + "_sb7")
        p.op("sync", lambda e, t=t, row=row: e.dma_start(out=t[:], in_=r3_h[row:row + 1, :]), dma=True)
        smalls[nm] = t
    ebrow = c.sb([1, 8 * D], BF16, "ebrow_sb")
    p.op("gpsimd", lambda e: e.dma_start(out=ebrow[:], in_=eb_h[:]), dma=True)
    p.bar()
    P1 = c.ps([128, 512], name="P1_7")
    P2 = c.ps([128, 512], name="P2_7")
    PB = c.ps([128, 256], BF16, name="PB_7")
    glog = c.sb([8, T], F32, "glog")
    for sl in range(0, T, 512):
        for ch in range(8):
            p.op("tensor", lambda e, sl=sl, ch=ch: e.matmul(
                P1[0:8, 0:512], gwT[:, ch * 8:(ch + 1) * 8],
                aoT[:, ch * T + sl: ch * T + sl + 512], start=(ch == 0), stop=(ch == 7)))
        p.bar()
        p.op("vector", lambda e, sl=sl: e.tensor_scalar(
            glog[0:8, sl:sl + 512], P1[0:8, 0:512], smalls["gb"][:, 0:1], None, ALU.add, ALU.bypass))
        p.bar()
    wcols = c.sb([128, 16 * 8], F32, "wcols")
    gt = c.sb([128, 8], F32, "gt7")
    eg = c.sb([128, 8], F32, "eg7")
    r1 = c.sb([128, 1], F32, "r1_7")
    r2 = c.sb([128, 1], F32, "r2_7")
    for tt in range(16):
        p.op("tensor", lambda e, tt=tt: e.transpose(P2[:, 0:8], glog[0:8, tt * 128:(tt + 1) * 128], identf[0:8, 0:8]))
        p.bar()
        p.op("vector", lambda e: e.tensor_copy(gt[:], P2[:, 0:8]))
        p.bar()
        p.op("vector", lambda e: e.reduce_max(r1[:], gt[:], axis=AX.X))
        p.bar()
        p.op("vector", lambda e: e.tensor_scalar(r2[:], r1[:], -1.0, None, ALU.mult, ALU.bypass))
        p.bar()
        p.op("scalar", lambda e: e.activation(eg[:], gt[:], AF.Exp, bias=r2[:, 0:1], scale=1.0))
        p.bar()
        p.op("vector", lambda e: e.reduce_sum(r1[:], eg[:], axis=AX.X))
        p.bar()
        p.op("vector", lambda e: e.reciprocal(r2[:], r1[:]))
        p.bar()
        p.op("vector", lambda e: e.tensor_scalar(gt[:], eg[:], r2[:, 0:1], None, ALU.mult, ALU.bypass))
        p.bar()
        p.op("vector", lambda e: e.reduce_max(r1[:], gt[:], axis=AX.X))
        p.bar()
        p.op("vector", lambda e: e.tensor_single_scalar(eg[:], gt[:], r1[:, 0:1], ALU.is_ge))
        p.bar()
        p.op("vector", lambda e: e.scalar_tensor_tensor(eg[:], eg[:], -1e9, gt[:], ALU.mult, ALU.add))
        p.bar()
        p.op("vector", lambda e: e.reduce_max(r2[:], eg[:], axis=AX.X))
        p.bar()
        p.op("vector", lambda e: e.tensor_single_scalar(eg[:], gt[:], r2[:, 0:1], ALU.is_ge))
        p.bar()
        p.op("vector", lambda e, tt=tt: e.tensor_mul(wcols[:, tt * 8:(tt + 1) * 8], gt[:], eg[:]))
        p.bar()
    ebuf = c.sb([128, 8 * D], BF16, "ebuf")
    macc = c.sb([128, 16 * 1024], F32, "macc")
    for e_ in range(8):
        for ch in range(8):
            p.op("gpsimd", lambda e, e_=e_, ch=ch: e.dma_start(
                out=ebuf[:, ch * D:(ch + 1) * D],
                in_=ew_h[e_ * D + ch * 128: e_ * D + (ch + 1) * 128, :]), dma=True)
        p.bar()
        for tt in range(16):
            for half, ph in ((0, P1), (1, P2)):
                for ch in range(8):
                    p.op("tensor", lambda e, tt=tt, ch=ch, half=half, ph=ph: e.matmul(
                        ph[:, 0:512], aoT[:, ch * T + tt * 128: ch * T + tt * 128 + 128],
                        ebuf[:, ch * D + half * 512: ch * D + half * 512 + 512],
                        start=(ch == 0), stop=False))
                p.op("tensor", lambda e, e_=e_, half=half, ph=ph: e.matmul(
                    ph[:, 0:512], c.ones_row_b[0:1, 0:128],
                    ebrow[0:1, e_ * D + half * 512: e_ * D + (half + 1) * 512], start=False, stop=True))
            p.bar()
            for half, ph in ((0, P1), (1, P2)):
                col = wcols[:, tt * 8 + e_: tt * 8 + e_ + 1]
                dst = macc[:, tt * 1024 + half * 512: tt * 1024 + half * 512 + 512]
                if e_ == 0:
                    p.op("vector", lambda e, ph=ph, col=col, dst=dst: e.tensor_scalar(
                        dst, ph[:, 0:512], col, None, ALU.mult, ALU.bypass))
                else:
                    p.op("vector", lambda e, ph=ph, col=col, dst=dst: e.scalar_tensor_tensor(
                        dst, ph[:, 0:512], col, dst, ALU.mult, ALU.add))
            p.bar()
    moeT = aoT
    stg = ebuf
    for tt in range(16):
        p.op("vector", lambda e, tt=tt: e.tensor_copy(stg[:, 0:1024], macc[:, tt * 1024:(tt + 1) * 1024]))
        p.bar()
        for ch in range(8):
            p.op("tensor", lambda e, ch=ch: e.transpose(PB[:, 0:128], stg[:, ch * 128:(ch + 1) * 128], identb[:]))
            p.bar()
            p.op("vector", lambda e, tt=tt, ch=ch: e.tensor_copy(
                moeT[:, ch * T + tt * 128:(ch * T + tt * 128) + 128], PB[:, 0:128]))
            p.bar()
    ffnw = ebuf
    for ch in range(8):
        p.op("gpsimd", lambda e, ch=ch: e.dma_start(
            out=ffnw[:, ch * D:(ch + 1) * D], in_=fw_h[ch * 128:(ch + 1) * 128, :]), dma=True)
    p.bar()
    mubc = c.sb([128, T], F32, "mubc7")
    rsbc = c.sb([128, T], F32, "rsbc7")
    for sl in range(0, T, 512):
        p.op("tensor", lambda e, sl=sl: e.matmul(P1[:, 0:512], c.ones_row[:], smalls["rmu"][0:1, sl:sl + 512], start=True, stop=True))
        p.bar()
        p.op("vector", lambda e, sl=sl: e.tensor_copy(mubc[:, sl:sl + 512], P1[:, 0:512]))
        p.bar()
        p.op("tensor", lambda e, sl=sl: e.matmul(P1[:, 0:512], c.ones_row[:], smalls["rrs"][0:1, sl:sl + 512], start=True, stop=True))
        p.bar()
        p.op("vector", lambda e, sl=sl: e.tensor_copy(rsbc[:, sl:sl + 512], P1[:, 0:512]))
        p.bar()
    tmp = c.sb([128, 512], F32, "tmp7")
    obuf = c.sb([128, 512], F32, "obuf7")
    xbuf = c.sb([128, 512], F32, "xbuf7")
    for fo in range(8):
        for sl in range(0, T, 512):
            p.op("sync", lambda e, fo=fo, sl=sl: e.dma_start(
                out=xbuf[:], in_=x_h[fo * 128:(fo + 1) * 128, sl:sl + 512]), dma=True)
            for ch in range(8):
                p.op("tensor", lambda e, fo=fo, sl=sl, ch=ch: e.matmul(
                    P2[:, 0:512], ffnw[:, ch * D + fo * 128: ch * D + fo * 128 + 128],
                    moeT[:, ch * T + sl: ch * T + sl + 512], start=(ch == 0), stop=(ch == 7)))
            p.bar()
            p.op("vector", lambda e, sl=sl: e.tensor_sub(tmp[:], xbuf[:], mubc[:, sl:sl + 512]))
            p.bar()
            p.op("vector", lambda e, sl=sl: e.tensor_mul(tmp[:], tmp[:], rsbc[:, sl:sl + 512]))
            p.bar()
            p.op("vector", lambda e, fo=fo: e.scalar_tensor_tensor(
                tmp[:], P2[:, 0:512], smalls["fb"][:, fo:fo + 1], tmp[:], ALU.add, ALU.add))
            p.bar()
            p.op("vector", lambda e: e.tensor_add(obuf[:], tmp[:], xbuf[:]))
            p.bar()
            p.op("sync", lambda e, fo=fo, sl=sl: e.dma_start(
                out=out_h[fo * 128:(fo + 1) * 128, sl:sl + 512], in_=obuf[:]), dma=True)
            p.bar()
    c.pop_scope()


def build_fused():
    c = Ctx()
    nc = c.nc
    d = {}
    d["xT"] = c.inp("xT", [D, TPC], F32)
    d["p1_down_w"] = c.inp("p1_down_w", [D, 256], F32)
    d["p1_down_b"] = c.inp("p1_down_b", [128, 2], F32)
    d["p2_wq"] = c.inp("p2_wq", [256, 128], F32)
    d["p2_wk"] = c.inp("p2_wk", [256, 128], F32)
    d["p2_wv"] = c.inp("p2_wv", [256, 128], F32)
    d["p2_lrw"] = c.inp("p2_lrw", [128, 2], F32)
    d["p2_lrb"] = c.inp("p2_lrb", [1, 1], F32)
    d["p2_tokc"] = c.inp("p2_tokc", [16, 1], F32)
    d["p2_invc"] = c.inp("p2_invc", [16, 1], F32)
    d["p2_maskT"] = c.inp("p2_maskT", [16, 16], F32)
    d["identf"] = c.inp("identf", [128, 128], F32)
    d["p2_W1"] = c.inp("p2_W1", [128, 128], F32)
    d["p2_b1"] = c.inp("p2_b1", [1, 128], F32)
    d["p2_gw"] = c.inp("p2_gw", [1, 128], F32)
    d["p2_gb"] = c.inp("p2_gb", [1, 128], F32)
    d["p2_gwc"] = c.inp("p2_gwc", [128, 1], F32)
    d["p3_wo"] = c.inp("p3_wo", [256, 256], F32)
    d["p3_up"] = c.inp("p3_up", [256, D], F32)
    d["p3_pw"] = c.inp("p3_pw", [128, 2], F32)
    d["p3_pb"] = c.inp("p3_pb", [128, 2], F32)
    d["p3_lw"] = c.inp("p3_lw", [128, 2], F32)
    d["p3_lb"] = c.inp("p3_lb", [128, 2], F32)
    d["p3_ub"] = c.inp("p3_ub", [128, 8], F32)
    d["p4_WT"] = c.inp("p4_WT", [D, D], F32)
    d["p4_ldT"] = c.inp("p4_ldT", [100, 4], F32)
    d["p4_da"] = c.inp("p4_da", [100, D], F32)
    d["p4_luT"] = c.inp("p4_luT", [4, 50], F32)
    d["p4_uaT"] = c.inp("p4_uaT", [50, D], F32)
    d["p4_b"] = c.inp("p4_b", [128, 8], F32)
    d["p5_W1"] = c.inp("p5_W1", [D, D], F32)
    d["p5_W2"] = c.inp("p5_W2", [D, D], F32)
    d["p5_mwT"] = c.inp("p5_mwT", [D, 41], F32)
    d["p5_b1c"] = c.inp("p5_b1c", [128, 8], F32)
    d["p5_b2c"] = c.inp("p5_b2c", [128, 8], F32)
    d["p5_hv"] = c.inp("p5_hv", [128, 1], F32)
    d["p6_qS"] = c.inp("p6_qS", [D, 256], F32)
    d["p6_kS"] = c.inp("p6_kS", [D, 256], F32)
    d["p6_vS"] = c.inp("p6_vS", [D, 256], F32)
    d["p6_qb"] = c.inp("p6_qb", [128, 2], F32)
    d["p6_kb"] = c.inp("p6_kb", [128, 2], F32)
    d["p6_vb"] = c.inp("p6_vb", [128, 2], F32)
    d["p6_cosT"] = c.inp("p6_cosT", [128, 8192], BF16)
    d["p6_sinsT"] = c.inp("p6_sinsT", [128, 8192], BF16)
    d["p7_gwT"] = c.inp("p7_gwT", [D, 8], F32)
    d["p7_gb"] = c.inp("p7_gb", [8, 1], F32)
    d["p7_ewT"] = c.inp("p7_ewT", [8 * D, D], F32)
    d["p7_ebrow"] = c.inp("p7_ebrow", [1, 8 * D], F32)
    d["p7_ffnT"] = c.inp("p7_ffnT", [D, D], F32)
    d["p7_fb"] = c.inp("p7_fb", [128, 8], F32)
    d["gidx_h"] = c.inp("gidx", [128, 120], I32)
    d["gidx1_h"] = c.inp("gidx1", [2, 18], I32)
    d["outT"] = c.out("outT", [D, TPC], F32)
    d["h_loc"] = c.dram("h_loc", [256, TPC], BF16)
    d["G1"] = c.dram("G1", [8 * 256, TPC], BF16, shared=True)
    d["outs_loc"] = c.dram("outs_loc", [8192, 128], BF16)
    d["G2"] = c.dram("G2", [4 * 8192, 128], BF16, shared=False)
    d["x1_loc"] = c.dram("x1_loc", [D, TPC], F32)
    d["x2_loc"] = c.dram("x2_loc", [D, TPC], F32)
    d["r2_loc"] = c.dram("r2_loc", [2, TPC], F32)
    d["part_loc"] = c.dram("part_loc", [128, 8], F32)
    d["partsum_loc"] = c.dram("partsum_loc", [128, 8], F32)
    d["B3"] = c.dram("B3", [1026, 40], F32)
    d["G3"] = c.dram("G3", [8 * 1026, 40], F32, shared=True)
    d["x3_loc"] = c.dram("x3_loc", [D, TPC], F32)
    d["x3b_loc"] = c.dram("x3b_loc", [D, TPC], BF16)
    d["r3_loc"] = c.dram("r3_loc", [2, TPC], F32)
    d["G4"] = c.dram("G4", [8 * D, TPC], BF16, shared=True)
    d["G4b"] = c.dram("G4b", [16, TPC], F32, shared=True)
    d["ao_loc"] = c.dram("ao_loc", [256, 8192], BF16)
    d["G5"] = c.dram("G5", [8 * 256, 8192], BF16, shared=True)
    for nm in ("cc1", "cc2", "cc3a", "cc3b", "cc4a", "cc4b", "cc5"):
        d[nm] = c.st.enter_context(nc.semaphore(nm))
    c.setup_common(consts=[0.0, 1.0, 1e-5, 1e-6, 1.0 / 128, -1.0 / 128, 1.0 / 16,
                           8192.0, -24.0, 0.125])
    gidx = c.sb([128, 120], I32, "gidx_sb")
    gidx1 = c.sb([2, 18], I32, "gidx1_sb")
    c.p.op("sync", lambda e: e.dma_start(out=gidx[:], in_=d["gidx_h"]), dma=True)
    c.p.op("sync", lambda e: e.dma_start(out=gidx1[:], in_=d["gidx1_h"]), dma=True)
    c.p.bar()
    d["gidx"] = gidx
    d["gidx1"] = gidx1
    p1_emit(c, d)
    p2_emit(c, d)
    p3_emit(c, d)
    p4_emit(c, d)
    p5_emit(c, d)
    p6_emit(c, d)
    p7_emit(c, d)
    return c


def host_fused(inputs):
    f32 = np.float32
    x = np.asarray(inputs["L0_attn_x"], f32).reshape(TOK, D)
    xTs = [np.ascontiguousarray(x[i * TPC:(i + 1) * TPC].T) for i in range(NCORE)]
    identf = np.eye(128, dtype=f32)
    maskT = np.tril(np.ones((16, 16), f32)).T.copy()
    invc = (1.0 / np.arange(1, 17, dtype=f32)).reshape(16, 1)
    S = 8192
    inv_freq = 1.0 / 10000.0 ** (np.arange(0, 64, 2, dtype=np.float64) / 64)
    fr = np.outer(np.arange(S, dtype=np.float64), inv_freq)
    emb = np.concatenate([fr, fr], axis=-1)
    cos = np.cos(emb).T
    sin = np.sin(emb).T
    sins = np.concatenate([-sin[:32], sin[32:]], 0)
    cosT = np.ascontiguousarray(np.tile(cos, (2, 1)).astype(f32)).astype(BF)
    sinsT = np.ascontiguousarray(np.tile(sins, (2, 1)).astype(f32)).astype(BF)
    com = {
        "p1_down_w": np.ascontiguousarray(np.asarray(inputs["L0_attn_down_w"], f32)),
        "p1_down_b": np.ascontiguousarray(np.asarray(inputs["L0_attn_down_b"], f32).reshape(2, 128).T),
        "p2_tokc": np.asarray(inputs["L0_attn_tok_idx"], f32).reshape(16, 1).copy(),
        "p2_invc": invc, "p2_maskT": maskT, "identf": identf,
        "p3_wo": np.ascontiguousarray(np.asarray(inputs["L0_attn_wo"], f32)),
        "p3_up": np.ascontiguousarray(np.asarray(inputs["L0_attn_up_w"], f32)),
        "p3_pw": np.ascontiguousarray(np.asarray(inputs["L0_attn_post_w"], f32).reshape(2, 128).T),
        "p3_pb": np.ascontiguousarray(np.asarray(inputs["L0_attn_post_b"], f32).reshape(2, 128).T),
        "p3_lw": np.ascontiguousarray(np.asarray(inputs["L0_attn_ln_w"], f32).reshape(2, 128).T),
        "p3_lb": np.ascontiguousarray(np.asarray(inputs["L0_attn_ln_b"], f32).reshape(2, 128).T),
        "p3_ub": np.ascontiguousarray(np.asarray(inputs["L0_attn_up_b"], f32).reshape(8, 128).T),
        "p4_WT": np.ascontiguousarray(np.asarray(inputs["L0_ffn_W"], f32).T),
        "p4_ldT": np.ascontiguousarray(np.asarray(inputs["L0_ffn_lora_down"], f32).T),
        "p4_da": np.ascontiguousarray(np.asarray(inputs["L0_ffn_down_aux"], f32)),
        "p4_luT": np.ascontiguousarray(np.asarray(inputs["L0_ffn_lora_up"], f32).T),
        "p4_uaT": np.ascontiguousarray(np.asarray(inputs["L0_ffn_up_aux"], f32).T),
        "p4_b": np.ascontiguousarray(np.asarray(inputs["L0_ffn_b"], f32).reshape(8, 128).T),
        "p5_W1": np.ascontiguousarray(np.asarray(inputs["L1_attn_W1"], f32)),
        "p5_W2": np.ascontiguousarray(np.asarray(inputs["L1_attn_W2"], f32)),
        "p5_mwT": np.ascontiguousarray(np.asarray(inputs["L1_attn_memory_weights"], f32).T),
        "p5_b1c": np.ascontiguousarray(np.asarray(inputs["L1_attn_bias1"], f32).reshape(8, 128).T),
        "p5_b2c": np.ascontiguousarray(np.asarray(inputs["L1_attn_bias2"], f32).reshape(8, 128).T),
        "p6_cosT": cosT, "p6_sinsT": sinsT,
        "p7_gwT": np.ascontiguousarray(np.asarray(inputs["L1_ffn_gate_w"], f32).T),
        "p7_gb": np.ascontiguousarray(np.asarray(inputs["L1_ffn_gate_b"], f32).reshape(8, 1)),
        "p7_ewT": np.ascontiguousarray(np.asarray(inputs["L1_ffn_expert_w"], f32).transpose(0, 2, 1).reshape(8 * D, D)),
        "p7_ebrow": np.ascontiguousarray(np.asarray(inputs["L1_ffn_expert_b"], f32).reshape(1, 8 * D)),
        "p7_ffnT": np.ascontiguousarray(np.asarray(inputs["L1_ffn_ffn_w"], f32).T),
        "p7_fb": np.ascontiguousarray(np.asarray(inputs["L1_ffn_ffn_b"], f32).reshape(8, 128).T),
    }
    qwT = np.asarray(inputs["L1_ffn_q_w"], f32).T
    kwT = np.asarray(inputs["L1_ffn_k_w"], f32).T
    vwT = np.asarray(inputs["L1_ffn_v_w"], f32).T
    qb = np.asarray(inputs["L1_ffn_q_b"], f32)
    kb = np.asarray(inputs["L1_ffn_k_b"], f32)
    vb = np.asarray(inputs["L1_ffn_v_b"], f32)
    maps = []
    pvec = np.arange(128, dtype=np.int64)
    for i in range(NCORE):
        b_tok = i // 4
        r_tok = i % 4
        s = i % 4
        bs, hh = s // 2, s % 2
        hg = i % 4
        prev = (i - 1) % 8
        gidx = np.zeros((128, 120), dtype=np.int64)
        for ch in range(2):
            for cc in range(4):
                gidx[:, ch * 4 + cc] = (4 * bs + cc) * 256 + ch * 128 + pvec
        for hhp in range(2):
            for tt in range(16):
                gidx[:, 8 + hhp * 16 + tt] = (2 * b_tok + hhp) * 8192 + r_tok * 2048 + tt * 128 + pvec
        for ch in range(8):
            gidx[:, 40 + ch] = prev * 1026 + ch * 128 + pvec
        for tsl in range(8):
            for ch in range(8):
                gidx[:, 48 + tsl * 8 + ch] = ((4 * b_tok + tsl // 2) * 1024 + ch * 128 + pvec) * 2 + (tsl % 2)
        for hgp in range(4):
            for fo in range(2):
                gidx[:, 112 + hgp * 2 + fo] = ((4 * b_tok + hgp) * 256 + fo * 128 + pvec) * 4 + r_tok
        gidx1 = np.zeros((2, 18), dtype=np.int64)
        gidx1[:, 0] = prev * 1026 + 1024
        gidx1[:, 1] = prev * 1026 + 1025
        for tsl in range(8):
            cctr = 4 * b_tok + tsl // 2
            gidx1[:, 2 + tsl] = (2 * cctr + 0) * 2 + (tsl % 2)
            gidx1[:, 10 + tsl] = (2 * cctr + 1) * 2 + (tsl % 2)
        cols6 = slice(hg * 256, (hg + 1) * 256)
        m = dict(com)
        m.update({
            "xT": xTs[i],
            "p2_wq": np.ascontiguousarray(np.asarray(inputs["L0_attn_wq"], f32)[:, hh * 128:(hh + 1) * 128]),
            "p2_wk": np.ascontiguousarray(np.asarray(inputs["L0_attn_wk"], f32)[:, hh * 128:(hh + 1) * 128]),
            "p2_wv": np.ascontiguousarray(np.asarray(inputs["L0_attn_wv"], f32)[:, hh * 128:(hh + 1) * 128]),
            "p2_lrw": np.ascontiguousarray(np.asarray(inputs["L0_attn_lr_w"], f32)[hh, 0, :].reshape(2, 128).T),
            "p2_lrb": np.asarray(inputs["L0_attn_lr_b"], f32)[hh].reshape(1, 1).copy(),
            "p2_W1": np.ascontiguousarray(np.asarray(inputs["L0_attn_W1"], f32)[hh]),
            "p2_b1": np.asarray(inputs["L0_attn_b1"], f32)[hh].reshape(1, 128).copy(),
            "p2_gw": np.asarray(inputs["L0_attn_ttt_ln_w"], f32)[hh].reshape(1, 128).copy(),
            "p2_gb": np.asarray(inputs["L0_attn_ttt_ln_b"], f32)[hh].reshape(1, 128).copy(),
            "p2_gwc": np.asarray(inputs["L0_attn_ttt_ln_w"], f32)[hh].reshape(128, 1).copy(),
            "p5_hv": (np.zeros((128, 1), f32) if i % 4 == 0 else np.ones((128, 1), f32)),
            "p6_qS": np.ascontiguousarray(qwT[:, cols6]),
            "p6_kS": np.ascontiguousarray(kwT[:, cols6]),
            "p6_vS": np.ascontiguousarray(vwT[:, cols6]),
            "p6_qb": np.ascontiguousarray(qb[cols6].reshape(2, 128).T),
            "p6_kb": np.ascontiguousarray(kb[cols6].reshape(2, 128).T),
            "p6_vb": np.ascontiguousarray(vb[cols6].reshape(2, 128).T),
            "gidx": gidx.astype(np.int32),
            "gidx1": gidx1.astype(np.int32),
        })
        maps.append(m)
    return maps


def kernel(**inputs):
    c = build_fused()
    res = c.run(host_fused(inputs), core_ids=list(range(NCORE)))
    out = np.concatenate([np.asarray(r["outT"], np.float32).T for r in res.results], 0)
    return np.ascontiguousarray(out.reshape(2, 8192, D))


# version 12
# speedup vs baseline: 1.0841x; 1.0841x over previous
import numpy as np
import ml_dtypes
import contextlib
import concourse.bass as bass
import concourse.mybir as mybir
from concourse.bass_utils import run_bass_kernel_spmd

F32 = mybir.dt.float32
BF16 = mybir.dt.bfloat16
I32 = mybir.dt.int32
AF = mybir.ActivationFunctionType
ALU = mybir.AluOpType
AX = mybir.AxisListType

NCORE = 8
TOK = 16384
TPC = TOK // NCORE
D = 1024
BF = ml_dtypes.bfloat16


class Prog:
    ENGINES = ("sync", "tensor", "vector", "scalar", "gpsimd")

    def __init__(self):
        self.stages = []
        self.cur = []

    def op(self, eng, fn, dma=False, inc=None, pre=None):
        assert eng in self.ENGINES
        if inc is None:
            inc = 16 if dma else 1
        self.cur.append((eng, fn, inc, pre))

    def bar(self):
        if self.cur:
            self.stages.append(self.cur)
            self.cur = []

    def emit(self, block, sem):
        self.bar()
        per = {e: [] for e in self.ENGINES}
        cum = 0
        for stage in self.stages:
            start = cum
            for eng, fn, inc, pre in stage:
                per[eng].append((fn, start, inc, pre))
                cum += inc
        total = cum

        def mk(items, is_sync):
            def body(eng):
                last_wait = 0
                for fn, wait, inc, pre in items:
                    if wait > last_wait:
                        eng.wait_ge(sem, wait)
                        last_wait = wait
                    if pre:
                        for (s, v) in pre:
                            eng.wait_ge(s, v)
                    ins = fn(eng)
                    if inc:
                        ins.then_inc(sem, inc)
                if is_sync:
                    eng.wait_ge(sem, total)
            return body

        for e in self.ENGINES:
            if per[e] or e == "sync":
                getattr(block, e)(mk(per[e], e == "sync"))


class Ctx:
    def __init__(self):
        self.nc = bass.Bass("TRN2", target_bir_lowering=False)
        self.st = contextlib.ExitStack()
        self.scopes = [self.st]
        self.p = Prog()
        self._n = 0
        self.prefix = ""

    def push_scope(self):
        self.scopes.append(contextlib.ExitStack())

    def pop_scope(self):
        self.scopes.pop().close()

    def sb(self, shape, dt, name=None):
        self._n += 1
        nm = (self.prefix + name) if name else f"sb{self._n}"
        return self.scopes[-1].enter_context(
            self.nc.sbuf_tensor(nm, shape, dt))

    def ps(self, shape, dt=F32, name=None):
        self._n += 1
        nm = (self.prefix + name) if name else f"ps{self._n}"
        return self.scopes[-1].enter_context(
            self.nc.psum_tensor(nm, shape, dt))

    def inp(self, name, shape, dt=F32):
        return self.nc.dram_tensor(name, shape, dt, kind="ExternalInput").ap()

    def out(self, name, shape, dt=F32):
        return self.nc.dram_tensor(name, shape, dt, kind="ExternalOutput").ap()

    def dram(self, name, shape, dt, shared=False):
        return self.nc.dram_tensor(name, shape, dt,
                                   addr_space="Shared" if shared else "Local")

    def setup_common(self, consts=()):
        self.ones_col = self.sb([128, 1], BF16, "ones_col")
        self.ones_row = self.sb([1, 128], F32, "ones_row")
        self.ones_row_b = self.sb([1, 128], BF16, "ones_row_b")
        p = self.p
        p.op("vector", lambda e: e.memset(self.ones_col[:], 1.0))
        p.op("vector", lambda e: e.memset(self.ones_row[:], 1.0))
        p.op("vector", lambda e: e.memset(self.ones_row_b[:], 1.0))
        cl = sorted(set(float(x) for x in consts))
        if cl:
            self.cpool = self.sb([128, len(cl)], F32, "cpool")
            for i, cv in enumerate(cl):
                p.op("vector", lambda e, i=i, cv=cv: e.memset(self.cpool[:, i:i + 1], cv))
                self.nc.const_aps.register(F32, cv, self.cpool[:, i:i + 1])
        p.bar()

    def run(self, in_maps, core_ids, **kw):
        nc = self.finalize()
        return run_bass_kernel_spmd(nc, in_maps, core_ids=core_ids, **kw)

    def finalize(self):
        block = self.st.enter_context(self.nc.Block())
        sem = self.st.enter_context(self.nc.semaphore("SEQ"))
        with self.st:
            self.p.emit(block, sem)
        return self.nc


def ln_stats(c, xb, C, T, eps, ps2, rows=None):
    p = c.p
    nch = C // 128
    if rows is None:
        sqb = c.sb([128, 512], BF16)
        mu = c.sb([1, T], F32)
        m2 = c.sb([1, T], F32)
        rstd = c.sb([1, T], F32)
        mur = c.sb([1, T], F32)
    else:
        sqb, mu, m2, rstd, mur = rows["sqb"], rows["mu"], rows["m2"], rows["rstd"], rows["mur"]
    for sl in range(0, T, 512):
        for ch in range(nch):
            xap = xb[:, ch * T + sl: ch * T + sl + 512]
            p.op("vector", lambda e, xap=xap: e.tensor_mul(sqb[:], xap, xap))
            p.bar()
            p.op("tensor", lambda e, xap=xap, ch=ch: e.matmul(
                ps2[0:1, 0:512], c.ones_col[:], xap, start=(ch == 0), stop=(ch == nch - 1)))
            p.op("tensor", lambda e, ch=ch: e.matmul(
                ps2[0:1, 512:1024], c.ones_col[:], sqb[:], start=(ch == 0), stop=(ch == nch - 1)))
            p.bar()
        p.op("vector", lambda e, sl=sl: e.tensor_scalar(
            mu[0:1, sl:sl + 512], ps2[0:1, 0:512], 1.0 / C, None, ALU.mult, ALU.bypass))
        p.op("scalar", lambda e, sl=sl: e.mul(m2[0:1, sl:sl + 512], ps2[0:1, 512:1024], 1.0 / C))
        p.bar()
    p.op("vector", lambda e: e.tensor_mul(rstd[:], mu[:], mu[:]))
    p.bar()
    p.op("vector", lambda e: e.tensor_sub(rstd[:], m2[:], rstd[:]))
    p.bar()
    p.op("scalar", lambda e: e.activation(m2[:], rstd[:], AF.Sqrt, bias=float(eps), scale=1.0))
    p.bar()
    p.op("vector", lambda e: e.reciprocal(rstd[:], m2[:]))
    p.bar()
    p.op("vector", lambda e: e.tensor_mul(mur[:], mu[:], rstd[:]))
    p.bar()
    return {"mu": mu, "rstd": rstd, "mur": mur, "m2": m2, "sqb": sqb}


def bcast_row(c, row, T, ps1, out=None, dt=F32):
    p = c.p
    out = out if out is not None else c.sb([128, T], dt)
    for sl in range(0, T, 512):
        p.op("tensor", lambda e, sl=sl: e.matmul(
            ps1[:, 0:512], c.ones_row[:], row[0:1, sl:sl + 512], start=True, stop=True))
        p.bar()
        p.op("vector", lambda e, sl=sl: e.tensor_copy(out[:, sl:sl + 512], ps1[:, 0:512]))
        p.bar()
    return out


def load_w_bf16(c, hbm, C, F, name=None, pre=None):
    p = c.p
    t = c.sb([128, (C // 128) * F], BF16, name)
    for ch in range(C // 128):
        p.op("gpsimd", lambda e, ch=ch: e.dma_start(
            out=t[:, ch * F:(ch + 1) * F], in_=hbm[ch * 128:(ch + 1) * 128, :]), dma=True, pre=pre)
    return t


def colsum_neg_row(c, wt, C, F, ps2):
    p = c.p
    nch = C // 128
    out = c.sb([1, F], F32)
    for sl in range(0, F, 512):
        w = min(512, F - sl)
        for ch in range(nch):
            p.op("tensor", lambda e, ch=ch, sl=sl, w=w: e.matmul(
                ps2[0:1, 0:w], c.ones_col[:], wt[:, ch * F + sl:ch * F + sl + w],
                start=(ch == 0), stop=(ch == nch - 1)))
        p.bar()
        p.op("vector", lambda e, sl=sl, w=w: e.tensor_scalar(
            out[0:1, sl:sl + w], ps2[0:1, 0:w], -1.0, None, ALU.mult, ALU.bypass))
        p.bar()
    return out


def ln_proj(c, xb, C, T, wt, Fdim, mu_row, rstd_bc, psA, out_cb, bias_col=None,
            act=None, out_dt=BF16, negcs_f32=None, extra_rank1=None, wt_stride=None):
    p = c.p
    nch = C // 128
    nfo = Fdim // 128
    WS = wt_stride if wt_stride is not None else Fdim
    for fo in range(nfo):
        for sl in range(0, T, 512):
            for ch in range(nch):
                p.op("tensor", lambda e, fo=fo, sl=sl, ch=ch: e.matmul(
                    psA[:, 0:512], wt[:, ch * WS + fo * 128: ch * WS + fo * 128 + 128],
                    xb[:, ch * T + sl: ch * T + sl + 512],
                    start=(ch == 0), stop=False))
            p.op("tensor", lambda e, fo=fo, sl=sl: e.matmul(
                psA[:, 0:512], negcs_f32[0:1, fo * 128:fo * 128 + 128],
                mu_row[0:1, sl:sl + 512], start=False, stop=True))
            p.bar()
            tgt = out_cb(fo, sl)
            if act is None and bias_col is None:
                p.op("vector", lambda e, fo=fo, sl=sl, tgt=tgt: e.tensor_mul(
                    tgt, psA[:, 0:512], rstd_bc[:, sl:sl + 512]))
                p.bar()
            else:
                tmp = c.sb([128, 512], F32, None) if not hasattr(c, "_lnp_tmp") else c._lnp_tmp
                c._lnp_tmp = tmp
                p.op("vector", lambda e, sl=sl, tmp=tmp: e.tensor_mul(
                    tmp[:], psA[:, 0:512], rstd_bc[:, sl:sl + 512]))
                p.bar()
                bias = bias_col[:, fo:fo + 1] if bias_col is not None else 0.0
                if act is None or act == AF.Copy:
                    p.op("vector", lambda e, tmp=tmp, bias=bias, tgt=tgt: e.tensor_scalar(
                        tgt, tmp[:], bias, None, ALU.add, ALU.bypass))
                else:
                    p.op("scalar", lambda e, tmp=tmp, bias=bias, fn=act, tgt=tgt: e.activation(
                        tgt, tmp[:], fn, bias=bias, scale=1.0))
                p.bar()


def igather(c, out_ap, src_ap, idx_col, pre=None):
    c.p.op("gpsimd", lambda e: e.indirect_dma_start(
        out=out_ap, out_offset=None, in_=src_ap,
        in_offset=bass.IndirectOffsetOnAxis(ap=idx_col, axis=0)), dma=True, pre=pre)


def p1_emit(c, d):
    c.prefix = "s1_"
    T = TPC
    x_h = d["xT"]
    dw_h = d["p1_down_w"]
    db_h = d["p1_down_b"]
    p = c.p
    c.push_scope()
    xb = c.sb([128, 8 * T], BF16, "xb")
    for ch in range(8):
        p.op("gpsimd", lambda e, ch=ch: e.dma_start(
            out=xb[:, ch * T:(ch + 1) * T], in_=x_h[ch * 128:(ch + 1) * 128, :]), dma=True)
    wt = load_w_bf16(c, dw_h, D, 256, "dw")
    dbc = c.sb([128, 2], F32, "dbc")
    p.op("sync", lambda e: e.dma_start(out=dbc[:], in_=db_h[:]), dma=True)
    p.bar()
    ps2 = c.ps([1, 1024], name="ps_stat")
    psA = c.ps([128, 512], name="psA")
    stats = ln_stats(c, xb, D, T, 1e-5, ps2)
    negcs = colsum_neg_row(c, wt, D, 256, ps2)
    rstd_bc = bcast_row(c, stats["rstd"], T, psA)
    hT = c.sb([128, 2 * T], BF16, "hT_sb")

    def out_cb(fo, sl):
        return hT[:, fo * T + sl: fo * T + sl + 512]

    ln_proj(c, xb, D, T, wt, 256, stats["mu"], rstd_bc, psA, out_cb,
            bias_col=dbc, act=AF.Silu, negcs_f32=negcs)
    for fo in range(2):
        p.op("sync", lambda e, fo=fo: e.dma_start(
            out=d["h_loc"].ap()[fo * 128:(fo + 1) * 128, :], in_=hT[:, fo * T:(fo + 1) * T]), dma=True)
    p.bar()
    p.op("gpsimd", lambda e: e.collective_compute(
        "AllGather", ALU.bypass, replica_groups=[list(range(NCORE))],
        ins=[d["h_loc"].ap().opt()], outs=[d["G1"].ap().opt()]).then_inc(d["cc1"], 1), inc=0)
    p.bar()
    c.pop_scope()
    if hasattr(c, "_lnp_tmp"):
        del c._lnp_tmp


def p2_emit(c, d, NM=512):
    c.prefix = "s2_"
    K = 16
    L = NM * K
    p = c.p
    wq_h = d["p2_wq"]
    wk_h = d["p2_wk"]
    wv_h = d["p2_wv"]
    lrw_h = d["p2_lrw"]
    lrb_h = d["p2_lrb"]
    tokc_h = d["p2_tokc"]
    invc_h = d["p2_invc"]
    maskT_h = d["p2_maskT"]
    identf_h = d["identf"]
    W1_h = d["p2_W1"]
    b1_h = d["p2_b1"]
    gw_h = d["p2_gw"]
    gb_h = d["p2_gb"]
    gwc_h = d["p2_gwc"]
    gidx = d["gidx"]
    c.push_scope()
    hT = c.sb([128, 2 * L], BF16, "hT_sb")
    xqT = c.sb([128, L], BF16, "xqT")
    xkT = c.sb([128, L], BF16, "xkT")
    tgtT = c.sb([128, L], BF16, "tgtT")
    wqs = c.sb([128, 2 * 128], BF16, "wqs")
    wks = c.sb([128, 2 * 128], BF16, "wks")
    wvs = c.sb([128, 2 * 128], BF16, "wvs")
    lrww = c.sb([128, 2], BF16, "lrww")
    lrow = c.sb([1, L], F32, "lrow")
    identf = c.sb([128, 128], F32, "identf_sb")
    identb = c.sb([128, 128], BF16, "identb_sb")
    maskT = c.sb([16, 16], F32, "maskT_sb")
    masktokT = c.sb([16, 16], F32, "masktokT")
    tokc = c.sb([16, 1], F32, "tokc_sb")
    invc = c.sb([16, 1], F32, "invc_sb")
    t15c = c.sb([16, 1], F32, "t15c")
    gwro = c.sb([1, 128], F32, "gwro")
    gbro = c.sb([1, 128], F32, "gbro")
    gw2r = c.sb([1, 128], F32, "gw2r")
    gbgw = c.sb([1, 128], F32, "gbgw")
    gbgwb = c.sb([16, 128], F32, "gbgwb")
    sgbc = c.sb([16, 1], F32, "sgbc")
    sgb1 = c.sb([1, 1], F32, "sgb1")
    gw2b = c.sb([16, 128], F32, "gw2b")
    gwb16 = c.sb([16, 1024], F32, "gwb16")
    gbb16 = c.sb([16, 1024], F32, "gbb16")
    W1sb = c.sb([128, 2 * 128], F32, "W1sb")
    W1bf = c.sb([128, 2 * 128], BF16, "W1bf")
    b1r = c.sb([1, 2 * 128], F32, "b1r")
    b1bf = c.sb([1, 2 * 128], BF16, "b1bf")
    o1 = c.sb([16, 16], F32, "o1")
    nct = c.sb([16, 16], BF16, "nct")
    s_ = c.sb([16, 1], F32, "s_")
    ssq = c.sb([16, 1], F32, "ssq")
    zsq = c.sb([16, 128], F32, "zsq")
    mu = c.sb([16, 1], F32, "mu")
    m2 = c.sb([16, 1], F32, "m2")
    u_ = c.sb([16, 1], F32, "u_")
    stdv = c.sb([16, 1], F32, "stdv")
    rstd = c.sb([16, 1], F32, "rstd")
    xh = c.sb([16, 128], F32, "xh")
    t1 = c.sb([16, 128], F32, "t1")
    st1 = c.sb([16, 1], F32, "st1")
    sgg = c.sb([16, 1], F32, "sgg")
    gy = c.sb([16, 128], F32, "gy")
    u1 = c.sb([16, 128], F32, "u1")
    mgy = c.sb([16, 1], F32, "mgy")
    q5 = c.sb([16, 128], F32, "q5")
    sgx = c.sb([16, 1], F32, "sgx")
    nmgyx = c.sb([16, 1], F32, "nmgyx")
    aa = c.sb([16, 128], F32, "aa")
    grad = c.sb([16, 128], BF16, "grad")
    sct = c.sb([16, 1], F32, "sct")
    leta = c.sb([16, 1], F32, "leta")
    letab = c.sb([16, 1], BF16, "letab")
    lk = c.sb([16, 128], BF16, "lk")
    xqtok = c.sb([16, 1024], F32, "xqtok")
    zbb = c.sb([16, 1024], F32, "zbb")
    srow = c.sb([16, 8], F32, "srow")
    sqrow = c.sb([16, 8], F32, "sqrow")
    murow = c.sb([16, 8], F32, "murow")
    m2row = c.sb([16, 8], F32, "m2row")
    urow = c.sb([16, 8], F32, "urow")
    stdrow = c.sb([16, 8], F32, "stdrow")
    rstdrow = c.sb([16, 8], F32, "rstdrow")
    zb2 = c.sb([16, 1024], F32, "zb2")
    outc = c.sb([16, 1024], BF16, "outc")
    pbig = c.ps([128, 512], name="pbig")
    pw1 = c.ps([128, 128], name="pw1")
    pzA = c.ps([16, 128], name="pzA")
    pD = c.ps([16, 512], name="pD")
    pgg = c.ps([16, 128], BF16, name="pgg")
    pxq = c.ps([16, 128], BF16, name="pxq")
    pxk = c.ps([16, 128], BF16, name="pxk")
    praw = pD[0:16, 0:16]
    plr = pD[0:16, 16:17]
    pb1 = pD[0:1, 32:160]
    pzbar = pbig[0:16, 0:128]

    G1ap = d["G1"].ap()
    for ch in range(2):
        for cc in range(4):
            igather(c, hT[:, ch * L + cc * 2048: ch * L + (cc + 1) * 2048],
                    G1ap, gidx[:, ch * 4 + cc: ch * 4 + cc + 1], pre=[(d["cc1"], 1)])
        for t, hh in ((wqs, wq_h), (wks, wk_h), (wvs, wv_h)):
            p.op("gpsimd", lambda e, t=t, ch=ch, hh=hh: e.dma_start(
                out=t[:, ch * 128:(ch + 1) * 128], in_=hh[ch * 128:(ch + 1) * 128, :]), dma=True)
    p.op("gpsimd", lambda e: e.dma_start(out=lrww[:], in_=lrw_h[:]), dma=True)
    p.op("sync", lambda e: e.dma_start(out=identf[:], in_=identf_h[:]), dma=True)
    p.op("gpsimd", lambda e: e.dma_start(out=identb[:], in_=identf_h[:]), dma=True)
    p.op("sync", lambda e: e.dma_start(out=maskT[:], in_=maskT_h[:]), dma=True)
    p.op("sync", lambda e: e.dma_start(out=tokc[:], in_=tokc_h[:]), dma=True)
    p.op("sync", lambda e: e.dma_start(out=invc[:], in_=invc_h[:]), dma=True)
    p.op("sync", lambda e: e.dma_start(out=W1sb[:, 0:128], in_=W1_h[:]), dma=True)
    p.op("sync", lambda e: e.dma_start(out=b1r[0:1, 0:128], in_=b1_h[:]), dma=True)
    p.op("sync", lambda e: e.dma_start(out=gwro[:], in_=gw_h[:]), dma=True)
    p.op("sync", lambda e: e.dma_start(out=gbro[:], in_=gb_h[:]), dma=True)
    lrbt = c.sb([1, 1], F32, "lrbt")
    p.op("sync", lambda e: e.dma_start(out=lrbt[:], in_=lrb_h[:]), dma=True)
    gwcol = c.sb([128, 1], F32, "gwcol")
    p.op("sync", lambda e: e.dma_start(out=gwcol[:], in_=gwc_h[:]), dma=True)
    p.bar()
    p.op("vector", lambda e: e.tensor_add(tokc[:], tokc[:], invc[:]))
    p.bar()
    p.op("vector", lambda e: e.tensor_scalar_max(tokc[:], tokc[:], 0.0))
    p.bar()
    tokrow = c.sb([1, 16], F32, "tokrow")
    p.op("tensor", lambda e: e.matmul(pD[0:1, 256:272], tokc[:], identf[0:16, 0:16], start=True, stop=True))
    p.bar()
    p.op("vector", lambda e: e.tensor_copy(tokrow[:], pD[0:1, 256:272]))
    p.bar()
    p.op("tensor", lambda e: e.matmul(praw, c.ones_row[0:1, 0:16], tokrow[:], start=True, stop=True))
    p.bar()
    p.op("vector", lambda e: e.tensor_mul(masktokT[:], maskT[:], praw))
    p.bar()
    p.op("tensor", lambda e: e.matmul(pD[0:1, 272:273], tokc[:], identf[0:16, 15:16], start=True, stop=True))
    p.bar()
    tok15s = c.sb([1, 1], F32, "tok15s")
    p.op("vector", lambda e: e.tensor_copy(tok15s[:], pD[0:1, 272:273]))
    p.bar()
    p.op("tensor", lambda e: e.matmul(pD[0:16, 288:289], c.ones_row[0:1, 0:16], tok15s[:], start=True, stop=True))
    p.bar()
    p.op("vector", lambda e: e.tensor_scalar(t15c[:], pD[0:16, 288:289], -1.0, None, ALU.mult, ALU.bypass))
    p.bar()
    p.op("vector", lambda e: e.tensor_mul(gw2r[:], gwro[:], gwro[:]))
    p.op("gpsimd", lambda e: e.tensor_mul(gbgw[:], gbro[:], gwro[:]))
    p.bar()
    p.op("tensor", lambda e: e.matmul(pbig[0:16, 0:128], c.ones_row[0:1, 0:16], gw2r[:], start=True, stop=True))
    p.bar()
    p.op("vector", lambda e: e.tensor_copy(gw2b[:], pbig[0:16, 0:128]))
    p.bar()
    p.op("tensor", lambda e: e.matmul(pbig[0:16, 0:128], c.ones_row[0:1, 0:16], gbgw[:], start=True, stop=True))
    p.bar()
    p.op("vector", lambda e: e.tensor_copy(gbgwb[:], pbig[0:16, 0:128]))
    p.bar()
    p.op("vector", lambda e: e.reduce_sum(sgb1[:], gbgw[:], axis=AX.X))
    p.bar()
    p.op("tensor", lambda e: e.matmul(pbig[0:16, 128:129], c.ones_row[0:1, 0:16], sgb1[:], start=True, stop=True))
    p.bar()
    p.op("vector", lambda e: e.tensor_copy(sgbc[:], pbig[0:16, 128:129]))
    p.bar()
    for r in range(8):
        p.op("tensor", lambda e: e.matmul(pbig[0:16, 0:128], c.ones_row[0:1, 0:16], gwro[:], start=True, stop=True))
        p.bar()
        p.op("vector", lambda e, r=r: e.tensor_copy(gwb16[:, r * 128:(r + 1) * 128], pbig[0:16, 0:128]))
        p.bar()
        p.op("tensor", lambda e: e.matmul(pbig[0:16, 0:128], c.ones_row[0:1, 0:16], gbro[:], start=True, stop=True))
        p.bar()
        p.op("vector", lambda e, r=r: e.tensor_copy(gbb16[:, r * 128:(r + 1) * 128], pbig[0:16, 0:128]))
        p.bar()
    p.op("vector", lambda e: e.tensor_copy(W1bf[:, 0:128], W1sb[:, 0:128]))
    p.op("scalar", lambda e: e.activation(b1bf[0:1, 0:128], b1r[0:1, 0:128], AF.Copy, bias=0.0, scale=1.0))
    p.bar()
    for (dst, wt) in ((xqT, wqs), (xkT, wks), (tgtT, wvs)):
        for sl in range(0, L, 512):
            for ch in range(2):
                p.op("tensor", lambda e, dst=dst, wt=wt, sl=sl, ch=ch: e.matmul(
                    pbig[:, 0:512], wt[:, ch * 128:(ch + 1) * 128],
                    hT[:, ch * L + sl: ch * L + sl + 512], start=(ch == 0), stop=(ch == 1)))
            p.bar()
            p.op("vector", lambda e, dst=dst, sl=sl: e.tensor_copy(dst[:, sl:sl + 512], pbig[:, 0:512]))
            p.bar()
    _w = min(2048, L)
    for sl in range(0, L, _w):
        p.op("vector", lambda e, sl=sl, _w=_w: e.tensor_sub(tgtT[:, sl:sl + _w], tgtT[:, sl:sl + _w], xkT[:, sl:sl + _w]))
        p.bar()
    for sl in range(0, L, 512):
        for ch in range(2):
            p.op("tensor", lambda e, sl=sl, ch=ch: e.matmul(
                pD[0:1, 0:512], lrww[:, ch:ch + 1], hT[:, ch * L + sl:ch * L + sl + 512],
                start=(ch == 0), stop=(ch == 1)))
        p.bar()
        p.op("scalar", lambda e, sl=sl: e.activation(lrow[0:1, sl:sl + 512], pD[0:1, 0:512], AF.Sigmoid, bias=lrbt[0:1, 0:1], scale=1.0))
        p.bar()
    outs_ap = d["outs_loc"].ap()
    for n in range(NM):
        par = n % 2
        q = n % 8
        t0 = n * K
        W1b_cur = W1bf[:, par * 128:(par + 1) * 128]
        W1s_cur = W1sb[:, par * 128:(par + 1) * 128]
        W1b_nxt = W1bf[:, (1 - par) * 128:(2 - par) * 128]
        W1s_nxt = W1sb[:, (1 - par) * 128:(2 - par) * 128]
        b1_cur = b1r[0:1, par * 128:(par + 1) * 128]
        b1b_cur = b1bf[0:1, par * 128:(par + 1) * 128]
        b1_nxt = b1r[0:1, (1 - par) * 128:(2 - par) * 128]
        b1b_nxt = b1bf[0:1, (1 - par) * 128:(2 - par) * 128]
        xk_sl = xkT[:, t0:t0 + K]
        xq_sl = xqT[:, t0:t0 + K]
        tg_sl = tgtT[:, t0:t0 + K]
        p.op("tensor", lambda e, xk_sl=xk_sl, xq_sl=xq_sl: e.matmul(praw, xk_sl, xq_sl, start=True, stop=True))
        p.op("tensor", lambda e, n=n: e.transpose(plr, lrow[0:1, n * K:(n + 1) * K], identf[0:1, 0:1]))
        p.op("tensor", lambda e, tg_sl=tg_sl: e.transpose(pgg[:], tg_sl, identb[:]))
        p.op("tensor", lambda e, xq_sl=xq_sl: e.transpose(pxq[:], xq_sl, identb[:]))
        p.op("tensor", lambda e, xk_sl=xk_sl: e.transpose(pxk[:], xk_sl, identb[:]))
        p.op("tensor", lambda e, xk_sl=xk_sl, W1b_cur=W1b_cur: e.matmul(pzA[:], xk_sl, W1b_cur, start=True, stop=False))
        p.op("tensor", lambda e, b1_cur=b1_cur: e.matmul(pzA[:], c.ones_row[0:1, 0:16], b1_cur, start=False, stop=True))
        p.op("tensor", lambda e, xq_sl=xq_sl, W1b_cur=W1b_cur: e.matmul(pzbar, xq_sl, W1b_cur, start=True, stop=False))
        p.op("tensor", lambda e, b1_cur=b1_cur: e.matmul(pzbar, c.ones_row[0:1, 0:16], b1_cur, start=False, stop=False, skip_group_check=True))
        p.bar()
        p.op("vector", lambda e: e.reduce_sum(s_[:], pzA[:], axis=AX.X))
        p.op("scalar", lambda e, q=q: e.activation(xqtok[:, q * 128:(q + 1) * 128], pxq[:], AF.Copy, bias=0.0, scale=1.0))
        p.op("vector", lambda e: e.tensor_mul(u1[:], pgg[:], gwb16[:, 0:128]))
        p.bar()
        p.op("vector", lambda e: e.scalar_tensor_tensor(o1[:], praw, 1.0, masktokT[:], ALU.add, ALU.mult))
        p.op("scalar", lambda e: e.activation(zsq[:], pzA[:], AF.Square, bias=0.0, scale=1.0, accum_out=ssq[:]))
        p.op("vector", lambda e: e.tensor_sub(gy[:], gbgwb[:], u1[:]))
        p.op("vector", lambda e: e.reduce_sum(sgg[:], u1[:], axis=AX.X))
        p.bar()
        p.op("vector", lambda e: e.tensor_scalar(mu[:], s_[:], 1.0 / 128, None, ALU.mult, ALU.bypass))
        p.op("gpsimd", lambda e: e.tensor_scalar(m2[:], ssq[:], 1.0 / 128, None, ALU.mult, ALU.bypass))
        p.op("vector", lambda e: e.tensor_scalar(sct[:], plr, -1.0 / 128, None, ALU.mult, ALU.bypass))
        p.bar()
        p.op("vector", lambda e: e.scalar_tensor_tensor(u_[:], mu[:], mu[:, 0:1], m2[:], ALU.mult, ALU.subtract))
        p.op("gpsimd", lambda e: e.tensor_scalar(leta[:], sct[:], t15c[:, 0:1], None, ALU.mult, ALU.bypass))
        p.op("vector", lambda e: e.tensor_scalar(nct[:], o1[:], sct[:, 0:1], None, ALU.mult, ALU.bypass))
        p.bar()
        p.op("scalar", lambda e: e.activation(stdv[:], u_[:], AF.Sqrt, bias=1e-6, scale=-1.0))
        p.op("vector", lambda e: e.tensor_copy(letab[:], leta[:]))
        p.op("vector", lambda e: e.tensor_scalar(lk[:], pxk[:], leta[:, 0:1], None, ALU.mult, ALU.bypass))
        p.bar()
        p.op("vector", lambda e: e.reciprocal(rstd[:], stdv[:]))
        p.bar()
        p.op("vector", lambda e: e.tensor_scalar(xh[:], pzA[:], mu[:, 0:1], rstd[:, 0:1], ALU.subtract, ALU.mult))
        p.bar()
        p.op("vector", lambda e: e.scalar_tensor_tensor(t1[:], xh[:], 1.0, gw2b[:], ALU.mult, ALU.mult, accum_out=st1[:]))
        p.bar()
        p.op("vector", lambda e: e.tensor_add(gy[:], gy[:], t1[:]))
        p.op("gpsimd", lambda e: e.tensor_sub(mgy[:], st1[:], sgg[:]))
        p.bar()
        p.op("vector", lambda e: e.scalar_tensor_tensor(q5[:], gy[:], -1.0 / 128, xh[:], ALU.mult, ALU.mult, accum_out=sgx[:]))
        p.op("gpsimd", lambda e: e.tensor_scalar(mgy[:], mgy[:], sgbc[:, 0:1], 1.0 / 128, ALU.add, ALU.mult))
        p.bar()
        p.op("vector", lambda e: e.scalar_tensor_tensor(aa[:], xh[:], sgx[:, 0:1], gy[:], ALU.mult, ALU.add))
        p.bar()
        p.op("vector", lambda e: e.tensor_scalar(grad[:], aa[:], mgy[:, 0:1], rstd[:, 0:1], ALU.subtract, ALU.mult))
        p.bar()
        p.op("tensor", lambda e: e.matmul(pzbar, nct[:], grad[:], start=False, stop=True, skip_group_check=True))
        p.op("tensor", lambda e: e.matmul(pw1[:], lk[:], grad[:], start=True, stop=True))
        p.op("tensor", lambda e: e.matmul(pb1, letab[:], grad[:], start=True, stop=True))
        p.bar()
        p.op("vector", lambda e, W1s_nxt=W1s_nxt, W1s_cur=W1s_cur: e.tensor_sub(W1s_nxt, W1s_cur, pw1[:]))
        p.op("vector", lambda e, W1b_nxt=W1b_nxt, W1s_cur=W1s_cur: e.tensor_sub(W1b_nxt, W1s_cur, pw1[:]))
        p.op("vector", lambda e, b1_nxt=b1_nxt, b1_cur=b1_cur: e.tensor_sub(b1_nxt, b1_cur, pb1))
        p.op("scalar", lambda e, q=q: e.activation(zbb[:, q * 128:(q + 1) * 128], pzbar, AF.Copy, bias=0.0, scale=1.0))
        p.bar()
        if q == 7:
            g0 = n - 7
            sview = zbb[:].rearrange("p (s n2) -> p s n2", s=8)
            p.op("vector", lambda e: e.reduce_sum(srow[:], sview, axis=AX.X))
            p.op("gpsimd", lambda e: e.tensor_mul(zb2[:], zbb[:], zbb[:]))
            p.bar()
            p.op("vector", lambda e: e.reduce_sum(sqrow[:], zb2[:].rearrange("p (s n2) -> p s n2", s=8), axis=AX.X))
            p.op("gpsimd", lambda e: e.tensor_scalar(murow[:], srow[:], 1.0 / 128, None, ALU.mult, ALU.bypass))
            p.bar()
            p.op("vector", lambda e: e.tensor_scalar(m2row[:], sqrow[:], 1.0 / 128, None, ALU.mult, ALU.bypass))
            p.bar()
            p.op("vector", lambda e: e.tensor_mul(urow[:], murow[:], murow[:]))
            p.bar()
            p.op("vector", lambda e: e.tensor_sub(urow[:], urow[:], m2row[:]))
            p.bar()
            p.op("scalar", lambda e: e.activation(stdrow[:], urow[:], AF.Sqrt, bias=1e-6, scale=-1.0))
            p.bar()
            p.op("vector", lambda e: e.reciprocal(rstdrow[:], stdrow[:]))
            p.bar()
            p.op("vector", lambda e: e.tensor_sub(
                zb2[:].rearrange("p (s n2) -> p s n2", s=8), sview,
                murow[:].rearrange("p s -> p s ()").broadcast_to((16, 8, 128))))
            p.bar()
            p.op("vector", lambda e: e.tensor_mul(
                zb2[:].rearrange("p (s n2) -> p s n2", s=8),
                zb2[:].rearrange("p (s n2) -> p s n2", s=8),
                rstdrow[:].rearrange("p s -> p s ()").broadcast_to((16, 8, 128))))
            p.bar()
            p.op("gpsimd", lambda e: e.tensor_add(xqtok[:], xqtok[:], gbb16[:]))
            p.bar()
            p.op("vector", lambda e: e.tensor_mul(zb2[:], zb2[:], gwb16[:]))
            p.bar()
            p.op("vector", lambda e: e.tensor_add(outc[:], zb2[:], xqtok[:]))
            p.bar()
            p.op("sync", lambda e, g0=g0: e.dma_start(
                out=outs_ap[g0 * K:g0 * K + 128, :].rearrange("(j k) f -> k j f", k=16),
                in_=outc[:].rearrange("k (j f) -> k j f", j=8)), dma=True)
            p.bar()
    p.op("gpsimd", lambda e: e.collective_compute(
        "AllGather", ALU.bypass, replica_groups=[[0, 1, 2, 3], [4, 5, 6, 7]],
        ins=[d["outs_loc"].ap().opt()], outs=[d["G2"].ap().opt()]).then_inc(d["cc2"], 1), inc=0)
    p.bar()
    c.pop_scope()


def p3_emit(c, d):
    c.prefix = "s3_"
    T = TPC
    p = c.p
    x_h = d["xT"]
    wo_h = d["p3_wo"]
    up_h = d["p3_up"]
    pw_h = d["p3_pw"]
    pb_h = d["p3_pb"]
    lw_h = d["p3_lw"]
    lb_h = d["p3_lb"]
    ub_h = d["p3_ub"]
    gidx = d["gidx"]
    c.push_scope()
    ob = c.sb([128, 2 * T], BF16, "ob")
    identb = c.sb([128, 128], BF16, "identb3")
    p.op("gpsimd", lambda e: e.dma_start(out=identb[:], in_=d["identf"][:]), dma=True)
    ttA = c.sb([128, 128], BF16, "ttA")
    ttB = c.sb([128, 128], BF16, "ttB")
    psT = c.ps([128, 128], BF16, name="psT")
    psU = c.ps([128, 128], BF16, name="psU")
    G2v = d["G2"].ap()
    p.bar()
    for hh in range(2):
        for tt in range(16):
            col = 8 + hh * 16 + tt
            tile = ttA if tt % 2 == 0 else ttB
            pst = psT if tt % 2 == 0 else psU
            igather(c, tile[:], G2v, gidx[:, col:col + 1], pre=[(d["cc2"], 1)])
            p.bar()
            p.op("tensor", lambda e, tile=tile, pst=pst: e.transpose(pst[:], tile[:], identb[:]))
            p.bar()
            p.op("vector", lambda e, hh=hh, tt=tt, pst=pst: e.tensor_copy(
                ob[:, hh * T + tt * 128: hh * T + tt * 128 + 128], pst[:]))
            p.bar()
    xT = c.sb([128, 8 * T], F32, "xT_sb")
    for ch in range(8):
        p.op("sync", lambda e, ch=ch: e.dma_start(out=xT[:, ch * T:(ch + 1) * T], in_=x_h[ch * 128:(ch + 1) * 128, :]), dma=True)
    wo = load_w_bf16(c, wo_h, 256, 256, "wo_sb")
    up = load_w_bf16(c, up_h, 256, D, "up_sb")
    small = {}
    for nm, hh, ncol in (("pw", pw_h, 2), ("pb", pb_h, 2), ("lw", lw_h, 2), ("lb", lb_h, 2), ("ub", ub_h, 8)):
        t = c.sb([128, ncol], F32, nm)
        p.op("sync", lambda e, t=t, hh=hh: e.dma_start(out=t[:], in_=hh[:]), dma=True)
        small[nm] = t
    p.bar()
    ps2 = c.ps([1, 1024], name="ps_stat")
    psA = c.ps([128, 512], name="psA")
    psB = c.ps([128, 512], name="psB")
    pbv = c.sb([128, 2], F32, "pbv")
    pb_bf = c.sb([128, 2], BF16, "pb_bf")
    p.op("vector", lambda e: e.tensor_copy(pb_bf[:], small["pb"][:]))
    p.bar()
    for fo in range(2):
        for ch in range(2):
            p.op("tensor", lambda e, fo=fo, ch=ch: e.matmul(
                psA[:, 0:1], wo[:, ch * 256 + fo * 128: ch * 256 + fo * 128 + 128],
                pb_bf[:, ch:ch + 1], start=(ch == 0), stop=(ch == 1)))
        p.bar()
        p.op("vector", lambda e, fo=fo: e.tensor_copy(pbv[:, fo:fo + 1], psA[:, 0:1]))
        p.bar()
    for ch in range(2):
        p.op("vector", lambda e, ch=ch: e.tensor_scalar(
            wo[:, ch * 256:(ch + 1) * 256], wo[:, ch * 256:(ch + 1) * 256],
            small["pw"][:, ch:ch + 1], None, ALU.mult, ALU.bypass))
    p.bar()
    st1 = ln_stats(c, ob, 256, T, 1e-5, ps2)
    negcs1 = colsum_neg_row(c, wo, 256, 256, ps2)
    rstd1bc = bcast_row(c, st1["rstd"], T, psA)
    o3 = c.sb([128, 2 * T], BF16, "o3")

    def cb1(fo, sl):
        return o3[:, fo * T + sl: fo * T + sl + 512]

    ln_proj(c, ob, 256, T, wo, 256, st1["mu"], rstd1bc, psA, cb1,
            bias_col=pbv, act=AF.Copy, negcs_f32=negcs1, wt_stride=256)
    biasv = c.sb([128, 8], F32, "biasv")
    lb_bf = c.sb([128, 2], BF16, "lb_bf")
    p.op("vector", lambda e: e.tensor_copy(lb_bf[:], small["lb"][:]))
    p.bar()
    for fo in range(8):
        for ch in range(2):
            p.op("tensor", lambda e, fo=fo, ch=ch: e.matmul(
                psA[:, 0:1], up[:, ch * D + fo * 128: ch * D + fo * 128 + 128],
                lb_bf[:, ch:ch + 1], start=(ch == 0), stop=(ch == 1)))
        p.bar()
        p.op("vector", lambda e, fo=fo: e.tensor_add(biasv[:, fo:fo + 1], psA[:, 0:1], small["ub"][:, fo:fo + 1]))
        p.bar()
    for ch in range(2):
        p.op("vector", lambda e, ch=ch: e.tensor_scalar(
            up[:, ch * D:(ch + 1) * D], up[:, ch * D:(ch + 1) * D],
            small["lw"][:, ch:ch + 1], None, ALU.mult, ALU.bypass))
    p.bar()
    st2 = ln_stats(c, o3, 256, T, 1e-5, ps2)
    rstd2bc = bcast_row(c, st2["rstd"], T, psA)
    negcs2 = colsum_neg_row(c, up, 256, D, ps2)
    tmp = c.sb([128, 512], F32, "l3tmp")
    x1_ap = d["x1_loc"].ap()
    for fo in range(8):
        for sl in range(0, T, 512):
            for ch in range(2):
                p.op("tensor", lambda e, fo=fo, sl=sl, ch=ch: e.matmul(
                    psB[:, 0:512], up[:, ch * D + fo * 128: ch * D + fo * 128 + 128],
                    o3[:, ch * T + sl: ch * T + sl + 512], start=(ch == 0), stop=False))
            p.op("tensor", lambda e, fo=fo, sl=sl: e.matmul(
                psB[:, 0:512], negcs2[0:1, fo * 128:fo * 128 + 128],
                st2["mu"][0:1, sl:sl + 512], start=False, stop=True))
            p.bar()
            p.op("vector", lambda e, sl=sl: e.tensor_mul(tmp[:], psB[:, 0:512], rstd2bc[:, sl:sl + 512]))
            p.bar()
            p.op("vector", lambda e, fo=fo, sl=sl: e.scalar_tensor_tensor(
                xT[:, fo * T + sl: fo * T + sl + 512], tmp[:], biasv[:, fo:fo + 1],
                xT[:, fo * T + sl: fo * T + sl + 512], ALU.add, ALU.add))
            p.bar()
    for ch in range(8):
        p.op("sync", lambda e, ch=ch: e.dma_start(out=x1_ap[ch * 128:(ch + 1) * 128, :], in_=xT[:, ch * T:(ch + 1) * T]), dma=True)
    p.bar()
    c.pop_scope()
    if hasattr(c, "_lnp_tmp"):
        del c._lnp_tmp


def p4_emit(c, d):
    c.prefix = "s4_"
    T = TPC
    p = c.p
    x_h = d["x1_loc"].ap()
    wT_h = d["p4_WT"]
    ldT_h = d["p4_ldT"]
    da_h = d["p4_da"]
    luT_h = d["p4_luT"]
    uaT_h = d["p4_uaT"]
    b_h = d["p4_b"]
    c.push_scope()
    xT = c.sb([128, 8 * T], F32, "xT_sb")
    xb = c.sb([128, 8 * T], BF16, "xb")
    for ch in range(8):
        p.op("sync", lambda e, ch=ch: e.dma_start(out=xT[:, ch * T:(ch + 1) * T], in_=x_h[ch * 128:(ch + 1) * 128, :]), dma=True)
        p.op("gpsimd", lambda e, ch=ch: e.dma_start(out=xb[:, ch * T:(ch + 1) * T], in_=x_h[ch * 128:(ch + 1) * 128, :]), dma=True)
    weff = c.sb([128, 8 * D], BF16, "weff")
    ldT = c.sb([100, 4], BF16, "ldT_sb")
    da = c.sb([100, D], BF16, "da_sb")
    luT = c.sb([4, 50], BF16, "luT_sb")
    uaT = c.sb([50, D], BF16, "uaT_sb")
    bcol = c.sb([128, 8], F32, "bcol_sb")
    for t, hh in ((ldT, ldT_h), (da, da_h), (luT, luT_h), (uaT, uaT_h)):
        p.op("gpsimd", lambda e, t=t, hh=hh: e.dma_start(out=t[:], in_=hh[:]), dma=True)
    p.op("sync", lambda e: e.dma_start(out=bcol[:], in_=b_h[:]), dma=True)
    p.bar()
    ps2 = c.ps([1, 1024], name="ps_stat")
    psA = c.ps([128, 512], name="psA")
    psB = c.ps([128, 512], name="psB")
    psC = c.ps([128, 1024], name="psC")
    t1 = c.sb([4, D], BF16, "t1_li")
    for sl in range(0, D, 512):
        p.op("tensor", lambda e, sl=sl: e.matmul(psA[0:4, 0:512], ldT[:], da[:, sl:sl + 512], start=True, stop=True))
        p.bar()
        p.op("vector", lambda e, sl=sl: e.tensor_copy(t1[:, sl:sl + 512], psA[0:4, 0:512]))
        p.bar()
    t2 = c.sb([50, D], BF16, "t2_li")
    for sl in range(0, D, 512):
        p.op("tensor", lambda e, sl=sl: e.matmul(psA[0:50, 0:512], luT[:], t1[:, sl:sl + 512], start=True, stop=True))
        p.bar()
        p.op("vector", lambda e, sl=sl: e.tensor_copy(t2[:, sl:sl + 512], psA[0:50, 0:512]))
        p.bar()
    for ch in range(8):
        p.op("gpsimd", lambda e, ch=ch: e.dma_start(out=weff[:, ch * D:(ch + 1) * D], in_=wT_h[ch * 128:(ch + 1) * 128, :]), dma=True)
    p.bar()
    for ch in range(8):
        for sl in range(0, D, 512):
            p.op("tensor", lambda e, ch=ch, sl=sl: e.matmul(
                psC[:, 0:512], t2[:, ch * 128:(ch + 1) * 128], uaT[:, sl:sl + 512], start=True, stop=True))
            p.bar()
            p.op("vector", lambda e, ch=ch, sl=sl: e.tensor_add(
                weff[:, ch * D + sl: ch * D + sl + 512], weff[:, ch * D + sl: ch * D + sl + 512], psC[:, 0:512]))
            p.bar()
    st = ln_stats(c, xb, D, T, 1e-5, ps2)
    negcs = colsum_neg_row(c, weff, D, D, ps2)
    rstd_bc = bcast_row(c, st["rstd"], T, psA)
    tmp = c.sb([128, 512], F32, "l4tmp")
    x2b = c.sb([128, 8 * T], BF16, "x2b")
    for fo in range(8):
        for sl in range(0, T, 512):
            for ch in range(8):
                p.op("tensor", lambda e, fo=fo, sl=sl, ch=ch: e.matmul(
                    psB[:, 0:512], weff[:, ch * D + fo * 128: ch * D + fo * 128 + 128],
                    xb[:, ch * T + sl: ch * T + sl + 512], start=(ch == 0), stop=False))
            p.op("tensor", lambda e, fo=fo, sl=sl: e.matmul(
                psB[:, 0:512], negcs[0:1, fo * 128:fo * 128 + 128],
                st["mu"][0:1, sl:sl + 512], start=False, stop=True))
            p.bar()
            p.op("vector", lambda e, sl=sl: e.tensor_mul(tmp[:], psB[:, 0:512], rstd_bc[:, sl:sl + 512]))
            p.bar()
            p.op("vector", lambda e, fo=fo, sl=sl: e.scalar_tensor_tensor(
                xT[:, fo * T + sl: fo * T + sl + 512], tmp[:], bcol[:, fo:fo + 1],
                xT[:, fo * T + sl: fo * T + sl + 512], ALU.add, ALU.add))
            p.bar()
            p.op("vector", lambda e, fo=fo, sl=sl: e.tensor_copy(
                x2b[:, fo * T + sl: fo * T + sl + 512], xT[:, fo * T + sl: fo * T + sl + 512]))
            p.bar()
    st2 = ln_stats(c, x2b, D, T, 1e-5, ps2, rows=st)
    rstd2bc = bcast_row(c, st2["rstd"], T, psA, out=rstd_bc)
    part = c.sb([128, 8], F32, "part_sb")
    scr = c.sb([128, 512], F32, "l4scr")
    pcol = c.sb([128, 1], F32, "pcol")
    smur = c.sb([1, 1], F32, "smur")
    p.op("vector", lambda e: e.reduce_sum(smur[:], st2["mur"][:], axis=AX.X))
    p.bar()
    smurc = c.sb([128, 1], F32, "smurc")
    p.op("tensor", lambda e: e.matmul(psA[:, 0:1], c.ones_row[:], smur[:], start=True, stop=True))
    p.bar()
    p.op("vector", lambda e: e.tensor_copy(smurc[:], psA[:, 0:1]))
    p.bar()
    for ch in range(8):
        for sl in range(0, T, 512):
            p.op("vector", lambda e, ch=ch, sl=sl: e.scalar_tensor_tensor(
                scr[:], x2b[:, ch * T + sl: ch * T + sl + 512], 1.0,
                rstd2bc[:, sl:sl + 512], ALU.mult, ALU.mult, accum_out=pcol[:]))
            p.bar()
            if sl == 0:
                p.op("vector", lambda e, ch=ch: e.tensor_copy(part[:, ch:ch + 1], pcol[:]))
            else:
                p.op("vector", lambda e, ch=ch: e.tensor_add(part[:, ch:ch + 1], part[:, ch:ch + 1], pcol[:]))
            p.bar()
        p.op("vector", lambda e, ch=ch: e.tensor_sub(part[:, ch:ch + 1], part[:, ch:ch + 1], smurc[:]))
        p.bar()
    x2_ap = d["x2_loc"].ap()
    r2_ap = d["r2_loc"].ap()
    for ch in range(8):
        p.op("sync", lambda e, ch=ch: e.dma_start(out=x2_ap[ch * 128:(ch + 1) * 128, :], in_=xT[:, ch * T:(ch + 1) * T]), dma=True)
    p.op("sync", lambda e: e.dma_start(out=d["part_loc"].ap(), in_=part[:]), dma=True)
    p.op("sync", lambda e: e.dma_start(out=r2_ap[0:1, :], in_=st2["mu"][:]), dma=True)
    p.op("sync", lambda e: e.dma_start(out=r2_ap[1:2, :], in_=st2["rstd"][:]), dma=True)
    p.bar()
    B3 = d["B3"].ap()
    p.op("sync", lambda e: e.dma_start(out=B3[0:1024, :], in_=x2_ap[:, T - 40:T]), dma=True)
    p.op("sync", lambda e: e.dma_start(out=B3[1024:1026, :], in_=r2_ap[:, T - 40:T]), dma=True)
    p.bar()
    p.op("gpsimd", lambda e: e.collective_compute(
        "AllReduce", ALU.add, replica_groups=[[0, 1, 2, 3], [4, 5, 6, 7]],
        ins=[d["part_loc"].ap().opt()], outs=[d["partsum_loc"].ap().opt()]).then_inc(d["cc3a"], 1), inc=0)
    p.op("gpsimd", lambda e: e.collective_compute(
        "AllGather", ALU.bypass, replica_groups=[list(range(NCORE))],
        ins=[d["B3"].ap().opt()], outs=[d["G3"].ap().opt()]).then_inc(d["cc3b"], 1), inc=0)
    p.bar()
    c.pop_scope()
    if hasattr(c, "_lnp_tmp"):
        del c._lnp_tmp


def p5_emit(c, d):
    c.prefix = "s5_"
    T = TPC
    HL = 40
    p = c.p
    x_h = d["x2_loc"].ap()
    r2_h = d["r2_loc"].ap()
    hv_h = d["p5_hv"]
    w1_h = d["p5_W1"]
    w2_h = d["p5_W2"]
    mw_h = d["p5_mwT"]
    b1_h = d["p5_b1c"]
    b2_h = d["p5_b2c"]
    gidx = d["gidx"]
    gidx1 = d["gidx1"]
    G3v = d["G3"].ap()
    c.push_scope()
    xb = c.sb([128, 8 * T], BF16, "xb")
    for ch in range(8):
        p.op("gpsimd", lambda e, ch=ch: e.dma_start(out=xb[:, ch * T:(ch + 1) * T], in_=x_h[ch * 128:(ch + 1) * 128, :]), dma=True)
    xhb = c.sb([128, 8 * HL], BF16, "xhb")
    xhf = c.sb([128, 8 * HL], F32, "xhf")
    for ch in range(8):
        igather(c, xhf[:, ch * HL:(ch + 1) * HL], G3v, gidx[:, 40 + ch:40 + ch + 1], pre=[(d["cc3b"], 1)])
    w1 = load_w_bf16(c, w1_h, D, D, "w1_sb")
    w2 = load_w_bf16(c, w2_h, D, D, "w2_sb")
    mwT = c.sb([128, 8 * 41], F32, "mwT_sb")
    for ch in range(8):
        p.op("sync", lambda e, ch=ch: e.dma_start(out=mwT[:, ch * 41:(ch + 1) * 41], in_=mw_h[ch * 128:(ch + 1) * 128, :]), dma=True)
    smalls = {}
    for nm, hh in (("hv", hv_h), ("b1", b1_h), ("b2", b2_h)):
        shp = [128, 1] if nm == "hv" else [128, 8]
        t = c.sb(shp, F32, nm + "_sb")
        p.op("sync", lambda e, t=t, hh=hh: e.dma_start(out=t[:], in_=hh[:]), dma=True)
        smalls[nm] = t
    for nm, row, w in (("r2mu", 0, T), ("r2rs", 1, T)):
        t = c.sb([1, w], F32, nm + "_sb")
        p.op("sync", lambda e, t=t, row=row: e.dma_start(out=t[:], in_=r2_h[row:row + 1, :]), dma=True)
        smalls[nm] = t
    for nm, colI in (("r2hmu", 0), ("r2hrs", 1)):
        t = c.sb([2, HL], F32, nm + "_sb")
        igather(c, t[:], G3v, gidx1[0:2, colI:colI + 1], pre=[(d["cc3b"], 1)])
        smalls[nm] = t
    lnsum = c.sb([128, 8], F32, "lnsum")
    p.op("sync", lambda e: e.dma_start(out=lnsum[:], in_=d["partsum_loc"].ap()), dma=True, pre=[(d["cc3a"], 1)])
    p.bar()
    for ch in range(8):
        p.op("vector", lambda e, ch=ch: e.tensor_copy(xhb[:, ch * HL:(ch + 1) * HL], xhf[:, ch * HL:(ch + 1) * HL]))
    p.bar()
    ps2 = c.ps([1, 1024], name="ps_stat")
    psA = c.ps([128, 512], name="psA")
    psB = c.ps([128, 512], name="psB")
    psH = c.ps([128, 40], name="psH")
    tapT = c.sb([128, 8 * 40], F32, "tapT")
    for ch in range(8):
        p.op("vector", lambda e, ch=ch: e.scalar_tensor_tensor(
            tapT[:, ch * 40:(ch + 1) * 40], mwT[:, ch * 41 + 1:(ch + 1) * 41], 1.0,
            mwT[:, ch * 41:ch * 41 + 1].broadcast_to((128, 40)), ALU.mult, ALU.subtract))
    p.bar()
    PW = HL + T
    pT = c.sb([128, 8 * PW], BF16, "pT")
    negcs = colsum_neg_row(c, w1, D, D, ps2)
    mu_r = smalls["r2mu"][:]
    rstd_r = smalls["r2rs"][:]
    rstd_bc = bcast_row(c, rstd_r, T, psA)
    tmp = c.sb([128, 512], F32, "l5tmp")
    for fo in range(8):
        for sl in range(0, T, 512):
            for ch in range(8):
                p.op("tensor", lambda e, fo=fo, sl=sl, ch=ch: e.matmul(
                    psB[:, 0:512], w1[:, ch * D + fo * 128: ch * D + fo * 128 + 128],
                    xb[:, ch * T + sl: ch * T + sl + 512], start=(ch == 0), stop=False))
            p.op("tensor", lambda e, fo=fo, sl=sl: e.matmul(
                psB[:, 0:512], negcs[0:1, fo * 128:fo * 128 + 128],
                mu_r[0:1, sl:sl + 512], start=False, stop=True))
            p.bar()
            p.op("vector", lambda e, sl=sl: e.tensor_mul(tmp[:], psB[:, 0:512], rstd_bc[:, sl:sl + 512]))
            p.bar()
            p.op("vector", lambda e, fo=fo, sl=sl: e.tensor_scalar(
                pT[:, fo * PW + HL + sl: fo * PW + HL + sl + 512], tmp[:],
                smalls["b1"][:, fo:fo + 1], None, ALU.add, ALU.bypass))
            p.bar()
    rstdh_bc = c.sb([128, HL], F32, "rstdh_bc")
    p.op("tensor", lambda e: e.matmul(psH[:], c.ones_row[:], smalls["r2hrs"][0:1, :], start=True, stop=True))
    p.bar()
    p.op("vector", lambda e: e.tensor_copy(rstdh_bc[:], psH[:]))
    p.bar()
    for fo in range(8):
        for ch in range(8):
            p.op("tensor", lambda e, fo=fo, ch=ch: e.matmul(
                psH[:], w1[:, ch * D + fo * 128: ch * D + fo * 128 + 128],
                xhb[:, ch * HL:(ch + 1) * HL], start=(ch == 0), stop=False))
        p.op("tensor", lambda e, fo=fo: e.matmul(
            psH[:], negcs[0:1, fo * 128:fo * 128 + 128],
            smalls["r2hmu"][0:1, :], start=False, stop=True))
        p.bar()
        p.op("vector", lambda e: e.tensor_mul(tmp[:, 0:HL], psH[:], rstdh_bc[:]))
        p.bar()
        p.op("vector", lambda e, fo=fo: e.tensor_scalar(
            tmp[:, 64:64 + HL], tmp[:, 0:HL], smalls["b1"][:, fo:fo + 1], None, ALU.add, ALU.bypass))
        p.bar()
        p.op("vector", lambda e, fo=fo: e.tensor_scalar(
            pT[:, fo * PW: fo * PW + HL], tmp[:, 64:64 + HL], smalls["hv"][:, 0:1], None, ALU.mult, ALU.bypass))
        p.bar()
    accV = c.sb([128, T], F32, "accV")
    accG = c.sb([128, T], F32, "accG")
    lnsumb = c.sb([128, 8], BF16, "lnsumb")
    totcol = c.sb([128, 8], F32, "totcol")
    b18 = c.sb([128, 8], F32, "b18")
    mtcol = c.sb([128, 8], F32, "mtcol")
    p.op("scalar", lambda e: e.mul(b18[:], smalls["b1"][:], 8192.0))
    p.bar()
    p.op("vector", lambda e: e.tensor_copy(lnsumb[:], lnsum[:]))
    p.bar()
    for ot in range(8):
        for ch in range(8):
            p.op("tensor", lambda e, ot=ot, ch=ch: e.matmul(
                psA[:, 0:1], w1[:, ch * D + ot * 128: ch * D + ot * 128 + 128],
                lnsumb[:, ch:ch + 1], start=(ch == 0), stop=(ch == 7)))
        p.bar()
        p.op("vector", lambda e, ot=ot: e.tensor_add(totcol[:, ot:ot + 1], psA[:, 0:1], b18[:, ot:ot + 1]))
        p.bar()
    for ch in range(8):
        p.op("vector", lambda e, ch=ch: e.tensor_mul(mtcol[:, ch:ch + 1], mwT[:, ch * 41:ch * 41 + 1], totcol[:, ch:ch + 1]))
    p.bar()
    tmpG = c.sb([128, T], F32, "tmpG")
    x3row = c.sb([128, T], F32, "x3row")
    tmpG2 = x3row
    tgbufs = (tmpG, tmpG2)
    for ch in range(8):
        pool_ops = []
        for jj in range(14):
            jg = 26 + jj
            srcG = pT[:, ch * PW + (HL - jg): ch * PW + (HL - jg) + T]
            tapG = tapT[:, ch * 40 + jg: ch * 40 + jg + 1]
            buf = jj % 2
            pool_ops.append(lambda e, srcG=srcG, tapG=tapG, buf=buf: e.tensor_scalar(
                tgbufs[buf][:], srcG, tapG, None, ALU.mult, ALU.bypass))
            if jj == 0:
                pool_ops.append(lambda e, buf=buf: e.tensor_copy(accG[:], tgbufs[buf][:]))
            else:
                pool_ops.append(lambda e, buf=buf: e.tensor_add(accG[:], accG[:], tgbufs[buf][:]))
        pq = list(pool_ops)
        for j in range(26):
            srcV = pT[:, ch * PW + (HL - j): ch * PW + (HL - j) + T]
            tapV = tapT[:, ch * 40 + j: ch * 40 + j + 1]
            if j == 0:
                p.op("vector", lambda e, srcV=srcV, tapV=tapV: e.tensor_scalar(
                    accV[:], srcV, tapV, None, ALU.mult, ALU.bypass))
            else:
                p.op("vector", lambda e, srcV=srcV, tapV=tapV: e.scalar_tensor_tensor(
                    accV[:], srcV, tapV, accV[:], ALU.mult, ALU.add))
            if pq:
                p.op("gpsimd", pq.pop(0))
                if len(pq) > 26 - j:
                    p.op("gpsimd", pq.pop(0))
            p.bar()
        while pq:
            p.op("gpsimd", pq.pop(0))
            p.bar()
        p.op("vector", lambda e, ch=ch: e.scalar_tensor_tensor(
            accV[:], accV[:], mtcol[:, ch:ch + 1], accG[:], ALU.add, ALU.add))
        p.bar()
        p.op("vector", lambda e, ch=ch: e.tensor_copy(pT[:, ch * PW + HL:(ch + 1) * PW], accV[:]))
        p.bar()
    x3b = xb
    xbuf = c.sb([128, 1024], F32, "xbuf")
    x3_ap = d["x3_loc"].ap()
    x3b_ap = d["x3b_loc"].ap()
    r3_ap = d["r3_loc"].ap()
    fsl = [(fo, sl) for fo in range(8) for sl in range(0, T, 512)]
    p.op("sync", lambda e: e.dma_start(
        out=xbuf[:, 0:512], in_=x_h[0:128, 0:512]), dma=True)
    p.bar()
    for idx, (fo, sl) in enumerate(fsl):
        bcur = idx % 2
        if idx + 1 < len(fsl):
            nfo, nsl = fsl[idx + 1]
            p.op("sync", lambda e, nfo=nfo, nsl=nsl, bn=(1 - bcur): e.dma_start(
                out=xbuf[:, bn * 512:(bn + 1) * 512],
                in_=x_h[nfo * 128:(nfo + 1) * 128, nsl:nsl + 512]), dma=True)
        if True:
            for ch in range(8):
                p.op("tensor", lambda e, fo=fo, sl=sl, ch=ch: e.matmul(
                    psB[:, 0:512], w2[:, ch * D + fo * 128: ch * D + fo * 128 + 128],
                    pT[:, ch * PW + HL + sl: ch * PW + HL + sl + 512], start=(ch == 0), stop=(ch == 7)))
            p.bar()
            p.op("vector", lambda e, fo=fo, sl=sl, bcur=bcur: e.scalar_tensor_tensor(
                x3row[:, sl:sl + 512], psB[:, 0:512], smalls["b2"][:, fo:fo + 1],
                xbuf[:, bcur * 512:(bcur + 1) * 512], ALU.add, ALU.add))
            p.bar()
            p.op("vector", lambda e, fo=fo, sl=sl: e.tensor_copy(
                x3b[:, fo * T + sl: fo * T + sl + 512], x3row[:, sl:sl + 512]))
            p.bar()
        if sl == T - 512:
            p.op("sync", lambda e, fo=fo: e.dma_start(out=x3_ap[fo * 128:(fo + 1) * 128, :], in_=x3row[:]), dma=True)
            p.op("sync", lambda e, fo=fo: e.dma_start(out=x3b_ap[fo * 128:(fo + 1) * 128, :], in_=x3b[:, fo * T:(fo + 1) * T]), dma=True)
            p.bar()
    st3 = ln_stats(c, x3b, D, T, 1e-5, ps2)
    p.op("sync", lambda e: e.dma_start(out=r3_ap[0:1, :], in_=st3["mu"][:]), dma=True)
    p.op("sync", lambda e: e.dma_start(out=r3_ap[1:2, :], in_=st3["rstd"][:]), dma=True)
    p.bar()
    p.op("gpsimd", lambda e: e.collective_compute(
        "AllGather", ALU.bypass, replica_groups=[list(range(NCORE))],
        ins=[d["x3b_loc"].ap().opt()], outs=[d["G4"].ap().opt()]).then_inc(d["cc4a"], 1), inc=0)
    p.op("gpsimd", lambda e: e.collective_compute(
        "AllGather", ALU.bypass, replica_groups=[list(range(NCORE))],
        ins=[d["r3_loc"].ap().opt()], outs=[d["G4b"].ap().opt()]).then_inc(d["cc4b"], 1), inc=0)
    p.bar()
    c.pop_scope()


def p6_emit(c, d, do_cc5=True, do_attn=True, kt_eng="gpsimd", lvl=9):
    c.prefix = "s6_"
    S = 8192
    p = c.p
    qS_h = d["p6_qS"]
    kS_h = d["p6_kS"]
    vS_h = d["p6_vS"]
    qb_h = d["p6_qb"]
    kb_h = d["p6_kb"]
    vb_h = d["p6_vb"]
    cos_h = d["p6_cosT"]
    sin_h = d["p6_sinsT"]
    gidx = d["gidx"]
    gidx1 = d["gidx1"]
    G4v = d["G4"].ap().rearrange("r (s q) -> (r s) q", s=2)
    G4bv = d["G4b"].ap().rearrange("r (s q) -> (r s) q", s=2)
    c.push_scope()
    QT = c.sb([128, 2 * S], BF16, "QT")
    KT = c.sb([128, 2 * S], BF16, "KT")
    VTO = c.sb([128, 64 * 264], BF16, "VTO")
    qS = load_w_bf16(c, qS_h, D, 256, "qS_sb")
    kS = load_w_bf16(c, kS_h, D, 256, "kS_sb")
    vS = load_w_bf16(c, vS_h, D, 256, "vS_sb")
    qb = c.sb([128, 2], F32, "qb_sb")
    kb = c.sb([128, 2], F32, "kb_sb")
    vb = c.sb([128, 2], F32, "vb_sb")
    identb6 = c.sb([128, 128], BF16, "identb6")
    p.op("gpsimd", lambda e: e.dma_start(out=identb6[:], in_=d["identf"][:]), dma=True)
    p.op("sync", lambda e: e.dma_start(out=qb[:], in_=qb_h[:]), dma=True)
    p.op("sync", lambda e: e.dma_start(out=kb[:], in_=kb_h[:]), dma=True)
    p.op("sync", lambda e: e.dma_start(out=vb[:], in_=vb_h[:]), dma=True)
    p.bar()
    TS = 1024
    c.push_scope()
    QKp = c.ps([128, 1024], name="QKp")
    Vtp = c.ps([128, 512], BF16, name="Vtp")
    ps_stat = c.ps([1, 512], name="ps_stat6")
    bcastp = c.ps([128, 512], name="bcastp")
    negq = colsum_neg_row(c, qS, D, 256, ps_stat)
    negk = colsum_neg_row(c, kS, D, 256, ps_stat)
    negv = colsum_neg_row(c, vS, D, 256, ps_stat)
    for h in range(4):
        p.op("vector", lambda e, h=h: e.memset(
            VTO[:].rearrange("p (t q) -> p t q", t=64)[:, :, h * 66 + 64: h * 66 + 66], 1.0))
    p.bar()
    xsb2 = c.sb([128, 2 * 8 * TS], BF16, "xsb2")
    murow2 = c.sb([2, 2 * TS], F32, "murow2")
    rsrow2 = c.sb([2, 2 * TS], F32, "rsrow2")
    rstd_bc = c.sb([128, TS], F32, "rstd_bc6")
    tmpE = c.sb([128, 1024], F32, "tmpE")
    vT = c.sb([128, 2 * TS], BF16, "vT6")
    rtq = c.sb([128, TS], BF16, "rtq")
    rtk = c.sb([128, TS], BF16, "rtk")
    cosT2 = c.sb([128, 2 * TS], BF16, "cosT2")
    sinsT2 = c.sb([128, 2 * TS], BF16, "sinsT2")

    def gather_ops(tsl, buf):
        ops = []
        for ch in range(8):
            ops.append(lambda e, tsl=tsl, buf=buf, ch=ch: e.indirect_dma_start(
                out=xsb2[:, buf * 8 * TS + ch * TS: buf * 8 * TS + (ch + 1) * TS],
                out_offset=None, in_=G4v,
                in_offset=bass.IndirectOffsetOnAxis(ap=gidx[:, 48 + tsl * 8 + ch: 48 + tsl * 8 + ch + 1], axis=0)))
        ops.append(lambda e, tsl=tsl, buf=buf: e.indirect_dma_start(
            out=murow2[:, buf * TS:(buf + 1) * TS], out_offset=None, in_=G4bv,
            in_offset=bass.IndirectOffsetOnAxis(ap=gidx1[0:2, 2 + tsl: 3 + tsl], axis=0)))
        ops.append(lambda e, tsl=tsl, buf=buf: e.indirect_dma_start(
            out=rsrow2[:, buf * TS:(buf + 1) * TS], out_offset=None, in_=G4bv,
            in_offset=bass.IndirectOffsetOnAxis(ap=gidx1[0:2, 10 + tsl: 11 + tsl], axis=0)))
        ops.append(lambda e, tsl=tsl, buf=buf: e.dma_start(
            out=cosT2[:, buf * TS:(buf + 1) * TS], in_=cos_h[:, tsl * TS:(tsl + 1) * TS]))
        ops.append(lambda e, tsl=tsl, buf=buf: e.dma_start(
            out=sinsT2[:, buf * TS:(buf + 1) * TS], in_=sin_h[:, tsl * TS:(tsl + 1) * TS]))
        return ops

    for op_ in gather_ops(0, 0):
        p.op("gpsimd", op_, dma=True, pre=[(d["cc4a"], 1), (d["cc4b"], 1)])
    p.bar()
    for tsl in range(8):
        t0 = tsl * TS
        buf = tsl % 2
        xsb = xsb2[:, buf * 8 * TS:(buf + 1) * 8 * TS]
        murow = murow2[0:1, buf * TS:(buf + 1) * TS]
        rsrow = rsrow2[0:1, buf * TS:(buf + 1) * TS]
        pend = gather_ops(tsl + 1, 1 - buf) if tsl < 7 else []
        if lvl < 1:
            p.bar()
            continue
        p.op("tensor", lambda e, rsrow=rsrow: e.matmul(bcastp[:, 0:512], c.ones_row[:], rsrow[0:1, 0:512], start=True, stop=True))
        p.bar()
        p.op("vector", lambda e: e.tensor_copy(rstd_bc[:, 0:512], bcastp[:, 0:512]))
        p.op("tensor", lambda e, rsrow=rsrow: e.matmul(QKp[:, 0:512], c.ones_row[:], rsrow[0:1, 512:1024], start=True, stop=True))
        p.bar()
        p.op("vector", lambda e: e.tensor_copy(rstd_bc[:, 512:1024], QKp[:, 0:512]))
        p.bar()
        units = []
        for (dst, wt, ng, bcol) in ((QT, qS, negq, qb), (KT, kS, negk, kb), (vT, vS, negv, vb)):
            for pair in range(2):
                for sl in range(0, TS, 512):
                    units.append((dst, wt, ng, bcol, pair, sl))
        NU = len(units)

        def unit_mm(u):
            dst, wt, ng, bcol, pair, sl = units[u]
            pb = QKp[:, (u % 2) * 512:(u % 2) * 512 + 512]
            for ch in range(8):
                p.op("tensor", lambda e, wt=wt, pair=pair, sl=sl, ch=ch, pb=pb, xsb=xsb: e.matmul(
                    pb, wt[:, ch * 256 + pair * 128: ch * 256 + pair * 128 + 128],
                    xsb[:, ch * TS + sl: ch * TS + sl + 512], start=(ch == 0), stop=False))
            p.op("tensor", lambda e, ng=ng, pair=pair, sl=sl, pb=pb, murow=murow: e.matmul(
                pb, ng[0:1, pair * 128:pair * 128 + 128],
                murow[0:1, sl:sl + 512], start=False, stop=True))

        def unit_mul(u):
            _, _, _, _, pair, sl = units[u]
            pb = QKp[:, (u % 2) * 512:(u % 2) * 512 + 512]
            tb = tmpE[:, (u % 2) * 512:(u % 2) * 512 + 512]
            p.op("vector", lambda e, pb=pb, tb=tb, sl=sl: e.tensor_mul(tb, pb, rstd_bc[:, sl:sl + 512]))

        def unit_bias(u):
            dst, _, _, bcol, pair, sl = units[u]
            tb = tmpE[:, (u % 2) * 512:(u % 2) * 512 + 512]
            if dst is vT:
                seg = vT[:, pair * TS + sl: pair * TS + sl + 512]
            else:
                seg = dst[:, pair * S + t0 + sl: pair * S + t0 + sl + 512]
            p.op("vector", lambda e, seg=seg, tb=tb, bcol=bcol, pair=pair: e.tensor_scalar(
                seg, tb, bcol[:, pair:pair + 1], None, ALU.add, ALU.bypass))

        import os as _os
        _ser = _os.environ.get("P6SER", "0") == "1"
        for st in range(NU + 2):
            if lvl < 2:
                break
            if st < NU:
                unit_mm(st)
                if _ser:
                    p.bar()
            if 1 <= st <= NU:
                unit_mul(st - 1)
                if _ser:
                    p.bar()
            if 2 <= st <= NU + 1:
                unit_bias(st - 2)
            if st < len(pend):
                p.op("gpsimd", pend[st], dma=True)
            p.bar()
        if lvl < 3:
            continue
        rope_q = []
        rope_k = []
        for (dst, rt, ops_l) in ((QT, rtq, rope_q), (KT, rtk, rope_k)):
            for pair in range(2):
                base = pair * S + t0
                seg = dst[:, base: base + TS]
                grp = []
                for r0 in range(0, 128, 32):
                    src_r = (r0 + 32) if (r0 % 64 == 0) else (r0 - 32)
                    grp.append(lambda e, seg=seg, r0=r0, src_r=src_r, rt=rt: e.tensor_copy(
                        rt[r0:r0 + 32, :], seg[src_r:src_r + 32, :]))
                ops_l.append(grp)
                ops_l.append([lambda e, rt=rt, buf=buf: e.tensor_mul(rt[:], rt[:], sinsT2[:, buf * TS:(buf + 1) * TS])])
                ops_l.append([lambda e, seg=seg, buf=buf: e.tensor_mul(seg, seg, cosT2[:, buf * TS:(buf + 1) * TS])])
                ops_l.append([lambda e, seg=seg, rt=rt: e.tensor_add(seg, seg, rt[:])])
        for s in range(9):
            if s < 8 and lvl >= 4:
                tt = s
                for pair in range(2):
                    p.op("tensor", lambda e, tt=tt, pair=pair: e.transpose(
                        Vtp[:, (tt % 2) * 256 + pair * 128: (tt % 2) * 256 + pair * 128 + 128],
                        vT[:, pair * TS + tt * 128: pair * TS + tt * 128 + 128], identb6[:]))
                if s < len(rope_q):
                    for f in rope_q[s]:
                        p.op("vector", f)
                    for f in rope_k[s]:
                        p.op(kt_eng, f)
            if s >= 1 and lvl >= 4:
                tt = s - 1
                gt = tsl * 8 + tt
                for pair in range(2):
                    for hh in range(2):
                        h4 = pair * 2 + hh
                        p.op("vector", lambda e, tt=tt, gt=gt, pair=pair, hh=hh, h4=h4: e.tensor_copy(
                            VTO[:, gt * 264 + h4 * 66: gt * 264 + h4 * 66 + 64],
                            Vtp[:, (tt % 2) * 256 + pair * 128 + hh * 64: (tt % 2) * 256 + pair * 128 + hh * 64 + 64]))
            p.bar()
    if d.get("dbg_xsb2") is not None:
        p.op("sync", lambda e: e.dma_start(out=d["dbg_xsb2"], in_=xsb2[:]), dma=True)
        p.op("sync", lambda e: e.dma_start(out=d["dbg_mu2"], in_=murow2[:]), dma=True)
        p.op("sync", lambda e: e.dma_start(out=d["dbg_rs2"], in_=rsrow2[:]), dma=True)
        p.bar()
    c.pop_scope()
    if d.get("dbg_QT") is not None:
        p.op("sync", lambda e: e.dma_start(out=d["dbg_QT"], in_=QT[:]), dma=True)
        p.op("sync", lambda e: e.dma_start(out=d["dbg_KT"], in_=KT[:]), dma=True)
        p.op("sync", lambda e: e.dma_start(out=d["dbg_VTO"], in_=VTO[:]), dma=True)
        p.bar()
    if not do_attn:
        c.pop_scope()
        return
    c.push_scope()
    SC0 = c.ps([128, 1024], name="SC0")
    SC1 = c.ps([128, 1024], name="SC1")
    ACC = c.ps([65, 1024], name="ACC")
    BCp = c.ps([64, 1024], name="BCp")
    SCs = (SC0, SC1)
    PT2 = c.sb([128, 2048], BF16, "PT2")
    accsb = c.sb([64, 1024], F32, "accsb")
    recr = c.sb([1, 1024], F32, "recr")
    abuf = c.sb([64, 1024], BF16, "abuf")
    ao_ap = d["ao_loc"].ap()
    iters = [(h, qb_) for h in range(4) for qb_ in range(8)]

    def emit_evac(ph, pq0):
        return [
            [("vector", lambda e: e.tensor_copy(accsb[:], ACC[0:64, :])),
             ("vector", lambda e: e.reciprocal(recr[:], ACC[64:65, :]))],
            [("tensor", lambda e: e.matmul(BCp[:, 0:512], c.ones_row[0:1, 0:64], recr[0:1, 0:512], start=True, stop=True)),
             ("tensor", lambda e: e.matmul(BCp[:, 512:1024], c.ones_row[0:1, 0:64], recr[0:1, 512:1024], start=True, stop=True))],
            [("vector", lambda e: e.tensor_mul(abuf[:], accsb[:], BCp[:]))],
            [("sync", lambda e, ph=ph, pq0=pq0: e.dma_start(
                out=ao_ap[ph * 64:(ph + 1) * 64, pq0:pq0 + 1024], in_=abuf[:]))],
        ]

    carried = []
    for (h, qb_) in iters:
        pair = h // 2
        r0 = (h % 2) * 64
        q0 = qb_ * 1024
        for t in range(66):
            if t < 64:
                sc = SCs[t % 2]
                for half in range(2):
                    p.op("tensor", lambda e, t=t, half=half, sc=sc, pair=pair, r0=r0, q0=q0: e.matmul(
                        sc[:, half * 512:(half + 1) * 512],
                        KT[r0:r0 + 64, pair * S + t * 128: pair * S + t * 128 + 128],
                        QT[r0:r0 + 64, pair * S + q0 + half * 512: pair * S + q0 + half * 512 + 512],
                        start=True, stop=True))
            if 1 <= t <= 64:
                sc = SCs[(t - 1) % 2]
                p.op("scalar", lambda e, t=t, sc=sc: e.activation(
                    PT2[:, ((t - 1) % 2) * 1024: ((t - 1) % 2) * 1024 + 1024],
                    sc[:], AF.Exp, bias=-24.0, scale=0.125))
            if 2 <= t <= 65:
                ta = t - 2
                for half in range(2):
                    p.op("tensor", lambda e, ta=ta, half=half, h=h: e.matmul(
                        ACC[:, half * 512:(half + 1) * 512],
                        VTO[:, ta * 264 + h * 66: ta * 264 + h * 66 + 65],
                        PT2[:, (ta % 2) * 1024 + half * 512: (ta % 2) * 1024 + half * 512 + 512],
                        start=(ta == 0), stop=(ta == 63)))
            if t < len(carried):
                for eng, fn in carried[t]:
                    p.op(eng, fn, dma=(eng == "sync"))
            p.bar()
        carried = emit_evac(h, q0)
    for slot in carried:
        for eng, fn in slot:
            p.op(eng, fn, dma=(eng == "sync"))
        p.bar()
    c.pop_scope()
    if do_cc5:
        p.op("gpsimd", lambda e: e.collective_compute(
            "AllGather", ALU.bypass, replica_groups=[list(range(NCORE))],
            ins=[d["ao_loc"].ap().opt()], outs=[d["G5"].ap().opt()]).then_inc(d["cc5"], 1), inc=0)
        p.bar()
    c.pop_scope()


def p7_emit(c, d):
    c.prefix = "s7_"
    T = TPC
    p = c.p
    x_h = d["x3_loc"].ap()
    r3_h = d["r3_loc"].ap()
    gw_h = d["p7_gwT"]
    gb_h = d["p7_gb"]
    ew_h = d["p7_ewT"]
    eb_h = d["p7_ebrow"]
    fw_h = d["p7_ffnT"]
    fb_h = d["p7_fb"]
    id_h = d["identf"]
    out_h = d["outT"]
    gidx = d["gidx"]
    G5v = d["G5"].ap().rearrange("r (s q) -> (r s) q", s=4)
    c.push_scope()
    aoT = c.sb([128, 8 * T], BF16, "aoT_sb")
    for ch in range(8):
        igather(c, aoT[:, ch * T:(ch + 1) * T], G5v, gidx[:, 112 + ch:112 + ch + 1], pre=[(d["cc5"], 1)])
    gwT = load_w_bf16(c, gw_h, D, 8, "gwT_sb")
    identf = c.sb([128, 128], F32, "identf7")
    identb = c.sb([128, 128], BF16, "identb7")
    p.op("sync", lambda e: e.dma_start(out=identf[:], in_=id_h[:]), dma=True)
    p.op("gpsimd", lambda e: e.dma_start(out=identb[:], in_=id_h[:]), dma=True)
    smalls = {}
    for nm, hh, shp in (("gb", gb_h, [8, 1]), ("fb", fb_h, [128, 8])):
        t = c.sb(shp, F32, nm + "_sb7")
        p.op("sync", lambda e, t=t, hh=hh: e.dma_start(out=t[:], in_=hh[:]), dma=True)
        smalls[nm] = t
    for nm, row in (("rmu", 0), ("rrs", 1)):
        t = c.sb([1, T], F32, nm + "_sb7")
        p.op("sync", lambda e, t=t, row=row: e.dma_start(out=t[:], in_=r3_h[row:row + 1, :]), dma=True)
        smalls[nm] = t
    ebrow = c.sb([1, 8 * D], BF16, "ebrow_sb")
    p.op("gpsimd", lambda e: e.dma_start(out=ebrow[:], in_=eb_h[:]), dma=True)
    p.bar()
    P1 = c.ps([128, 512], name="P1_7")
    P2 = c.ps([128, 512], name="P2_7")
    PB = c.ps([128, 256], BF16, name="PB_7")
    glog = c.sb([8, T], F32, "glog")
    for sl in range(0, T, 512):
        for ch in range(8):
            p.op("tensor", lambda e, sl=sl, ch=ch: e.matmul(
                P1[0:8, 0:512], gwT[:, ch * 8:(ch + 1) * 8],
                aoT[:, ch * T + sl: ch * T + sl + 512], start=(ch == 0), stop=(ch == 7)))
        p.bar()
        p.op("vector", lambda e, sl=sl: e.tensor_scalar(
            glog[0:8, sl:sl + 512], P1[0:8, 0:512], smalls["gb"][:, 0:1], None, ALU.add, ALU.bypass))
        p.bar()
    wcols = c.sb([128, 16 * 8], F32, "wcols")
    gt = c.sb([128, 8], F32, "gt7")
    eg = c.sb([128, 8], F32, "eg7")
    r1 = c.sb([128, 1], F32, "r1_7")
    r2 = c.sb([128, 1], F32, "r2_7")
    for tt in range(16):
        p.op("tensor", lambda e, tt=tt: e.transpose(P2[:, 0:8], glog[0:8, tt * 128:(tt + 1) * 128], identf[0:8, 0:8]))
        p.bar()
        p.op("vector", lambda e: e.tensor_copy(gt[:], P2[:, 0:8]))
        p.bar()
        p.op("vector", lambda e: e.reduce_max(r1[:], gt[:], axis=AX.X))
        p.bar()
        p.op("vector", lambda e: e.tensor_scalar(r2[:], r1[:], -1.0, None, ALU.mult, ALU.bypass))
        p.bar()
        p.op("scalar", lambda e: e.activation(eg[:], gt[:], AF.Exp, bias=r2[:, 0:1], scale=1.0))
        p.bar()
        p.op("vector", lambda e: e.reduce_sum(r1[:], eg[:], axis=AX.X))
        p.bar()
        p.op("vector", lambda e: e.reciprocal(r2[:], r1[:]))
        p.bar()
        p.op("vector", lambda e: e.tensor_scalar(gt[:], eg[:], r2[:, 0:1], None, ALU.mult, ALU.bypass))
        p.bar()
        p.op("vector", lambda e: e.reduce_max(r1[:], gt[:], axis=AX.X))
        p.bar()
        p.op("vector", lambda e: e.tensor_single_scalar(eg[:], gt[:], r1[:, 0:1], ALU.is_ge))
        p.bar()
        p.op("vector", lambda e: e.scalar_tensor_tensor(eg[:], eg[:], -1e9, gt[:], ALU.mult, ALU.add))
        p.bar()
        p.op("vector", lambda e: e.reduce_max(r2[:], eg[:], axis=AX.X))
        p.bar()
        p.op("vector", lambda e: e.tensor_single_scalar(eg[:], gt[:], r2[:, 0:1], ALU.is_ge))
        p.bar()
        p.op("vector", lambda e, tt=tt: e.tensor_mul(wcols[:, tt * 8:(tt + 1) * 8], gt[:], eg[:]))
        p.bar()
    ebuf = c.sb([128, 8 * D], BF16, "ebuf")
    macc = c.sb([128, 16 * 1024], F32, "macc")
    for e_ in range(8):
        for ch in range(8):
            p.op("gpsimd", lambda e, e_=e_, ch=ch: e.dma_start(
                out=ebuf[:, ch * D:(ch + 1) * D],
                in_=ew_h[e_ * D + ch * 128: e_ * D + (ch + 1) * 128, :]), dma=True)
        p.bar()
        for tt in range(16):
            for half, ph in ((0, P1), (1, P2)):
                for ch in range(8):
                    p.op("tensor", lambda e, tt=tt, ch=ch, half=half, ph=ph: e.matmul(
                        ph[:, 0:512], aoT[:, ch * T + tt * 128: ch * T + tt * 128 + 128],
                        ebuf[:, ch * D + half * 512: ch * D + half * 512 + 512],
                        start=(ch == 0), stop=False))
                p.op("tensor", lambda e, e_=e_, half=half, ph=ph: e.matmul(
                    ph[:, 0:512], c.ones_row_b[0:1, 0:128],
                    ebrow[0:1, e_ * D + half * 512: e_ * D + (half + 1) * 512], start=False, stop=True))
            p.bar()
            for half, ph in ((0, P1), (1, P2)):
                col = wcols[:, tt * 8 + e_: tt * 8 + e_ + 1]
                dst = macc[:, tt * 1024 + half * 512: tt * 1024 + half * 512 + 512]
                if e_ == 0:
                    p.op("vector", lambda e, ph=ph, col=col, dst=dst: e.tensor_scalar(
                        dst, ph[:, 0:512], col, None, ALU.mult, ALU.bypass))
                else:
                    p.op("vector", lambda e, ph=ph, col=col, dst=dst: e.scalar_tensor_tensor(
                        dst, ph[:, 0:512], col, dst, ALU.mult, ALU.add))
            p.bar()
    moeT = aoT
    stg = ebuf
    for tt in range(16):
        p.op("vector", lambda e, tt=tt: e.tensor_copy(stg[:, 0:1024], macc[:, tt * 1024:(tt + 1) * 1024]))
        p.bar()
        for ch in range(8):
            p.op("tensor", lambda e, ch=ch: e.transpose(PB[:, 0:128], stg[:, ch * 128:(ch + 1) * 128], identb[:]))
            p.bar()
            p.op("vector", lambda e, tt=tt, ch=ch: e.tensor_copy(
                moeT[:, ch * T + tt * 128:(ch * T + tt * 128) + 128], PB[:, 0:128]))
            p.bar()
    ffnw = ebuf
    for ch in range(8):
        p.op("gpsimd", lambda e, ch=ch: e.dma_start(
            out=ffnw[:, ch * D:(ch + 1) * D], in_=fw_h[ch * 128:(ch + 1) * 128, :]), dma=True)
    p.bar()
    mubc = c.sb([128, T], F32, "mubc7")
    rsbc = c.sb([128, T], F32, "rsbc7")
    for sl in range(0, T, 512):
        p.op("tensor", lambda e, sl=sl: e.matmul(P1[:, 0:512], c.ones_row[:], smalls["rmu"][0:1, sl:sl + 512], start=True, stop=True))
        p.bar()
        p.op("vector", lambda e, sl=sl: e.tensor_copy(mubc[:, sl:sl + 512], P1[:, 0:512]))
        p.bar()
        p.op("tensor", lambda e, sl=sl: e.matmul(P1[:, 0:512], c.ones_row[:], smalls["rrs"][0:1, sl:sl + 512], start=True, stop=True))
        p.bar()
        p.op("vector", lambda e, sl=sl: e.tensor_copy(rsbc[:, sl:sl + 512], P1[:, 0:512]))
        p.bar()
    tmp = c.sb([128, 512], F32, "tmp7")
    obuf = c.sb([128, 512], F32, "obuf7")
    xbuf = c.sb([128, 1024], F32, "xbuf7")
    fsl7 = [(fo, sl) for fo in range(8) for sl in range(0, T, 512)]
    p.op("sync", lambda e: e.dma_start(out=xbuf[:, 0:512], in_=x_h[0:128, 0:512]), dma=True)
    p.bar()
    for idx, (fo, sl) in enumerate(fsl7):
        bcur = idx % 2
        if idx + 1 < len(fsl7):
            nfo, nsl = fsl7[idx + 1]
            p.op("sync", lambda e, nfo=nfo, nsl=nsl, bn=(1 - bcur): e.dma_start(
                out=xbuf[:, bn * 512:(bn + 1) * 512],
                in_=x_h[nfo * 128:(nfo + 1) * 128, nsl:nsl + 512]), dma=True)
        if True:
            for ch in range(8):
                p.op("tensor", lambda e, fo=fo, sl=sl, ch=ch: e.matmul(
                    P2[:, 0:512], ffnw[:, ch * D + fo * 128: ch * D + fo * 128 + 128],
                    moeT[:, ch * T + sl: ch * T + sl + 512], start=(ch == 0), stop=(ch == 7)))
            p.bar()
            p.op("vector", lambda e, sl=sl, bcur=bcur: e.tensor_sub(
                tmp[:], xbuf[:, bcur * 512:(bcur + 1) * 512], mubc[:, sl:sl + 512]))
            p.bar()
            p.op("vector", lambda e, sl=sl: e.tensor_mul(tmp[:], tmp[:], rsbc[:, sl:sl + 512]))
            p.bar()
            p.op("vector", lambda e, fo=fo: e.scalar_tensor_tensor(
                tmp[:], P2[:, 0:512], smalls["fb"][:, fo:fo + 1], tmp[:], ALU.add, ALU.add))
            p.bar()
            p.op("vector", lambda e, bcur=bcur: e.tensor_add(
                obuf[:], tmp[:], xbuf[:, bcur * 512:(bcur + 1) * 512]))
            p.bar()
            p.op("sync", lambda e, fo=fo, sl=sl: e.dma_start(
                out=out_h[fo * 128:(fo + 1) * 128, sl:sl + 512], in_=obuf[:]), dma=True)
            p.bar()
    c.pop_scope()


def build_fused():
    c = Ctx()
    nc = c.nc
    d = {}
    d["xT"] = c.inp("xT", [D, TPC], F32)
    d["p1_down_w"] = c.inp("p1_down_w", [D, 256], F32)
    d["p1_down_b"] = c.inp("p1_down_b", [128, 2], F32)
    d["p2_wq"] = c.inp("p2_wq", [256, 128], F32)
    d["p2_wk"] = c.inp("p2_wk", [256, 128], F32)
    d["p2_wv"] = c.inp("p2_wv", [256, 128], F32)
    d["p2_lrw"] = c.inp("p2_lrw", [128, 2], F32)
    d["p2_lrb"] = c.inp("p2_lrb", [1, 1], F32)
    d["p2_tokc"] = c.inp("p2_tokc", [16, 1], F32)
    d["p2_invc"] = c.inp("p2_invc", [16, 1], F32)
    d["p2_maskT"] = c.inp("p2_maskT", [16, 16], F32)
    d["identf"] = c.inp("identf", [128, 128], F32)
    d["p2_W1"] = c.inp("p2_W1", [128, 128], F32)
    d["p2_b1"] = c.inp("p2_b1", [1, 128], F32)
    d["p2_gw"] = c.inp("p2_gw", [1, 128], F32)
    d["p2_gb"] = c.inp("p2_gb", [1, 128], F32)
    d["p2_gwc"] = c.inp("p2_gwc", [128, 1], F32)
    d["p3_wo"] = c.inp("p3_wo", [256, 256], F32)
    d["p3_up"] = c.inp("p3_up", [256, D], F32)
    d["p3_pw"] = c.inp("p3_pw", [128, 2], F32)
    d["p3_pb"] = c.inp("p3_pb", [128, 2], F32)
    d["p3_lw"] = c.inp("p3_lw", [128, 2], F32)
    d["p3_lb"] = c.inp("p3_lb", [128, 2], F32)
    d["p3_ub"] = c.inp("p3_ub", [128, 8], F32)
    d["p4_WT"] = c.inp("p4_WT", [D, D], F32)
    d["p4_ldT"] = c.inp("p4_ldT", [100, 4], F32)
    d["p4_da"] = c.inp("p4_da", [100, D], F32)
    d["p4_luT"] = c.inp("p4_luT", [4, 50], F32)
    d["p4_uaT"] = c.inp("p4_uaT", [50, D], F32)
    d["p4_b"] = c.inp("p4_b", [128, 8], F32)
    d["p5_W1"] = c.inp("p5_W1", [D, D], F32)
    d["p5_W2"] = c.inp("p5_W2", [D, D], F32)
    d["p5_mwT"] = c.inp("p5_mwT", [D, 41], F32)
    d["p5_b1c"] = c.inp("p5_b1c", [128, 8], F32)
    d["p5_b2c"] = c.inp("p5_b2c", [128, 8], F32)
    d["p5_hv"] = c.inp("p5_hv", [128, 1], F32)
    d["p6_qS"] = c.inp("p6_qS", [D, 256], F32)
    d["p6_kS"] = c.inp("p6_kS", [D, 256], F32)
    d["p6_vS"] = c.inp("p6_vS", [D, 256], F32)
    d["p6_qb"] = c.inp("p6_qb", [128, 2], F32)
    d["p6_kb"] = c.inp("p6_kb", [128, 2], F32)
    d["p6_vb"] = c.inp("p6_vb", [128, 2], F32)
    d["p6_cosT"] = c.inp("p6_cosT", [128, 8192], BF16)
    d["p6_sinsT"] = c.inp("p6_sinsT", [128, 8192], BF16)
    d["p7_gwT"] = c.inp("p7_gwT", [D, 8], F32)
    d["p7_gb"] = c.inp("p7_gb", [8, 1], F32)
    d["p7_ewT"] = c.inp("p7_ewT", [8 * D, D], F32)
    d["p7_ebrow"] = c.inp("p7_ebrow", [1, 8 * D], F32)
    d["p7_ffnT"] = c.inp("p7_ffnT", [D, D], F32)
    d["p7_fb"] = c.inp("p7_fb", [128, 8], F32)
    d["gidx_h"] = c.inp("gidx", [128, 120], I32)
    d["gidx1_h"] = c.inp("gidx1", [2, 18], I32)
    d["outT"] = c.out("outT", [D, TPC], F32)
    d["h_loc"] = c.dram("h_loc", [256, TPC], BF16)
    d["G1"] = c.dram("G1", [8 * 256, TPC], BF16, shared=True)
    d["outs_loc"] = c.dram("outs_loc", [8192, 128], BF16)
    d["G2"] = c.dram("G2", [4 * 8192, 128], BF16, shared=False)
    d["x1_loc"] = c.dram("x1_loc", [D, TPC], F32)
    d["x2_loc"] = c.dram("x2_loc", [D, TPC], F32)
    d["r2_loc"] = c.dram("r2_loc", [2, TPC], F32)
    d["part_loc"] = c.dram("part_loc", [128, 8], F32)
    d["partsum_loc"] = c.dram("partsum_loc", [128, 8], F32)
    d["B3"] = c.dram("B3", [1026, 40], F32)
    d["G3"] = c.dram("G3", [8 * 1026, 40], F32, shared=True)
    d["x3_loc"] = c.dram("x3_loc", [D, TPC], F32)
    d["x3b_loc"] = c.dram("x3b_loc", [D, TPC], BF16)
    d["r3_loc"] = c.dram("r3_loc", [2, TPC], F32)
    d["G4"] = c.dram("G4", [8 * D, TPC], BF16, shared=True)
    d["G4b"] = c.dram("G4b", [16, TPC], F32, shared=True)
    d["ao_loc"] = c.dram("ao_loc", [256, 8192], BF16)
    d["G5"] = c.dram("G5", [8 * 256, 8192], BF16, shared=True)
    for nm in ("cc1", "cc2", "cc3a", "cc3b", "cc4a", "cc4b", "cc5"):
        d[nm] = c.st.enter_context(nc.semaphore(nm))
    c.setup_common(consts=[0.0, 1.0, 1e-5, 1e-6, 1.0 / 128, -1.0 / 128, 1.0 / 16,
                           8192.0, -24.0, 0.125])
    gidx = c.sb([128, 120], I32, "gidx_sb")
    gidx1 = c.sb([2, 18], I32, "gidx1_sb")
    c.p.op("sync", lambda e: e.dma_start(out=gidx[:], in_=d["gidx_h"]), dma=True)
    c.p.op("sync", lambda e: e.dma_start(out=gidx1[:], in_=d["gidx1_h"]), dma=True)
    c.p.bar()
    d["gidx"] = gidx
    d["gidx1"] = gidx1
    p1_emit(c, d)
    p2_emit(c, d)
    p3_emit(c, d)
    p4_emit(c, d)
    p5_emit(c, d)
    p6_emit(c, d)
    p7_emit(c, d)
    return c


def host_fused(inputs):
    f32 = np.float32
    x = np.asarray(inputs["L0_attn_x"], f32).reshape(TOK, D)
    xTs = [np.ascontiguousarray(x[i * TPC:(i + 1) * TPC].T) for i in range(NCORE)]
    identf = np.eye(128, dtype=f32)
    maskT = np.tril(np.ones((16, 16), f32)).T.copy()
    invc = (1.0 / np.arange(1, 17, dtype=f32)).reshape(16, 1)
    S = 8192
    inv_freq = 1.0 / 10000.0 ** (np.arange(0, 64, 2, dtype=np.float64) / 64)
    fr = np.outer(np.arange(S, dtype=np.float64), inv_freq)
    emb = np.concatenate([fr, fr], axis=-1)
    cos = np.cos(emb).T
    sin = np.sin(emb).T
    sins = np.concatenate([-sin[:32], sin[32:]], 0)
    cosT = np.ascontiguousarray(np.tile(cos, (2, 1)).astype(f32)).astype(BF)
    sinsT = np.ascontiguousarray(np.tile(sins, (2, 1)).astype(f32)).astype(BF)
    com = {
        "p1_down_w": np.ascontiguousarray(np.asarray(inputs["L0_attn_down_w"], f32)),
        "p1_down_b": np.ascontiguousarray(np.asarray(inputs["L0_attn_down_b"], f32).reshape(2, 128).T),
        "p2_tokc": np.asarray(inputs["L0_attn_tok_idx"], f32).reshape(16, 1).copy(),
        "p2_invc": invc, "p2_maskT": maskT, "identf": identf,
        "p3_wo": np.ascontiguousarray(np.asarray(inputs["L0_attn_wo"], f32)),
        "p3_up": np.ascontiguousarray(np.asarray(inputs["L0_attn_up_w"], f32)),
        "p3_pw": np.ascontiguousarray(np.asarray(inputs["L0_attn_post_w"], f32).reshape(2, 128).T),
        "p3_pb": np.ascontiguousarray(np.asarray(inputs["L0_attn_post_b"], f32).reshape(2, 128).T),
        "p3_lw": np.ascontiguousarray(np.asarray(inputs["L0_attn_ln_w"], f32).reshape(2, 128).T),
        "p3_lb": np.ascontiguousarray(np.asarray(inputs["L0_attn_ln_b"], f32).reshape(2, 128).T),
        "p3_ub": np.ascontiguousarray(np.asarray(inputs["L0_attn_up_b"], f32).reshape(8, 128).T),
        "p4_WT": np.ascontiguousarray(np.asarray(inputs["L0_ffn_W"], f32).T),
        "p4_ldT": np.ascontiguousarray(np.asarray(inputs["L0_ffn_lora_down"], f32).T),
        "p4_da": np.ascontiguousarray(np.asarray(inputs["L0_ffn_down_aux"], f32)),
        "p4_luT": np.ascontiguousarray(np.asarray(inputs["L0_ffn_lora_up"], f32).T),
        "p4_uaT": np.ascontiguousarray(np.asarray(inputs["L0_ffn_up_aux"], f32).T),
        "p4_b": np.ascontiguousarray(np.asarray(inputs["L0_ffn_b"], f32).reshape(8, 128).T),
        "p5_W1": np.ascontiguousarray(np.asarray(inputs["L1_attn_W1"], f32)),
        "p5_W2": np.ascontiguousarray(np.asarray(inputs["L1_attn_W2"], f32)),
        "p5_mwT": np.ascontiguousarray(np.asarray(inputs["L1_attn_memory_weights"], f32).T),
        "p5_b1c": np.ascontiguousarray(np.asarray(inputs["L1_attn_bias1"], f32).reshape(8, 128).T),
        "p5_b2c": np.ascontiguousarray(np.asarray(inputs["L1_attn_bias2"], f32).reshape(8, 128).T),
        "p6_cosT": cosT, "p6_sinsT": sinsT,
        "p7_gwT": np.ascontiguousarray(np.asarray(inputs["L1_ffn_gate_w"], f32).T),
        "p7_gb": np.ascontiguousarray(np.asarray(inputs["L1_ffn_gate_b"], f32).reshape(8, 1)),
        "p7_ewT": np.ascontiguousarray(np.asarray(inputs["L1_ffn_expert_w"], f32).transpose(0, 2, 1).reshape(8 * D, D)),
        "p7_ebrow": np.ascontiguousarray(np.asarray(inputs["L1_ffn_expert_b"], f32).reshape(1, 8 * D)),
        "p7_ffnT": np.ascontiguousarray(np.asarray(inputs["L1_ffn_ffn_w"], f32).T),
        "p7_fb": np.ascontiguousarray(np.asarray(inputs["L1_ffn_ffn_b"], f32).reshape(8, 128).T),
    }
    qwT = np.asarray(inputs["L1_ffn_q_w"], f32).T
    kwT = np.asarray(inputs["L1_ffn_k_w"], f32).T
    vwT = np.asarray(inputs["L1_ffn_v_w"], f32).T
    qb = np.asarray(inputs["L1_ffn_q_b"], f32)
    kb = np.asarray(inputs["L1_ffn_k_b"], f32)
    vb = np.asarray(inputs["L1_ffn_v_b"], f32)
    maps = []
    pvec = np.arange(128, dtype=np.int64)
    for i in range(NCORE):
        b_tok = i // 4
        r_tok = i % 4
        s = i % 4
        bs, hh = s // 2, s % 2
        hg = i % 4
        prev = (i - 1) % 8
        gidx = np.zeros((128, 120), dtype=np.int64)
        for ch in range(2):
            for cc in range(4):
                gidx[:, ch * 4 + cc] = (4 * bs + cc) * 256 + ch * 128 + pvec
        for hhp in range(2):
            for tt in range(16):
                gidx[:, 8 + hhp * 16 + tt] = (2 * b_tok + hhp) * 8192 + r_tok * 2048 + tt * 128 + pvec
        for ch in range(8):
            gidx[:, 40 + ch] = prev * 1026 + ch * 128 + pvec
        for tsl in range(8):
            for ch in range(8):
                gidx[:, 48 + tsl * 8 + ch] = ((4 * b_tok + tsl // 2) * 1024 + ch * 128 + pvec) * 2 + (tsl % 2)
        for hgp in range(4):
            for fo in range(2):
                gidx[:, 112 + hgp * 2 + fo] = ((4 * b_tok + hgp) * 256 + fo * 128 + pvec) * 4 + r_tok
        gidx1 = np.zeros((2, 18), dtype=np.int64)
        gidx1[:, 0] = prev * 1026 + 1024
        gidx1[:, 1] = prev * 1026 + 1025
        for tsl in range(8):
            cctr = 4 * b_tok + tsl // 2
            gidx1[:, 2 + tsl] = (2 * cctr + 0) * 2 + (tsl % 2)
            gidx1[:, 10 + tsl] = (2 * cctr + 1) * 2 + (tsl % 2)
        cols6 = slice(hg * 256, (hg + 1) * 256)
        m = dict(com)
        m.update({
            "xT": xTs[i],
            "p2_wq": np.ascontiguousarray(np.asarray(inputs["L0_attn_wq"], f32)[:, hh * 128:(hh + 1) * 128]),
            "p2_wk": np.ascontiguousarray(np.asarray(inputs["L0_attn_wk"], f32)[:, hh * 128:(hh + 1) * 128]),
            "p2_wv": np.ascontiguousarray(np.asarray(inputs["L0_attn_wv"], f32)[:, hh * 128:(hh + 1) * 128]),
            "p2_lrw": np.ascontiguousarray(np.asarray(inputs["L0_attn_lr_w"], f32)[hh, 0, :].reshape(2, 128).T),
            "p2_lrb": np.asarray(inputs["L0_attn_lr_b"], f32)[hh].reshape(1, 1).copy(),
            "p2_W1": np.ascontiguousarray(np.asarray(inputs["L0_attn_W1"], f32)[hh]),
            "p2_b1": np.asarray(inputs["L0_attn_b1"], f32)[hh].reshape(1, 128).copy(),
            "p2_gw": np.asarray(inputs["L0_attn_ttt_ln_w"], f32)[hh].reshape(1, 128).copy(),
            "p2_gb": np.asarray(inputs["L0_attn_ttt_ln_b"], f32)[hh].reshape(1, 128).copy(),
            "p2_gwc": np.asarray(inputs["L0_attn_ttt_ln_w"], f32)[hh].reshape(128, 1).copy(),
            "p5_hv": (np.zeros((128, 1), f32) if i % 4 == 0 else np.ones((128, 1), f32)),
            "p6_qS": np.ascontiguousarray(qwT[:, cols6]),
            "p6_kS": np.ascontiguousarray(kwT[:, cols6]),
            "p6_vS": np.ascontiguousarray(vwT[:, cols6]),
            "p6_qb": np.ascontiguousarray(qb[cols6].reshape(2, 128).T),
            "p6_kb": np.ascontiguousarray(kb[cols6].reshape(2, 128).T),
            "p6_vb": np.ascontiguousarray(vb[cols6].reshape(2, 128).T),
            "gidx": gidx.astype(np.int32),
            "gidx1": gidx1.astype(np.int32),
        })
        maps.append(m)
    return maps


def kernel(**inputs):
    c = build_fused()
    res = c.run(host_fused(inputs), core_ids=list(range(NCORE)))
    out = np.concatenate([np.asarray(r["outT"], np.float32).T for r in res.results], 0)
    return np.ascontiguousarray(out.reshape(2, 8192, D))
